# version 05
# baseline (speedup 1.0000x reference)
import contextlib
import numpy as np
import concourse.bass as bass
import concourse.mybir as mybir
from concourse.bass_utils import run_bass_kernel_spmd

F32 = mybir.dt.float32
BF16 = mybir.dt.bfloat16
AF = mybir.ActivationFunctionType
ALU = mybir.AluOpType

NCORES = 8
T = 1024
C = 16
IC = 64
H = 2048
EPS = 1e-5
NEG = -1e38

V_G1, V_B1, V_G2, V_B2, V_TMF, V_MK, V_MV, V_MR, V_CMK, V_CMR = (
    0, 16, 32, 48, 64, 80, 96, 112, 128, 144)
V_EPS, V_NHALF, V_NONE, V_HALF, V_NEG = 160, 161, 162, 163, 164
NVEC = 168

ROWS3 = [0, 32, 64]


def _tiles(n, cap=512):
    out, s = [], 0
    while s < n:
        l = min(cap, n - s)
        out.append((s, l))
        s += l
    return out


class Sched:
    ENGS = ("pe", "dve", "act", "sp", "pool")

    def __init__(self):
        self.items = {e: [] for e in self.ENGS}
        self.sems = {}
        self.ctr = {}
        self.hiw = {}

    def sem(self, name):
        if name not in self.sems:
            self.sems[name] = None
            self.ctr[name] = 0

    def emit(self, eng, fn, inc=None, incn=1):
        if inc is not None:
            self.ctr[inc] += incn
        self.items[eng].append((fn, inc, incn))
        return self.ctr[inc] if inc is not None else None

    def wait(self, eng, sem, val):
        if val is None or val <= 0:
            return
        key = (eng, sem)
        if self.hiw.get(key, 0) >= val:
            return
        self.hiw[key] = val

        def f(e, sem=sem, val=val):
            e.wait_ge(self.sems[sem], val)
            return None
        self.items[eng].append((f, None, 1))

    def barrier(self, engs=None):
        vals = [(s, self.ctr[s]) for s in self.ctr]
        for eng in (engs or self.ENGS):
            for s, v in vals:
                self.wait(eng, s, v)

    def waitt(self, eng, tok):
        if tok:
            self.wait(eng, tok[0], tok[1])

    def replay(self, block):
        def mk(name):
            def run(e):
                for fn, inc, incn in self.items[name]:
                    inst = fn(e)
                    if inc is not None:
                        inst.then_inc(self.sems[inc], incn)
            return run
        block.gpsimd(mk("pool"))
        block.sync(mk("sp"))
        block.tensor(mk("pe"))
        block.vector(mk("dve"))
        block.scalar(mk("act"))


def _build(dbg=()):
    nc = bass.Bass("TRN2", target_bir_lowering=False)
    dbg = set(dbg)

    x_img = nc.dram_tensor("x_img", [128, C * 1026], F32, kind="ExternalInput").ap()
    wkvr_h = nc.dram_tensor("wkvr", [C, 128, 3 * 2048], F32, kind="ExternalInput").ap()
    wo_h = nc.dram_tensor("wo", [C, 128, 2048], F32, kind="ExternalInput").ap()
    wkey_h = nc.dram_tensor("wkey", [IC, 128, 2048], F32, kind="ExternalInput").ap()
    wval_h = nc.dram_tensor("wval", [IC, 128, 2048], F32, kind="ExternalInput").ap()
    wgu_h = nc.dram_tensor("wgu", [IC, 128, 4096], F32, kind="ExternalInput").ap()
    wdn_h = nc.dram_tensor("wdn", [IC, 128, 2048], F32, kind="ExternalInput").ap()
    wcr_h = nc.dram_tensor("wcr", [C, 128, 2048], F32, kind="ExternalInput").ap()
    vecs_h = nc.dram_tensor("vecs", [128, NVEC], F32, kind="ExternalInput").ap()
    hmask_h = nc.dram_tensor("hmask", [128, 1], F32, kind="ExternalInput").ap()
    out_h = nc.dram_tensor("out", [C, 128, T], F32, kind="ExternalOutput").ap()

    def scratch(name, shape, dt):
        kind = "ExternalOutput" if name in dbg else "Internal"
        return nc.dram_tensor(name, shape, dt, kind=kind).ap()

    xm_spill = scratch("xm_sp", [C, 128, 1025], F32)
    cmr_spill = scratch("cmr_sp", [C, 128, T], BF16)
    x1_spill = scratch("x1_sp", [C, 128, T], F32)
    kv_spill = scratch("kv_sp", [C, 128, T], F32)

    dbg_out = {}
    for nm, shape, dt in [
        ("d_bca", [128, 1026], F32), ("d_bcb", [128, 1026], F32),
        ("d_xk", [C, 128, 1025], BF16), ("d_xv", [C, 128, 1025], BF16),
        ("d_xr", [C, 128, 1025], BF16), ("d_u", [C, 128, 1025], BF16),
        ("d_cmk", [C, 128, T], BF16), ("d_h3", [C, 128, T], BF16),
        ("d_silu", [IC, 128, T], BF16), ("d_mid", [IC, 128, T], BF16),
        ("d_k", [C, 128, T], F32), ("d_v", [C, 128, T], F32),
        ("d_r", [C, 128, T], F32), ("d_stv", [6, 1, 1026], F32),
        ("d_hkvr", [128, 192], F32),
    ]:
        if nm in dbg:
            dbg_out[nm] = nc.dram_tensor(nm, shape, dt, kind="ExternalOutput").ap()

    S = Sched()
    for s in ("pe", "dv", "ac", "pl"):
        S.sem(s)

    BASE = (nc.sbuf_base + 127) // 128 * 128
    TOP = nc.sbuf_base + nc.sbuf_bytes_remaining

    def sba(name, shape, dtype, off):
        a = BASE + off
        nb = int(np.prod(shape[1:])) * (2 if dtype == BF16 else 4)
        assert a + nb <= TOP, (name, a + nb, TOP)
        return nc.alloc_sbuf_tensor_at(name, shape, dtype, offset=a).ap()

    VEC = sba("VEC", [128, NVEC], F32, 0)
    HM = sba("HM", [128, 1], F32, 768)
    ONESB = sba("ONESB", [128, 1], BF16, 896)
    ONESC = sba("ONESC", [1, 128], F32, 1024)
    STV = [sba(f"STV{i}", [1, 1026], F32, 1536 + i * 4224) for i in range(6)]
    SL_A0, SL_B0, SL_A, SL_B, SL_T0, SL_T1 = range(6)
    SC_H = sba("SC_H", [128, 1026], F32, 26880)
    SC_D = sba("SC_D", [128, 1026], F32, 31104)
    D0 = 35328

    def stv(i, n=1026):
        return STV[i][0:1, 0:n]

    def vcol(base, fc):
        return VEC[:, base + fc: base + fc + 1]

    PS = [nc.alloc_psum_tensor(f"PSB{i}", [128, 512], F32).ap() for i in range(8)]

    def dma(eng, dst, src, sem):
        S.sem(sem)

        def f(e, dst=dst, src=src):
            return e.dma_start(dst, src)
        return (sem, S.emit(eng, f, inc=sem, incn=16))

    def mm(psum, lhsT, rhs, start, stop, inc=None):
        def f(e, psum=psum, lhsT=lhsT, rhs=rhs, start=start, stop=stop):
            return e.matmul(psum, lhsT, rhs, start=start, stop=stop)
        return S.emit("pe", f, inc=inc)

    def act(out, in_, func, bias=0.0, scale=1.0, inc=None):
        def f(e, out=out, in_=in_, func=func, bias=bias, scale=scale):
            return e.activation(out, in_, func, bias=bias, scale=scale)
        return S.emit("act", f, inc=inc)

    def dve(op, *args, inc=None):
        def f(e, op=op, args=args):
            return getattr(e, op)(*args)
        return S.emit("dve", f, inc=inc)

    def pool(op, *args, inc=None):
        def f(e, op=op, args=args):
            return getattr(e, op)(*args)
        return S.emit("pool", f, inc=inc)

    def stats_finish(pa, pb, ncols, slot_a, slot_b, pe_done):
        tl = _tiles(ncols)
        S.wait("act", "pe", pe_done)
        for ti, (s, l) in enumerate(tl):
            act(stv(SL_T0)[0:1, s:s + l], pa[ROWS3[ti]:ROWS3[ti] + 1, 0:l], AF.Identity)
        for ti, (s, l) in enumerate(tl):
            act(stv(SL_T1)[0:1, s:s + l], pb[ROWS3[ti]:ROWS3[ti] + 1, 0:l], AF.Identity,
                inc=("ac" if ti == len(tl) - 1 else None))
        S.wait("dve", "ac", S.ctr["ac"])
        m = stv(SL_T0, ncols)
        q = stv(SL_T1, ncols)
        a = stv(slot_a, ncols)
        b = stv(slot_b, ncols)
        dve("tensor_scalar", m, m, 1.0 / H, None, ALU.mult)
        dve("tensor_scalar", q, q, 1.0 / H, None, ALU.mult)
        dve("tensor_mul", b, m, m)
        dve("tensor_sub", q, q, b, inc="dv")
        S.wait("act", "dv", S.ctr["dv"])
        act(a, q, AF.Ln, bias=VEC[0:1, V_EPS:V_EPS + 1])
        act(a, a, AF.Exp, scale=VEC[0:1, V_NHALF:V_NHALF + 1], inc="ac")
        S.wait("dve", "ac", S.ctr["ac"])
        dve("scalar_tensor_tensor", b, m, -1.0, a, ALU.mult, ALU.mult, inc="dv")
        return S.ctr["dv"]

    def broadcast(vec_ap, dst, ncols, dv_ready, psb):
        S.wait("pe", "dv", dv_ready)
        last = None
        for (s, l) in _tiles(ncols):
            S.wait("pe", "ac", S.ctr["ac"])
            mm(psb[:, 0:l], ONESC, vec_ap[0:1, s:s + l], start=True, stop=True, inc="pe")
            S.wait("act", "pe", S.ctr["pe"])
            last = act(dst[:, s:s + l], psb[:, 0:l], AF.Identity, inc="ac")
        return last

    M2 = sba("M2", [128, 512], F32, 209920)
    dma("sp", VEC, vecs_h, "ld")
    hm_tok = dma("sp", HM, hmask_h, "ld")
    for e_ in ("act", "dve", "pool"):
        S.wait(e_, "ld", 32)
    dve("memset", ONESB, 1.0)
    dve("memset", ONESC, 1.0, inc="dv")
    dve("memset", M2, 1.0, inc="dv")
    S.wait("dve", "dv", S.ctr["dv"])
    dve("tensor_copy", M2[:, 0:1], HM, inc="dv")
    S.wait("dve", "dv", S.ctr["dv"])
    dve("tensor_copy", M2[:, 1:2], HM, inc="dv")

    XBF = sba("XBF", [128, C * 1026], BF16, D0)
    xbf_tok = dma("pool", XBF, x_img, "xbf")

    SQ = [sba(f"SQ{i}", [128, 1026], BF16, D0 + 33024 + i * 2176) for i in range(2)]
    pa, pb = PS[0], PS[1]
    tl6 = _tiles(1026)
    sq_pe_free = [0, 0]
    for fc in range(C):
        b = fc % 2
        xv = XBF[:, fc * 1026:(fc + 1) * 1026]
        S.waitt("act", xbf_tok)
        S.wait("act", "pe", sq_pe_free[b])
        act(SQ[b], xv, AF.Square, inc="ac")
        sq_ac = S.ctr["ac"]
        S.waitt("pe", xbf_tok)
        S.wait("pe", "ac", sq_ac)
        for ti, (s, l) in enumerate(tl6):
            mm(pa[ROWS3[ti]:ROWS3[ti] + 1, 0:l], ONESB, xv[:, s:s + l],
               start=(fc == 0), stop=(fc == C - 1))
        for ti, (s, l) in enumerate(tl6):
            mm(pb[ROWS3[ti]:ROWS3[ti] + 1, 0:l], ONESB, SQ[b][:, s:s + l],
               start=(fc == 0), stop=(fc == C - 1),
               inc=("pe" if ti == len(tl6) - 1 else None))
        sq_pe_free[b] = S.ctr["pe"]
    a0_dv = stats_finish(pa, pb, 1026, SL_A0, SL_B0, S.ctr["pe"])

    BC_A = sba("BC_A1", [128, 1026], F32, D0 + 140032)
    BC_B = sba("BC_B1", [128, 1026], F32, D0 + 144256)
    broadcast(stv(SL_A0), BC_A, 1026, a0_dv, PS[2])
    bc_ac = broadcast(stv(SL_B0), BC_B, 1026, a0_dv, PS[2])

    XCH = [sba(f"XCHa{i}", [128, 1026], F32, D0 + 131584 + i * 4224) for i in range(2)]
    XABF = [sba(f"XABF{i}", [128, 1026], BF16, D0 + 37376 + i * 2176) for i in range(2)]
    xch_free = [0, 0]
    st_pe_free = [0, 0]
    sch_ac_free = 0
    for fc in range(C):
        b = fc % 2
        S.wait("sp", "dv", xch_free[b])
        xd_tok = dma("sp", XCH[b], x_img[:, fc * 1026:(fc + 1) * 1026], f"xch{b}")
        S.waitt("dve", xd_tok)
        S.wait("dve", "ac", bc_ac)
        S.wait("dve", "pe", st_pe_free[b])
        S.wait("dve", "ac", sch_ac_free)
        dve("tensor_mul", SC_H, XCH[b], BC_A)
        dve("tensor_add", SC_H, SC_H, BC_B)
        dve("tensor_copy", XABF[b], SC_H, inc="dv")
        xch_free[b] = S.ctr["dv"]
        S.wait("act", "dv", S.ctr["dv"])
        S.wait("act", "pe", st_pe_free[b])
        act(SQ[b], SC_H, AF.Square, inc="ac")
        sch_ac_free = S.ctr["ac"]
        S.wait("pe", "dv", S.ctr["dv"])
        S.wait("pe", "ac", S.ctr["ac"])
        for ti, (s, l) in enumerate(tl6):
            mm(pa[ROWS3[ti]:ROWS3[ti] + 1, 0:l], ONESB, XABF[b][:, s:s + l],
               start=(fc == 0), stop=(fc == C - 1))
        for ti, (s, l) in enumerate(tl6):
            mm(pb[ROWS3[ti]:ROWS3[ti] + 1, 0:l], ONESB, SQ[b][:, s:s + l],
               start=(fc == 0), stop=(fc == C - 1),
               inc=("pe" if ti == len(tl6) - 1 else None))
        st_pe_free[b] = S.ctr["pe"]
    a1_dv = stats_finish(pa, pb, 1026, SL_A, SL_B, S.ctr["pe"])
    S.wait("dve", "dv", a1_dv)
    dve("tensor_mul", stv(SL_T0), stv(SL_B0), stv(SL_A))
    dve("tensor_add", stv(SL_T0), stv(SL_T0), stv(SL_B))
    dve("tensor_mul", stv(SL_T1), stv(SL_A0), stv(SL_A), inc="dv")
    a01_dv = S.ctr["dv"]
    broadcast(stv(SL_T1), BC_A, 1026, a01_dv, PS[2])
    bc_ac = broadcast(stv(SL_T0), BC_B, 1026, a01_dv, PS[2])
    if "d_stv" in dbg_out:
        S.wait("sp", "dv", a01_dv)
        for i in range(6):
            dma("sp", dbg_out["d_stv"][i], STV[i][0:1, 0:1026], "dbg")

    XK = sba("XK", [128, C * 1025], BF16, D0)
    XV = sba("XV", [128, C * 1025], BF16, D0 + 32896)
    XR = sba("XR", [128, C * 1025], BF16, D0 + 65792)
    WKVR = [sba(f"WKVR{i}", [128, 3 * 2048], BF16, D0 + 148480 + i * 12288)
            for i in range(2)]
    wkvr_wd = {}
    wkvr_wd[0] = dma("pool", WKVR[0], wkvr_h[0], "wk0")
    wkvr_wd[1] = dma("pool", WKVR[1], wkvr_h[1], "wk1")

    SC_H2 = sba("SC_H2", [128, 1026], F32, D0 + 98688)
    SC_HX = [SC_H, SC_H2]
    hx_free = [0, 0]
    S.wait("sp", "dv", max(xch_free))
    xch_free = [0, 0]
    for fc in range(C):
        b = fc % 2
        S.wait("sp", "pl", xch_free[b])
        xd_tok = dma("sp", XCH[b], x_img[:, fc * 1026:(fc + 1) * 1026], f"xch{b}")
        hx = SC_HX[b]
        S.waitt("pool", xd_tok)
        S.wait("pool", "ac", bc_ac)
        S.wait("pool", "dv", hx_free[b])
        pool("tensor_mul", hx, XCH[b], BC_A)
        pool("tensor_add", hx, hx, BC_B, inc="pl")
        xch_free[b] = S.ctr["pl"]
        S.wait("act", "pl", S.ctr["pl"])
        act(hx, hx, AF.Identity, scale=vcol(V_G1, fc), bias=vcol(V_B1, fc), inc="ac")
        S.wait("dve", "ac", S.ctr["ac"])
        dve("tensor_mul", hx[:, 0:512], hx[:, 0:512], M2)
        dve("tensor_sub", SC_D[:, 0:1025], hx[:, 1:1026], hx[:, 0:1025])
        o = fc * 1025
        dve("scalar_tensor_tensor", XK[:, o:o + 1025], SC_D[:, 0:1025], vcol(V_MK, fc),
            hx[:, 0:1025], ALU.mult, ALU.add)
        dve("scalar_tensor_tensor", XV[:, o:o + 1025], SC_D[:, 0:1025], vcol(V_MV, fc),
            hx[:, 0:1025], ALU.mult, ALU.add)
        dve("scalar_tensor_tensor", XR[:, o:o + 1025], SC_D[:, 0:1025], vcol(V_MR, fc),
            hx[:, 0:1025], ALU.mult, ALU.add, inc="dv")
        hx_free[b] = S.ctr["dv"]
    mixes_dv = S.ctr["dv"]
    mixes_pl = S.ctr["pl"]
    if "d_xk" in dbg_out:
        S.wait("sp", "dv", mixes_dv)
        for fc in range(C):
            dma("sp", dbg_out["d_xk"][fc], XK[:, fc * 1025:(fc + 1) * 1025], "dbg")
            dma("sp", dbg_out["d_xv"][fc], XV[:, fc * 1025:(fc + 1) * 1025], "dbg")
            dma("sp", dbg_out["d_xr"][fc], XR[:, fc * 1025:(fc + 1) * 1025], "dbg")
    if "d_bca" in dbg_out:
        S.wait("sp", "ac", bc_ac)
        dma("sp", dbg_out["d_bca"], BC_A, "dbg")
        dma("sp", dbg_out["d_bcb"], BC_B, "dbg")

    S.barrier()
    U = sba("U", [128, C * 1025], BF16, D0 + 98688)
    CH = [sba(f"CH{i}", [128, 512], F32, D0 + 131584 + i * 2048) for i in range(8)]
    PSK = [PS[0], PS[1]]
    PSV = [PS[2], PS[3]]
    PSR = [PS[4], PS[5]]
    HK, HV, HR = PS[0][:, 0:1], PS[2][:, 0:1], PS[4][:, 0:1]
    HW = PS[6][:, 0:1]
    HSTX = sba("HSTX", [128, 192], F32, 209152)
    HST = HSTX[:, 0:48]
    SPC = SC_H[:, 0:512]

    chain_dv_set = [0, 0]
    wkvr_pe_free = [0, 0]
    halo_stage_ac = 0
    for oc in range(C):
        wb = WKVR[oc % 2]
        S.waitt("pe", wkvr_wd[oc])
        S.wait("pe", "dv", mixes_dv)
        for tt in range(2):
            S.wait("pe", "dv", chain_dv_set[tt])
            if tt == 0:
                S.wait("pe", "ac", halo_stage_ac)
            tok0 = 1 + tt * 512
            for mi, dst, mixsrc in ((0, PSK, XK), (1, PSV, XV), (2, PSR, XR)):
                for kc in range(C):
                    mm(dst[tt][:, 0:512],
                       wb[:, (mi * 16 + kc) * 128:(mi * 16 + kc + 1) * 128],
                       mixsrc[:, kc * 1025 + tok0: kc * 1025 + tok0 + 512],
                       start=(kc == 0), stop=(kc == C - 1),
                       inc=("pe" if kc == C - 1 else None))
            pe_kvr = S.ctr["pe"]
            c0, c1, c2, c3 = CH[tt * 4: tt * 4 + 4]
            uo = U[:, oc * 1025 + tok0: oc * 1025 + tok0 + 512]
            S.wait("act", "pe", pe_kvr - 2)
            act(c0, PSK[tt][:, 0:512], AF.Identity, bias=vcol(V_TMF, oc), inc="ac")
            S.wait("dve", "ac", S.ctr["ac"])
            dve("tensor_scalar", c1, c0, VEC[:, V_NEG:V_NEG + 1], None, ALU.max)
            dve("tensor_sub", c0, c0, c1, inc="dv")
            S.wait("act", "dv", S.ctr["dv"])
            act(c1, c0, AF.Exp)
            act(c2, c0, AF.Exp, scale=VEC[:, V_NONE:V_NONE + 1])
            S.wait("act", "pe", pe_kvr)
            act(c3, PSR[tt][:, 0:512], AF.Tanh, scale=VEC[:, V_HALF:V_HALF + 1], inc="ac")
            S.wait("dve", "ac", S.ctr["ac"])
            S.wait("dve", "pe", pe_kvr)
            dve("scalar_tensor_tensor", c0, PSV[tt][:, 0:512], 0.5, c1, ALU.mult, ALU.mult)
            dve("tensor_mul", c0, c0, c2)
            dve("scalar_tensor_tensor", uo, c3, 1.0, c0, ALU.add, ALU.mult, inc="dv")
            chain_dv_set[tt] = S.ctr["dv"]
            if "d_k" in dbg_out:
                DBG1, DBG2, DBG3 = SC_H[:, 0:512], SC_H[:, 513:1025], SC_D[:, 514:1026]
                S.wait("act", "dv", chain_dv_set[tt])
                act(DBG1, PSK[tt][:, 0:512], AF.Identity)
                act(DBG2, PSV[tt][:, 0:512], AF.Identity)
                act(DBG3, PSR[tt][:, 0:512], AF.Identity, inc="ac")
                S.wait("sp", "ac", S.ctr["ac"])
                dma("sp", dbg_out["d_k"][oc][:, tt * 512:tt * 512 + 512], DBG1, "dbg")
                dma("sp", dbg_out["d_v"][oc][:, tt * 512:tt * 512 + 512], DBG2, "dbg")
                dv_dump = dma("sp", dbg_out["d_r"][oc][:, tt * 512:tt * 512 + 512], DBG3, "dbg")
                S.waitt("pe", dv_dump)
                S.waitt("dve", dv_dump)
        S.wait("pe", "ac", halo_stage_ac)
        S.wait("pe", "dv", chain_dv_set[0])
        for mi, dsth, mixsrc in ((0, HK, XK), (1, HV, XV), (2, HR, XR)):
            for kc in range(C):
                mm(dsth, wb[:, (mi * 16 + kc) * 128:(mi * 16 + kc + 1) * 128],
                   mixsrc[:, kc * 1025: kc * 1025 + 1],
                   start=(kc == 0), stop=(kc == C - 1),
                   inc=("pe" if (mi == 2 and kc == C - 1) else None))
        pe_halo = S.ctr["pe"]
        wkvr_pe_free[oc % 2] = pe_halo
        if oc + 2 < C:
            S.wait("pool", "pe", pe_halo)
            wkvr_wd[oc + 2] = dma("pool", WKVR[oc % 2], wkvr_h[oc + 2], f"wk{oc % 2}")
        S.wait("act", "pe", pe_halo)
        act(HST[:, oc:oc + 1], HK, AF.Identity)
        act(HST[:, 16 + oc:16 + oc + 1], HV, AF.Identity)
        act(HST[:, 32 + oc:32 + oc + 1], HR, AF.Identity, inc="ac")
        halo_stage_ac = S.ctr["ac"]
    W16 = HSTX[:, 48:64]
    P16 = HSTX[:, 64:80]
    E16 = HSTX[:, 80:96]
    R16 = HSTX[:, 96:112]
    T16 = HSTX[:, 112:128]
    N16 = HSTX[:, 128:144]
    UH16 = HSTX[:, 144:160]
    act(SPC, SPC, AF.Identity, inc="ac")
    S.wait("dve", "ac", S.ctr["ac"])
    dve("tensor_add", W16, HST[:, 0:16], VEC[:, V_TMF:V_TMF + 16])
    dve("tensor_copy", SPC, SPC)
    dve("tensor_scalar", P16, W16, VEC[:, V_NEG:V_NEG + 1], None, ALU.max)
    dve("tensor_copy", SPC, SPC)
    dve("tensor_sub", W16, W16, P16)
    dve("tensor_copy", SPC, SPC, inc="dv")
    S.wait("act", "dv", S.ctr["dv"])
    act(E16, W16, AF.Exp)
    act(R16, W16, AF.Exp, scale=VEC[:, V_NONE:V_NONE + 1])
    act(T16, HST[:, 32:48], AF.Tanh, scale=VEC[:, V_HALF:V_HALF + 1])
    act(SPC, SPC, AF.Identity, inc="ac")
    S.wait("dve", "ac", S.ctr["ac"])
    dve("scalar_tensor_tensor", N16, HST[:, 16:32], 0.5, E16, ALU.mult, ALU.mult)
    dve("tensor_copy", SPC, SPC)
    dve("tensor_mul", N16, N16, R16)
    dve("tensor_copy", SPC, SPC)
    dve("scalar_tensor_tensor", UH16, T16, 1.0, N16, ALU.add, ALU.mult)
    dve("tensor_copy", SPC, SPC)
    for oc in range(C):
        dve("tensor_copy", U[:, oc * 1025: oc * 1025 + 1], UH16[:, oc:oc + 1])
    dve("tensor_copy", SPC, SPC, inc="dv")
    if "d_hkvr" in dbg_out:
        S.wait("sp", "dv", S.ctr["dv"])
        dma("sp", dbg_out["d_hkvr"], HSTX, "dbg")
    u_done_dv = S.ctr["dv"]
    if "d_u" in dbg_out:
        S.wait("sp", "dv", u_done_dv)
        for fc in range(C):
            dma("sp", dbg_out["d_u"][fc], U[:, fc * 1025:(fc + 1) * 1025], "dbg")

    S.barrier()
    BC_A = sba("BC_A2", [128, 1026], F32, D0 + 165376)
    BC_B = sba("BC_B2", [128, 1026], F32, D0 + 169600)
    broadcast(stv(SL_A0), BC_A, 1026, a0_dv, PS[7])
    bc_ac = broadcast(stv(SL_B0), BC_B, 1026, a0_dv, PS[7])

    XCH = [sba(f"XCHb{i}", [128, 1026], F32, D0 + 139776 + i * 4224) for i in range(2)]
    WO = [sba(f"WO{i}", [128, 2048], BF16, D0 + 131584 + i * 4096) for i in range(2)]
    PSW = [[PS[0], PS[1]], [PS[2], PS[3]]]
    XMCH = [sba(f"XMCH{i}", [128, 1025], F32, D0 + 148224 + i * 4224) for i in range(2)]
    XMBF = [sba(f"XMBF{i}", [128, 1025], BF16, D0 + 156672 + i * 2176) for i in range(2)]
    SQ5 = [sba(f"SQ5{i}", [128, 1025], BF16, D0 + 161024 + i * 2176) for i in range(2)]
    pa, pb = PS[4], PS[5]
    tl5 = _tiles(1025)

    wo_wd = {}
    wo_wd[0] = dma("pool", WO[0], wo_h[0], "wwo0")
    wo_wd[1] = dma("pool", WO[1], wo_h[1], "wwo1")
    wo_pe_free = [0, 0]
    xm_halo_dv_last = 0
    xm_dv = [0, 0]
    xm_spilled = [None, None]
    xm_stats_pe = [0, 0]
    xch_free = [0, 0]
    sq5_ac = [0, 0]
    for oc in range(C):
        b = oc % 2
        S.waitt("pe", wo_wd[oc])
        S.wait("pe", "dv", u_done_dv)
        S.wait("pe", "dv", xm_dv[b])
        for tt in range(2):
            tok0 = 1 + tt * 512
            for kc in range(C):
                mm(PSW[b][tt][:, 0:512], WO[b][:, kc * 128:(kc + 1) * 128],
                   U[:, kc * 1025 + tok0: kc * 1025 + tok0 + 512],
                   start=(kc == 0), stop=(kc == C - 1),
                   inc=("pe" if kc == C - 1 else None))
        S.wait("pe", "dv", xm_halo_dv_last)
        for kc in range(C):
            mm(HW, WO[b][:, kc * 128:(kc + 1) * 128],
               U[:, kc * 1025: kc * 1025 + 1],
               start=(kc == 0), stop=(kc == C - 1),
               inc=("pe" if kc == C - 1 else None))
        pe_wo = S.ctr["pe"]
        wo_pe_free[b] = pe_wo
        if oc + 2 < C:
            S.wait("pool", "pe", pe_wo)
            wo_wd[oc + 2] = dma("pool", WO[b], wo_h[oc + 2], f"wwo{b}")
        S.wait("sp", "dv", xch_free[b])
        xd_tok = dma("sp", XCH[b], x_img[:, oc * 1026:(oc + 1) * 1026], f"xch{b}")
        S.waitt("dve", xd_tok)
        S.wait("dve", "ac", bc_ac)
        S.waitt("dve", xm_spilled[b])
        S.wait("dve", "ac", sq5_ac[b])
        dve("tensor_mul", XMCH[b], XCH[b][:, 1:1026], BC_A[:, 1:1026])
        dve("tensor_add", XMCH[b], XMCH[b], BC_B[:, 1:1026], inc="dv")
        xch_free[b] = S.ctr["dv"]
        S.wait("dve", "pe", pe_wo)
        dve("tensor_add", XMCH[b][:, 0:1], XMCH[b][:, 0:1], HW, inc="dv")
        xm_halo_dv_last = S.ctr["dv"]
        dve("tensor_add", XMCH[b][:, 1:513], XMCH[b][:, 1:513], PSW[b][0][:, 0:512])
        dve("tensor_add", XMCH[b][:, 513:1025], XMCH[b][:, 513:1025], PSW[b][1][:, 0:512])
        S.wait("dve", "pe", xm_stats_pe[b])
        dve("tensor_copy", XMBF[b], XMCH[b], inc="dv")
        xm_dv[b] = S.ctr["dv"]
        S.wait("act", "dv", xm_dv[b])
        S.wait("act", "pe", xm_stats_pe[b])
        act(SQ5[b], XMCH[b], AF.Square, inc="ac")
        ac_sq = S.ctr["ac"]
        sq5_ac[b] = ac_sq
        S.wait("sp", "dv", xm_dv[b])
        xm_spilled[b] = dma("sp", xm_spill[oc], XMCH[b], f"sxm{b}")
        S.wait("pe", "dv", xm_dv[b])
        S.wait("pe", "ac", ac_sq)
        for ti, (s, l) in enumerate(tl5):
            mm(pa[ROWS3[ti]:ROWS3[ti] + 1, 0:l], ONESB, XMBF[b][:, s:s + l],
               start=(oc == 0), stop=(oc == C - 1))
        for ti, (s, l) in enumerate(tl5):
            mm(pb[ROWS3[ti]:ROWS3[ti] + 1, 0:l], ONESB, SQ5[b][:, s:s + l],
               start=(oc == 0), stop=(oc == C - 1),
               inc=("pe" if ti == len(tl5) - 1 else None))
        xm_stats_pe[b] = S.ctr["pe"]
    a2_dv = stats_finish(pa, pb, 1025, SL_A, SL_B, S.ctr["pe"])
    broadcast(stv(SL_A, 1025), BC_A[:, 0:1025], 1025, a2_dv, PS[7])
    bc_ac = broadcast(stv(SL_B, 1025), BC_B[:, 0:1025], 1025, a2_dv, PS[7])

    S.barrier()
    CMK = sba("CMK", [128, C * T], BF16, D0)
    CMR = [sba(f"CMR{i}", [128, T], BF16, D0 + 32896 + i * 2048) for i in range(2)]
    SC_H2b = sba("SC_H2b", [128, 1026], F32, D0 + 73856)
    SC_H2c = sba("SC_H2c", [128, 1026], F32, D0 + 78080)
    XCHb3 = sba("XCHb3", [128, 1026], F32, D0 + 82304)
    CMR.append(sba("CMR2", [128, T], BF16, D0 + 36992))
    SC_HX = [SC_H, SC_H2b, SC_H2c]
    XCH6 = [XCH[0], XCH[1], XCHb3]
    hx_free = [0, 0, 0]
    S.wait("sp", "dv", max(xch_free))
    xch_free = [0, 0, 0]
    cmr_spilled = [None, None, None]
    for fc in range(C):
        b = fc % 3
        S.wait("sp", "pl", xch_free[b])
        xd_tok = dma("sp", XCH6[b][:, 0:1025], xm_spill[fc], f"xch{b}")
        hx = SC_HX[b]
        S.waitt("pool", xd_tok)
        S.wait("pool", "ac", bc_ac)
        S.wait("pool", "dv", hx_free[b])
        pool("tensor_mul", hx[:, 0:1025], XCH6[b][:, 0:1025], BC_A[:, 0:1025])
        pool("tensor_add", hx[:, 0:1025], hx[:, 0:1025], BC_B[:, 0:1025], inc="pl")
        xch_free[b] = S.ctr["pl"]
        S.wait("act", "pl", S.ctr["pl"])
        act(hx[:, 0:1025], hx[:, 0:1025], AF.Identity,
            scale=vcol(V_G2, fc), bias=vcol(V_B2, fc), inc="ac")
        S.wait("dve", "ac", S.ctr["ac"])
        dve("tensor_mul", hx[:, 0:511], hx[:, 0:511], M2[:, 1:512])
        dve("tensor_sub", SC_D[:, 0:T], hx[:, 1:1025], hx[:, 0:T])
        dve("scalar_tensor_tensor", CMK[:, fc * T:(fc + 1) * T], SC_D[:, 0:T],
            vcol(V_CMK, fc), hx[:, 0:T], ALU.mult, ALU.add)
        S.waitt("dve", cmr_spilled[b])
        dve("scalar_tensor_tensor", CMR[b], SC_D[:, 0:T],
            vcol(V_CMR, fc), hx[:, 0:T], ALU.mult, ALU.add, inc="dv")
        hx_free[b] = S.ctr["dv"]
        S.wait("sp", "dv", S.ctr["dv"])
        cmr_spilled[b] = dma("sp", cmr_spill[fc], CMR[b], f"scm{b}")
    cmk_done_dv = S.ctr["dv"]
    if "d_cmk" in dbg_out:
        S.wait("sp", "dv", cmk_done_dv)
        for fc in range(C):
            dma("sp", dbg_out["d_cmk"][fc], CMK[:, fc * T:(fc + 1) * T], "dbg")

    S.barrier()
    SILU = sba("SILU", [128, IC * T], BF16, D0 + 32896)
    WK3 = [sba(f"WK3{i}", [128, 2048], BF16, D0 + 163968 + i * 4096) for i in range(2)]
    PSA = [PS[0], PS[1]]
    PSB = [PS[2], PS[3]]
    wkey_wd = {}
    wk_pe_free = [0, 0]
    silu_ac = [0, 0]
    for i in range(2):
        wkey_wd[i] = dma("pool", WK3[i], wkey_h[i], f"wy{i}")
    for oi in range(IC):
        S.waitt("pe", wkey_wd[oi])
        S.wait("pe", "dv", cmk_done_dv)
        ps = (PSA, PSB)[oi % 2]
        S.wait("pe", "ac", silu_ac[oi % 2])
        for tt in range(2):
            for kc in range(C):
                mm(ps[tt][:, 0:512], WK3[oi % 2][:, kc * 128:(kc + 1) * 128],
                   CMK[:, kc * T + tt * 512: kc * T + tt * 512 + 512],
                   start=(kc == 0), stop=(kc == C - 1),
                   inc=("pe" if kc == C - 1 else None))
        pe_kk = S.ctr["pe"]
        wk_pe_free[oi % 2] = pe_kk
        if oi + 2 < IC:
            S.wait("pool", "pe", pe_kk)
            wkey_wd[oi + 2] = dma("pool", WK3[oi % 2], wkey_h[oi + 2], f"wy{oi % 2}")
        S.wait("act", "pe", pe_kk)
        act(SILU[:, oi * T: oi * T + 512], ps[0][:, 0:512], AF.Silu)
        silu_ac[oi % 2] = act(SILU[:, oi * T + 512: oi * T + 1024], ps[1][:, 0:512],
                              AF.Silu, inc="ac")
    silu_done_ac = S.ctr["ac"]
    if "d_silu" in dbg_out:
        S.wait("sp", "ac", silu_done_ac)
        for oi in range(IC):
            dma("sp", dbg_out["d_silu"][oi], SILU[:, oi * T:(oi + 1) * T], "dbg")

    S.barrier()
    WV2 = [sba(f"WV2{i}", [128, 2048], BF16, D0 + 163968 + i * 4096) for i in range(2)]
    PKV = [[PS[0], PS[1]], [PS[2], PS[3]]]
    KVCH = [SC_H[:, 0:T], SC_D[:, 0:T]]
    CMRR = sba("CMRR", [128, C * T], BF16, D0)
    cmr_re_xd = 0
    for fc in range(C):
        cmr_re_xd = dma("sp", CMRR[:, fc * T:(fc + 1) * T], cmr_spill[fc], "cmrr")
    wval_wd = {}
    wv_pe_free = [0, 0]
    kv_evac_ac = [0, 0]
    kv_spilled = [None, None]
    for i in range(2):
        wval_wd[i] = dma("pool", WV2[i], wval_h[i], f"wv{i}")
    for oc2 in range(C):
        b = oc2 % 2
        S.wait("pe", "ac", kv_evac_ac[b])
        for kcg in range(4):
            idx = oc2 * 4 + kcg
            S.waitt("pe", wval_wd[idx])
            S.wait("pe", "ac", silu_done_ac)
            for kc in range(C):
                ic = kcg * 16 + kc
                for tt in range(2):
                    mm(PKV[b][tt][:, 0:512], WV2[idx % 2][:, kc * 128:(kc + 1) * 128],
                       SILU[:, ic * T + tt * 512: ic * T + tt * 512 + 512],
                       start=(kcg == 0 and kc == 0), stop=(kcg == 3 and kc == C - 1),
                       inc=("pe" if (kc == C - 1 and tt == 1) else None))
            wv_pe_free[idx % 2] = S.ctr["pe"]
            if idx + 2 < IC:
                S.wait("pool", "pe", wv_pe_free[idx % 2])
                wval_wd[idx + 2] = dma("pool", WV2[idx % 2], wval_h[idx + 2], f"wv{idx % 2}")
        pe_kv = S.ctr["pe"]
        S.wait("act", "pe", pe_kv)
        S.waitt("act", kv_spilled[b])
        act(KVCH[b][:, 0:512], PKV[b][0][:, 0:512], AF.Identity)
        kv_evac_ac[b] = act(KVCH[b][:, 512:1024], PKV[b][1][:, 0:512], AF.Identity,
                            inc="ac")
        S.wait("sp", "ac", kv_evac_ac[b])
        kv_spilled[b] = dma("sp", kv_spill[oc2], KVCH[b], f"skv{b}")

    S.barrier()
    WC2 = [sba(f"WC2{i}", [128, 2048], BF16, D0 + 32896 + i * 4096) for i in range(2)]
    PCR = [[PS[0], PS[1]], [PS[2], PS[3]]]
    pa, pb = PS[4], PS[5]
    XCH = [sba(f"XCHc{i}", [128, T], F32, D0 + 41088 + i * 4096) for i in range(2)]
    XM2 = [sba(f"XM2{i}", [128, T], F32, D0 + 49280 + i * 4096) for i in range(2)]
    KV2 = [sba(f"KV2{i}", [128, T], F32, D0 + 57472 + i * 4096) for i in range(2)]
    X1CH = [sba(f"X1CH{i}", [128, T], F32, D0 + 65664 + i * 4096) for i in range(2)]
    X1BF = [sba(f"X1BF{i}", [128, T], BF16, D0 + 73856 + i * 2048) for i in range(2)]
    SQ8 = [sba(f"SQ8{i}", [128, T], BF16, D0 + 77952 + i * 2048) for i in range(2)]
    CH0 = SC_H[:, 0:512]
    CH1 = SC_H[:, 513:1025]
    wcr_wd = {}
    wcr_wd[0] = dma("pool", WC2[0], wcr_h[0], "wc0")
    wcr_wd[1] = dma("pool", WC2[1], wcr_h[1], "wc1")
    wc_pe_free = [0, 0]
    sq8_ac = [0, 0]
    last_asm_dv = 0
    x1_dv = [0, 0]
    x1_spilled = [None, None]
    x1_stats_pe = [0, 0]
    tl4 = _tiles(T)
    stats_ready = {}

    def emit_x1_stats(j):
        bj = j % 2
        dvv, acv = stats_ready[j]
        S.wait("pe", "dv", dvv)
        S.wait("pe", "ac", acv)
        for ti, (s, l) in enumerate(tl4):
            mm(pa[ROWS3[ti]:ROWS3[ti] + 1, 0:l], ONESB, X1BF[bj][:, s:s + l],
               start=(j == 0), stop=(j == C - 1),
               inc=("pe" if ti == len(tl4) - 1 else None))
        for ti, (s, l) in enumerate(tl4):
            mm(pb[ROWS3[ti]:ROWS3[ti] + 1, 0:l], ONESB, SQ8[bj][:, s:s + l],
               start=(j == 0), stop=(j == C - 1),
               inc=("pe" if ti == len(tl4) - 1 else None))
        x1_stats_pe[bj] = S.ctr["pe"]
    def p8b_loads(j):
        bj = j % 2
        S.wait("sp", "dv", x1_dv[bj])
        t1 = dma("sp", XCH[bj], x_img[:, j * 1026 + 2:(j + 1) * 1026], f"xch{bj}")
        t2 = dma("sp", XM2[bj], xm_spill[j][:, 1:1025], f"xm2{bj}")
        t3 = dma("sp", KV2[bj], kv_spill[j], f"kv2{bj}")
        return (t1, t2, t3)

    xd_loads_of = {0: p8b_loads(0), 1: p8b_loads(1)}
    for oc2 in range(C):
        b = oc2 % 2
        xd_loads = xd_loads_of[oc2]
        S.waitt("pe", wcr_wd[oc2])
        S.waitt("pe", cmr_re_xd)
        S.wait("pe", "dv", x1_dv[b])
        for tt in range(2):
            for kc in range(C):
                mm(PCR[b][tt][:, 0:512], WC2[b][:, kc * 128:(kc + 1) * 128],
                   CMRR[:, kc * T + tt * 512: kc * T + tt * 512 + 512],
                   start=(kc == 0), stop=(kc == C - 1),
                   inc=("pe" if (kc == C - 1 and tt == 1) else None))
        pe_cr = S.ctr["pe"]
        wc_pe_free[b] = pe_cr
        if oc2 + 2 < C:
            S.wait("pool", "pe", pe_cr)
            wcr_wd[oc2 + 2] = dma("pool", WC2[b], wcr_h[oc2 + 2], f"wc{b}")
        for tok_ in xd_loads:
            S.waitt("dve", tok_)
        S.waitt("dve", x1_spilled[b])
        S.wait("dve", "ac", sq8_ac[b])
        S.wait("dve", "pe", pe_cr)
        S.wait("act", "pe", pe_cr)
        for tt in range(2):
            sl = slice(tt * 512, tt * 512 + 512)
            S.wait("act", "dv", last_asm_dv)
            act(CH0, PCR[b][tt][:, 0:512], AF.Sigmoid, inc="ac")
            S.wait("dve", "ac", S.ctr["ac"])
            dve("tensor_add", X1CH[b][:, sl], XCH[b][:, sl], XM2[b][:, sl])
            dve("tensor_mul", CH1, CH0, KV2[b][:, sl])
            dve("tensor_add", X1CH[b][:, sl], X1CH[b][:, sl], CH1, inc="dv")
            last_asm_dv = S.ctr["dv"]
        S.wait("dve", "pe", x1_stats_pe[b])
        dve("tensor_copy", X1BF[b], X1CH[b], inc="dv")
        x1_dv[b] = S.ctr["dv"]
        S.wait("act", "dv", x1_dv[b])
        S.wait("act", "pe", x1_stats_pe[b])
        act(SQ8[b], X1CH[b], AF.Square, inc="ac")
        ac_sq = S.ctr["ac"]
        sq8_ac[b] = ac_sq
        S.wait("sp", "dv", x1_dv[b])
        x1_spilled[b] = dma("sp", x1_spill[oc2], X1CH[b], f"sx1{b}")
        if oc2 + 2 < C:
            xd_loads_of[oc2 + 2] = p8b_loads(oc2 + 2)
        stats_ready[oc2] = (x1_dv[b], ac_sq)
        if oc2 > 0:
            emit_x1_stats(oc2 - 1)
    emit_x1_stats(C - 1)
    a3_dv = stats_finish(pa, pb, T, SL_A, SL_B, S.ctr["pe"])

    S.barrier()
    H3 = sba("H3", [128, C * T], BF16, D0)
    BC_A = sba("BC_A3", [128, T], F32, D0 + 32896)
    BC_B = sba("BC_B3", [128, T], F32, D0 + 36992)
    broadcast(stv(SL_A, T), BC_A, T, a3_dv, PS[0])
    bc_ac = broadcast(stv(SL_B, T), BC_B, T, a3_dv, PS[0])
    XCH = [sba(f"XCHd{i}", [128, T], F32, D0 + 41088 + i * 4224) for i in range(4)]
    HXD = SC_H
    HXP = SC_D
    xch_free = [0, 0, 0, 0]
    for fc in range(C):
        b = fc % 4
        eng, hx = (dve, HXD) if fc % 2 == 0 else (pool, HXP)
        sem = "dv" if fc % 2 == 0 else "pl"
        ename = "dve" if fc % 2 == 0 else "pool"
        S.wait("sp", sem, xch_free[b])
        xd_tok = dma("sp", XCH[b], x1_spill[fc], f"xch{b}")
        S.waitt(ename, xd_tok)
        S.wait(ename, "ac", bc_ac)
        eng("tensor_mul", hx[:, 0:T], XCH[b], BC_A)
        eng("tensor_add", H3[:, fc * T:(fc + 1) * T], hx[:, 0:T], BC_B, inc=sem)
        xch_free[b] = S.ctr[sem]
    h3_dv = S.ctr["dv"]
    h3_pl = S.ctr["pl"]
    if "d_h3" in dbg_out:
        S.wait("sp", "dv", h3_dv)
        for fc in range(C):
            dma("sp", dbg_out["d_h3"][fc], H3[:, fc * T:(fc + 1) * T], "dbg")

    S.barrier()
    MID = sba("MID", [128, IC * T], BF16, D0 + 32896)
    WG = [sba(f"WG{i}", [128, 2048], BF16, D0 + 163968 + i * 4096) for i in range(2)]
    PG = [[PS[0], PS[1]], [PS[2], PS[3]]]
    PU = [[PS[4], PS[5]], [PS[6], PS[7]]]
    GS = [SC_H[:, 0:512], SC_H[:, 513:1025]]
    wgu_wd = {}
    piece_pe_free = [0, 0]
    mid_dv = [0, 0]
    gs_free = [0, 0]
    wgu_wd[0] = dma("pool", WG[0], wgu_h[0][:, 0:2048], "wg0")
    wgu_wd[1] = dma("pool", WG[1], wgu_h[0][:, 2048:4096], "wg1")
    for oi in range(IC):
        b = oi % 2
        S.wait("pe", "pl", h3_pl)
        S.wait("pe", "dv", h3_dv)
        S.wait("pe", "dv", mid_dv[b])
        grp_pe = []
        for part in range(2):
            ip = oi * 2 + part
            S.waitt("pe", wgu_wd[ip])
            ps_ = PG[b] if part == 0 else PU[b]
            for tt in range(2):
                for kc in range(C):
                    mm(ps_[tt][:, 0:512], WG[ip % 2][:, kc * 128:(kc + 1) * 128],
                       H3[:, kc * T + tt * 512: kc * T + tt * 512 + 512],
                       start=(kc == 0), stop=(kc == C - 1),
                       inc=("pe" if kc == C - 1 else None))
                grp_pe.append(S.ctr["pe"])
            piece_pe_free[ip % 2] = S.ctr["pe"]
            if ip + 2 < IC * 2:
                nip = ip + 2
                S.wait("pool", "pe", piece_pe_free[ip % 2])
                srcp = wgu_h[nip // 2][:, (nip % 2) * 2048:(nip % 2) * 2048 + 2048]
                wgu_wd[nip] = dma("pool", WG[nip % 2], srcp, f"wg{nip % 2}")
        for tt in range(2):
            S.wait("act", "pe", grp_pe[tt])
            S.wait("act", "dv", gs_free[tt])
            act(GS[tt], PG[b][tt][:, 0:512], AF.Silu, inc="ac")
            S.wait("dve", "ac", S.ctr["ac"])
            S.wait("dve", "pe", grp_pe[2 + tt])
            dve("tensor_mul", MID[:, oi * T + tt * 512: oi * T + tt * 512 + 512],
                GS[tt], PU[b][tt][:, 0:512], inc="dv")
            gs_free[tt] = S.ctr["dv"]
        mid_dv[b] = S.ctr["dv"]
    mid_done_dv = S.ctr["dv"]
    if "d_mid" in dbg_out:
        S.wait("sp", "dv", mid_done_dv)
        for oi in range(IC):
            dma("sp", dbg_out["d_mid"][oi], MID[:, oi * T:(oi + 1) * T], "dbg")

    S.barrier()
    WD2 = [sba("WD20", [128, 2048], BF16, D0 + 163968),
           sba("WD21", [128, 2048], BF16, D0 + 168064),
           sba("WD22", [128, 2048], BF16, D0)]
    PD = [[PS[0], PS[1]], [PS[2], PS[3]]]
    XCH = [sba(f"XCHe{i}", [128, T], F32, D0 + 4224 + i * 4096) for i in range(2)]
    OCH = [sba(f"OCH{i}", [128, T], F32, D0 + 12544 + i * 4096) for i in range(2)]
    wdn_wd = {}
    wd_pe_free = [0, 0, 0]
    out_dv = [0, 0]
    out_od = [None, None]
    for i in range(3):
        wdn_wd[i] = dma("pool", WD2[i], wdn_h[i], f"wn{i}")
    for oc2 in range(C):
        b = oc2 % 2
        S.wait("sp", "dv", out_dv[b])
        xd_tok = dma("sp", XCH[b], x1_spill[oc2], f"xch{b}")
        S.wait("pe", "dv", out_dv[b])
        for kcg in range(4):
            idx = oc2 * 4 + kcg
            S.waitt("pe", wdn_wd[idx])
            S.wait("pe", "dv", mid_done_dv)
            for kc in range(C):
                ic = kcg * 16 + kc
                for tt in range(2):
                    mm(PD[b][tt][:, 0:512], WD2[idx % 3][:, kc * 128:(kc + 1) * 128],
                       MID[:, ic * T + tt * 512: ic * T + tt * 512 + 512],
                       start=(kcg == 0 and kc == 0), stop=(kcg == 3 and kc == C - 1),
                       inc=("pe" if (kc == C - 1 and tt == 1) else None))
            wd_pe_free[idx % 3] = S.ctr["pe"]
            if idx + 3 < IC:
                S.wait("pool", "pe", wd_pe_free[idx % 3])
                wdn_wd[idx + 3] = dma("pool", WD2[idx % 3], wdn_h[idx + 3], f"wn{idx % 3}")
        pe_dn = S.ctr["pe"]
        S.waitt("dve", xd_tok)
        S.wait("dve", "pe", pe_dn)
        S.waitt("dve", out_od[b])
        dve("tensor_add", OCH[b][:, 0:512], XCH[b][:, 0:512], PD[b][0][:, 0:512])
        dve("tensor_add", OCH[b][:, 512:1024], XCH[b][:, 512:1024], PD[b][1][:, 0:512],
            inc="dv")
        out_dv[b] = S.ctr["dv"]
        S.wait("sp", "dv", out_dv[b])
        out_od[b] = dma("sp", out_h[oc2], OCH[b], f"od{b}")
    S.wait("sp", "od0", S.ctr.get("od0", 0))
    S.wait("sp", "od1", S.ctr.get("od1", 0))

    with contextlib.ExitStack() as st:
        for name in list(S.sems):
            S.sems[name] = st.enter_context(nc.semaphore(name))
        with nc.Block() as block:
            S.replay(block)
    return nc


def _prep(inputs):
    f32 = np.float32
    _ = inputs.get("L0_attn_tm_decay")
    x = np.asarray(inputs["L0_attn_x"], f32)[0]
    Wk = np.asarray(inputs["L0_attn_Wk"], f32)
    Wv = np.asarray(inputs["L0_attn_Wv"], f32)
    Wr = np.asarray(inputs["L0_attn_Wr"], f32)
    Wo = np.asarray(inputs["L0_attn_Wo"], f32)
    Wkey = np.asarray(inputs["L0_attn_Wkey"], f32)
    Wval = np.asarray(inputs["L0_attn_Wval"], f32)
    Wcr = np.asarray(inputs["L0_attn_Wcr"], f32)
    Wgu = np.asarray(inputs["L0_ffn_w_gate_up"], f32)
    Wdn = np.asarray(inputs["L0_ffn_w_down"], f32)

    A = np.stack([Wk, Wv, Wr])
    wkvr = np.ascontiguousarray(
        A.reshape(3, 16, 128, 16, 128).transpose(1, 4, 0, 3, 2).reshape(16, 128, 3 * 2048))
    wo = np.ascontiguousarray(
        Wo.reshape(16, 128, 16, 128).transpose(0, 3, 2, 1).reshape(16, 128, 2048))
    wkey = np.ascontiguousarray(
        Wkey.reshape(64, 128, 16, 128).transpose(0, 3, 2, 1).reshape(64, 128, 2048))
    wval = np.ascontiguousarray(
        Wval.reshape(16, 128, 4, 16, 128).transpose(0, 2, 4, 3, 1).reshape(64, 128, 2048))
    G = Wgu[:8192].reshape(64, 128, 16, 128)
    Uu = Wgu[8192:].reshape(64, 128, 16, 128)
    wgu = np.ascontiguousarray(
        np.stack([G, Uu]).transpose(1, 4, 0, 3, 2).reshape(64, 128, 4096))
    wdn = np.ascontiguousarray(
        Wdn.reshape(16, 128, 4, 16, 128).transpose(0, 2, 4, 3, 1).reshape(64, 128, 2048))
    wcr = np.ascontiguousarray(
        Wcr.reshape(16, 128, 16, 128).transpose(0, 3, 2, 1).reshape(16, 128, 2048))

    vec_list = [
        inputs["L0_attn_ln1_g"], inputs["L0_attn_ln1_b"],
        inputs["L0_attn_ln2_g"], inputs["L0_attn_ln2_b"],
        inputs["L0_attn_tm_first"],
        np.asarray(inputs["L0_attn_tm_mk"]).reshape(-1),
        np.asarray(inputs["L0_attn_tm_mv"]).reshape(-1),
        np.asarray(inputs["L0_attn_tm_mr"]).reshape(-1),
        np.asarray(inputs["L0_attn_cm_mk"]).reshape(-1),
        np.asarray(inputs["L0_attn_cm_mr"]).reshape(-1),
    ]
    V = np.stack([np.asarray(v, f32).reshape(2048) for v in vec_list])
    vecs = np.zeros((128, NVEC), f32)
    vecs[:, :160] = V.reshape(10, 16, 128).transpose(2, 0, 1).reshape(128, 160)
    vecs[:, V_EPS] = EPS
    vecs[:, V_NHALF] = -0.5
    vecs[:, V_NONE] = -1.0
    vecs[:, V_HALF] = 0.5
    vecs[:, V_NEG] = NEG
    vecs = np.ascontiguousarray(vecs)

    x_fm = np.ascontiguousarray(x.T)
    in_maps = []
    for ci in range(NCORES):
        s = ci * T
        sl = np.zeros((2048, 1026), f32)
        if s - 2 < 0:
            sl[:, 2:] = x_fm[:, s:s + T]
        else:
            sl[:, :] = x_fm[:, s - 2:s + T]
        ximg = np.ascontiguousarray(
            sl.reshape(16, 128, 1026).transpose(1, 0, 2).reshape(128, 16 * 1026))
        hm = np.full((128, 1), 0.0 if ci == 0 else 1.0, f32)
        in_maps.append(dict(
            x_img=ximg, wkvr=wkvr, wo=wo, wkey=wkey, wval=wval,
            wgu=wgu, wdn=wdn, wcr=wcr, vecs=vecs, hmask=hm))
    return in_maps


def kernel(**inputs):
    in_maps = _prep(inputs)
    nc = _build()
    res = run_bass_kernel_spmd(nc, in_maps, core_ids=list(range(NCORES)))
    parts = []
    for ci in range(NCORES):
        o = np.asarray(res.results[ci]["out"], np.float32)
        parts.append(o.reshape(2048, T))
    full = np.concatenate(parts, axis=1)
    return np.ascontiguousarray(full.T).reshape(1, 8192, 2048).astype(np.float32)


# version 07
# speedup vs baseline: 1.0335x; 1.0335x over previous
import contextlib
import numpy as np
import concourse.bass as bass
import concourse.mybir as mybir
from concourse.bass_utils import run_bass_kernel_spmd

F32 = mybir.dt.float32
BF16 = mybir.dt.bfloat16
AF = mybir.ActivationFunctionType
ALU = mybir.AluOpType

NCORES = 8
T = 1024
C = 16
IC = 64
H = 2048
EPS = 1e-5
NEG = -1e38

V_G1, V_B1, V_G2, V_B2, V_TMF, V_MK, V_MV, V_MR, V_CMK, V_CMR = (
    0, 16, 32, 48, 64, 80, 96, 112, 128, 144)
V_EPS, V_NHALF, V_NONE, V_HALF, V_NEG = 160, 161, 162, 163, 164
NVEC = 168

ROWS3 = [0, 32, 64]


def _tiles(n, cap=512):
    out, s = [], 0
    while s < n:
        l = min(cap, n - s)
        out.append((s, l))
        s += l
    return out


class Sched:
    ENGS = ("pe", "dve", "act", "sp", "pool")

    def __init__(self):
        self.items = {e: [] for e in self.ENGS}
        self.sems = {}
        self.ctr = {}
        self.hiw = {}

    def sem(self, name):
        if name not in self.sems:
            self.sems[name] = None
            self.ctr[name] = 0

    def emit(self, eng, fn, inc=None, incn=1):
        if inc is not None:
            self.ctr[inc] += incn
        self.items[eng].append((fn, inc, incn))
        return self.ctr[inc] if inc is not None else None

    def wait(self, eng, sem, val):
        if val is None or val <= 0:
            return
        key = (eng, sem)
        if self.hiw.get(key, 0) >= val:
            return
        self.hiw[key] = val

        def f(e, sem=sem, val=val):
            e.wait_ge(self.sems[sem], val)
            return None
        self.items[eng].append((f, None, 1))

    def barrier(self, engs=None):
        vals = [(s, self.ctr[s]) for s in self.ctr]
        for eng in (engs or self.ENGS):
            for s, v in vals:
                self.wait(eng, s, v)

    def waitt(self, eng, tok):
        if tok:
            self.wait(eng, tok[0], tok[1])

    def replay(self, block):
        def mk(name):
            def run(e):
                for fn, inc, incn in self.items[name]:
                    inst = fn(e)
                    if inc is not None:
                        inst.then_inc(self.sems[inc], incn)
            return run
        block.gpsimd(mk("pool"))
        block.sync(mk("sp"))
        block.tensor(mk("pe"))
        block.vector(mk("dve"))
        block.scalar(mk("act"))


def _build(dbg=()):
    nc = bass.Bass("TRN2", target_bir_lowering=False)
    dbg = set(dbg)

    x_img = nc.dram_tensor("x_img", [128, C * 1026], F32, kind="ExternalInput").ap()
    wkvr_h = nc.dram_tensor("wkvr", [C, 128, 3 * 2048], F32, kind="ExternalInput").ap()
    wo_h = nc.dram_tensor("wo", [C, 128, 2048], F32, kind="ExternalInput").ap()
    wkey_h = nc.dram_tensor("wkey", [IC, 128, 2048], F32, kind="ExternalInput").ap()
    wval_h = nc.dram_tensor("wval", [IC, 128, 2048], F32, kind="ExternalInput").ap()
    wgu_h = nc.dram_tensor("wgu", [IC, 128, 4096], F32, kind="ExternalInput").ap()
    wdn_h = nc.dram_tensor("wdn", [IC, 128, 2048], F32, kind="ExternalInput").ap()
    wcr_h = nc.dram_tensor("wcr", [C, 128, 2048], F32, kind="ExternalInput").ap()
    vecs_h = nc.dram_tensor("vecs", [128, NVEC], F32, kind="ExternalInput").ap()
    hmask_h = nc.dram_tensor("hmask", [128, 1], F32, kind="ExternalInput").ap()
    out_h = nc.dram_tensor("out", [C, 128, T], F32, kind="ExternalOutput").ap()

    def scratch(name, shape, dt):
        kind = "ExternalOutput" if name in dbg else "Internal"
        return nc.dram_tensor(name, shape, dt, kind=kind).ap()

    xm_spill = scratch("xm_sp", [C, 128, 1025], F32)
    cmr_spill = scratch("cmr_sp", [C, 128, T], BF16)
    x1_spill = scratch("x1_sp", [C, 128, T], F32)
    kv_spill = scratch("kv_sp", [C, 128, T], F32)

    dbg_out = {}
    for nm, shape, dt in [
        ("d_bca", [128, 1026], F32), ("d_bcb", [128, 1026], F32),
        ("d_xk", [C, 128, 1025], BF16), ("d_xv", [C, 128, 1025], BF16),
        ("d_xr", [C, 128, 1025], BF16), ("d_u", [C, 128, 1025], BF16),
        ("d_cmk", [C, 128, T], BF16), ("d_h3", [C, 128, T], BF16),
        ("d_silu", [IC, 128, T], BF16), ("d_mid", [IC, 128, T], BF16),
        ("d_k", [C, 128, T], F32), ("d_v", [C, 128, T], F32),
        ("d_r", [C, 128, T], F32), ("d_stv", [6, 1, 1026], F32),
        ("d_hkvr", [128, 192], F32),
    ]:
        if nm in dbg:
            dbg_out[nm] = nc.dram_tensor(nm, shape, dt, kind="ExternalOutput").ap()

    S = Sched()
    for s in ("pe", "dv", "ac", "pl"):
        S.sem(s)

    BASE = (nc.sbuf_base + 127) // 128 * 128
    TOP = nc.sbuf_base + nc.sbuf_bytes_remaining

    def sba(name, shape, dtype, off):
        a = BASE + off
        nb = int(np.prod(shape[1:])) * (2 if dtype == BF16 else 4)
        assert a + nb <= TOP, (name, a + nb, TOP)
        return nc.alloc_sbuf_tensor_at(name, shape, dtype, offset=a).ap()

    VEC = sba("VEC", [128, NVEC], F32, 0)
    HM = sba("HM", [128, 1], F32, 768)
    ONESB = sba("ONESB", [128, 1], BF16, 896)
    ONESC = sba("ONESC", [1, 128], F32, 1024)
    STV = [sba(f"STV{i}", [1, 1026], F32, 1536 + i * 4224) for i in range(6)]
    SL_A0, SL_B0, SL_A, SL_B, SL_T0, SL_T1 = range(6)
    SC_H = sba("SC_H", [128, 1026], F32, 26880)
    SC_D = sba("SC_D", [128, 1026], F32, 31104)
    D0 = 35328

    def stv(i, n=1026):
        return STV[i][0:1, 0:n]

    def vcol(base, fc):
        return VEC[:, base + fc: base + fc + 1]

    PS = [nc.alloc_psum_tensor(f"PSB{i}", [128, 512], F32).ap() for i in range(8)]

    def dma(eng, dst, src, sem):
        S.sem(sem)

        def f(e, dst=dst, src=src):
            return e.dma_start(dst, src)
        return (sem, S.emit(eng, f, inc=sem, incn=16))

    def mm(psum, lhsT, rhs, start, stop, inc=None):
        def f(e, psum=psum, lhsT=lhsT, rhs=rhs, start=start, stop=stop):
            return e.matmul(psum, lhsT, rhs, start=start, stop=stop)
        return S.emit("pe", f, inc=inc)

    def act(out, in_, func, bias=0.0, scale=1.0, inc=None):
        def f(e, out=out, in_=in_, func=func, bias=bias, scale=scale):
            return e.activation(out, in_, func, bias=bias, scale=scale)
        return S.emit("act", f, inc=inc)

    def dve(op, *args, inc=None):
        def f(e, op=op, args=args):
            return getattr(e, op)(*args)
        return S.emit("dve", f, inc=inc)

    def pool(op, *args, inc=None):
        def f(e, op=op, args=args):
            return getattr(e, op)(*args)
        return S.emit("pool", f, inc=inc)

    def stats_finish(pa, pb, ncols, slot_a, slot_b, pe_done):
        tl = _tiles(ncols)
        S.wait("act", "pe", pe_done)
        for ti, (s, l) in enumerate(tl):
            act(stv(SL_T0)[0:1, s:s + l], pa[ROWS3[ti]:ROWS3[ti] + 1, 0:l], AF.Identity)
        for ti, (s, l) in enumerate(tl):
            act(stv(SL_T1)[0:1, s:s + l], pb[ROWS3[ti]:ROWS3[ti] + 1, 0:l], AF.Identity,
                inc=("ac" if ti == len(tl) - 1 else None))
        S.wait("dve", "ac", S.ctr["ac"])
        m = stv(SL_T0, ncols)
        q = stv(SL_T1, ncols)
        a = stv(slot_a, ncols)
        b = stv(slot_b, ncols)
        dve("tensor_scalar", m, m, 1.0 / H, None, ALU.mult)
        dve("tensor_scalar", q, q, 1.0 / H, None, ALU.mult)
        dve("tensor_mul", b, m, m)
        dve("tensor_sub", q, q, b, inc="dv")
        S.wait("act", "dv", S.ctr["dv"])
        act(a, q, AF.Ln, bias=VEC[0:1, V_EPS:V_EPS + 1])
        act(a, a, AF.Exp, scale=VEC[0:1, V_NHALF:V_NHALF + 1], inc="ac")
        S.wait("dve", "ac", S.ctr["ac"])
        dve("scalar_tensor_tensor", b, m, -1.0, a, ALU.mult, ALU.mult, inc="dv")
        return S.ctr["dv"]

    def broadcast(vec_ap, dst, ncols, dv_ready, psb):
        S.wait("pe", "dv", dv_ready)
        last = None
        for (s, l) in _tiles(ncols):
            S.wait("pe", "ac", S.ctr["ac"])
            mm(psb[:, 0:l], ONESC, vec_ap[0:1, s:s + l], start=True, stop=True, inc="pe")
            S.wait("act", "pe", S.ctr["pe"])
            last = act(dst[:, s:s + l], psb[:, 0:l], AF.Identity, inc="ac")
        return last

    M2 = sba("M2", [128, 512], F32, 209920)
    dma("sp", VEC, vecs_h, "ld")
    hm_tok = dma("sp", HM, hmask_h, "ld")
    for e_ in ("act", "dve", "pool"):
        S.wait(e_, "ld", 32)
    dve("memset", ONESB, 1.0)
    dve("memset", ONESC, 1.0, inc="dv")
    dve("memset", M2, 1.0, inc="dv")
    S.wait("dve", "dv", S.ctr["dv"])
    dve("tensor_copy", M2[:, 0:1], HM, inc="dv")
    S.wait("dve", "dv", S.ctr["dv"])
    dve("tensor_copy", M2[:, 1:2], HM, inc="dv")

    XBF = sba("XBF", [128, C * 1026], BF16, D0)
    xbf_tok = dma("pool", XBF, x_img, "xbf")

    SQ = [sba(f"SQ{i}", [128, 1026], BF16, D0 + 33024 + i * 2176) for i in range(2)]
    pa, pb = PS[0], PS[1]
    tl6 = _tiles(1026)
    sq_pe_free = [0, 0]
    for fc in range(C):
        b = fc % 2
        xv = XBF[:, fc * 1026:(fc + 1) * 1026]
        S.waitt("act", xbf_tok)
        S.wait("act", "pe", sq_pe_free[b])
        act(SQ[b], xv, AF.Square, inc="ac")
        sq_ac = S.ctr["ac"]
        S.waitt("pe", xbf_tok)
        S.wait("pe", "ac", sq_ac)
        for ti, (s, l) in enumerate(tl6):
            mm(pa[ROWS3[ti]:ROWS3[ti] + 1, 0:l], ONESB, xv[:, s:s + l],
               start=(fc == 0), stop=(fc == C - 1))
        for ti, (s, l) in enumerate(tl6):
            mm(pb[ROWS3[ti]:ROWS3[ti] + 1, 0:l], ONESB, SQ[b][:, s:s + l],
               start=(fc == 0), stop=(fc == C - 1),
               inc=("pe" if ti == len(tl6) - 1 else None))
        sq_pe_free[b] = S.ctr["pe"]
    a0_dv = stats_finish(pa, pb, 1026, SL_A0, SL_B0, S.ctr["pe"])

    BC_A = sba("BC_A1", [128, 1026], F32, D0 + 165056)
    BC_B = sba("BC_B1", [128, 1026], F32, D0 + 169280)
    broadcast(stv(SL_A0), BC_A, 1026, a0_dv, PS[2])
    bc_ac = broadcast(stv(SL_B0), BC_B, 1026, a0_dv, PS[2])

    XCH = [sba(f"XCHa{i}", [128, 1026], F32, D0 + 131584 + i * 4224) for i in range(2)]
    XABF = [sba(f"XABF{i}", [128, 1026], BF16, D0 + 37376 + i * 2176) for i in range(2)]
    xch_free = [0, 0]
    st_pe_free = [0, 0]
    sch_ac_free = 0
    for fc in range(C):
        b = fc % 2
        S.wait("sp", "dv", xch_free[b])
        xd_tok = dma("sp", XCH[b], x_img[:, fc * 1026:(fc + 1) * 1026], f"xch{b}")
        S.waitt("dve", xd_tok)
        S.wait("dve", "ac", bc_ac)
        S.wait("dve", "pe", st_pe_free[b])
        S.wait("dve", "ac", sch_ac_free)
        dve("tensor_mul", SC_H, XCH[b], BC_A)
        dve("tensor_add", SC_H, SC_H, BC_B)
        dve("tensor_copy", XABF[b], SC_H, inc="dv")
        xch_free[b] = S.ctr["dv"]
        S.wait("act", "dv", S.ctr["dv"])
        S.wait("act", "pe", st_pe_free[b])
        act(SQ[b], SC_H, AF.Square, inc="ac")
        sch_ac_free = S.ctr["ac"]
        S.wait("pe", "dv", S.ctr["dv"])
        S.wait("pe", "ac", S.ctr["ac"])
        for ti, (s, l) in enumerate(tl6):
            mm(pa[ROWS3[ti]:ROWS3[ti] + 1, 0:l], ONESB, XABF[b][:, s:s + l],
               start=(fc == 0), stop=(fc == C - 1))
        for ti, (s, l) in enumerate(tl6):
            mm(pb[ROWS3[ti]:ROWS3[ti] + 1, 0:l], ONESB, SQ[b][:, s:s + l],
               start=(fc == 0), stop=(fc == C - 1),
               inc=("pe" if ti == len(tl6) - 1 else None))
        st_pe_free[b] = S.ctr["pe"]
    a1_dv = stats_finish(pa, pb, 1026, SL_A, SL_B, S.ctr["pe"])
    S.wait("dve", "dv", a1_dv)
    dve("tensor_mul", stv(SL_T0), stv(SL_B0), stv(SL_A))
    dve("tensor_add", stv(SL_T0), stv(SL_T0), stv(SL_B))
    dve("tensor_mul", stv(SL_T1), stv(SL_A0), stv(SL_A), inc="dv")
    a01_dv = S.ctr["dv"]
    broadcast(stv(SL_T1), BC_A, 1026, a01_dv, PS[2])
    bc_ac = broadcast(stv(SL_T0), BC_B, 1026, a01_dv, PS[2])
    if "d_stv" in dbg_out:
        S.wait("sp", "dv", a01_dv)
        for i in range(6):
            dma("sp", dbg_out["d_stv"][i], STV[i][0:1, 0:1026], "dbg")

    XK = sba("XK", [128, C * 1025], BF16, D0)
    XV = sba("XV", [128, C * 1025], BF16, D0 + 32896)
    XR = sba("XR", [128, C * 1025], BF16, D0 + 65792)
    WKV3 = [sba(f"WKV3{i}", [128, 2048], BF16, D0 + 148480 + i * 4096)
            for i in range(3)]
    wkvr_wd = {}

    def wkvr_slice(si):
        oc, mat = (si % 48) // 3, si % 3
        return dma("pool", WKV3[si % 3], wkvr_h[oc][:, mat * 2048:(mat + 1) * 2048],
                   f"wk{si % 3}")

    for si in range(3):
        wkvr_wd[si] = wkvr_slice(si)

    XCH3 = [sba(f"XCH3{i}", [128, 1026], F32, D0 + 139776 + i * 4224) for i in range(2)]
    SC_H2 = sba("SC_H2", [128, 1026], F32, D0 + 160768)
    SC_HX = [SC_H, SC_H2]
    hx_free = [0, 0]
    S.wait("sp", "dv", max(xch_free))
    xch_free = [0, 0]

    def mix_pass(lo, hi, first):
        for fc in range(C):
            b = fc % 2
            S.wait("sp", "pl", xch_free[b])
            xd_tok = dma("sp", XCH3[b], x_img[:, fc * 1026:(fc + 1) * 1026], f"xch{b}")
            hx = SC_HX[b]
            S.waitt("pool", xd_tok)
            S.wait("pool", "ac", bc_ac)
            S.wait("pool", "dv", hx_free[b])
            pool("tensor_mul", hx, XCH3[b], BC_A)
            pool("tensor_add", hx, hx, BC_B, inc="pl")
            xch_free[b] = S.ctr["pl"]
            S.wait("act", "pl", S.ctr["pl"])
            act(hx, hx, AF.Identity, scale=vcol(V_G1, fc), bias=vcol(V_B1, fc), inc="ac")
            S.wait("dve", "ac", S.ctr["ac"])
            if first:
                dve("tensor_mul", hx[:, 0:512], hx[:, 0:512], M2)
            dve("tensor_sub", SC_D[:, lo:hi], hx[:, lo + 1:hi + 1], hx[:, lo:hi])
            o = fc * 1025
            dve("scalar_tensor_tensor", XK[:, o + lo:o + hi], SC_D[:, lo:hi],
                vcol(V_MK, fc), hx[:, lo:hi], ALU.mult, ALU.add)
            dve("scalar_tensor_tensor", XV[:, o + lo:o + hi], SC_D[:, lo:hi],
                vcol(V_MV, fc), hx[:, lo:hi], ALU.mult, ALU.add)
            dve("scalar_tensor_tensor", XR[:, o + lo:o + hi], SC_D[:, lo:hi],
                vcol(V_MR, fc), hx[:, lo:hi], ALU.mult, ALU.add, inc="dv")
            hx_free[b] = S.ctr["dv"]

    mix_pass(0, 513, True)
    mix1_dv = S.ctr["dv"]
    mix_pass(513, 1025, False)
    mix2_dv = S.ctr["dv"]
    mix2_pl = S.ctr["pl"]
    if "d_xk" in dbg_out:
        S.wait("sp", "dv", mix2_dv)
        for fc in range(C):
            dma("sp", dbg_out["d_xk"][fc], XK[:, fc * 1025:(fc + 1) * 1025], "dbg")
            dma("sp", dbg_out["d_xv"][fc], XV[:, fc * 1025:(fc + 1) * 1025], "dbg")
            dma("sp", dbg_out["d_xr"][fc], XR[:, fc * 1025:(fc + 1) * 1025], "dbg")
    if "d_bca" in dbg_out:
        S.wait("sp", "ac", bc_ac)
        dma("sp", dbg_out["d_bca"], BC_A, "dbg")
        dma("sp", dbg_out["d_bcb"], BC_B, "dbg")

    U = sba("U", [128, C * 1025], BF16, D0 + 98688)
    CH = [sba(f"CH{i}", [128, 512], F32, D0 + 131584 + i * 2048) for i in range(4)]
    PSK = [PS[0], PS[1]]
    PSV = [PS[2], PS[3]]
    PSR = [PS[4], PS[5]]
    HK, HV, HR = PS[1][:, 0:1], PS[3][:, 0:1], PS[5][:, 0:1]
    HW = PS[6][:, 0:1]
    HSTX = sba("HSTX", [128, 192], F32, 209152)
    HST = HSTX[:, 0:48]
    SPC = SC_H2[:, 0:512]

    def kvr_chain(oc, tt, pe_k, pe_v, pe_r):
        c0, c1, c2, c3 = CH
        tok0 = 1 + tt * 512
        uo = U[:, oc * 1025 + tok0: oc * 1025 + tok0 + 512]
        S.wait("act", "pe", pe_k)
        act(c0, PSK[tt][:, 0:512], AF.Identity, bias=vcol(V_TMF, oc), inc="ac")
        S.wait("dve", "ac", S.ctr["ac"])
        dve("tensor_scalar", c1, c0, VEC[:, V_NEG:V_NEG + 1], None, ALU.max)
        dve("tensor_sub", c0, c0, c1, inc="dv")
        S.wait("act", "dv", S.ctr["dv"])
        act(c1, c0, AF.Exp)
        act(c2, c0, AF.Exp, scale=VEC[:, V_NONE:V_NONE + 1])
        S.wait("act", "pe", pe_r)
        act(c3, PSR[tt][:, 0:512], AF.Tanh, scale=VEC[:, V_HALF:V_HALF + 1], inc="ac")
        S.wait("dve", "ac", S.ctr["ac"])
        S.wait("dve", "pe", pe_v)
        dve("scalar_tensor_tensor", c0, PSV[tt][:, 0:512], 0.5, c1, ALU.mult, ALU.mult)
        dve("tensor_mul", c0, c0, c2)
        dve("scalar_tensor_tensor", uo, c3, 1.0, c0, ALU.add, ALU.mult, inc="dv")
        return S.ctr["dv"]

    chain_dv0 = 0
    halo_stage_ac = 0
    slice_pe = {}
    for oc in range(C):
        S.wait("pe", "dv", mix1_dv)
        S.wait("pe", "dv", chain_dv0)
        S.wait("pe", "ac", halo_stage_ac)
        pes = []
        for mi, dst, dsth, mixsrc in ((0, PSK, HK, XK), (1, PSV, HV, XV),
                                      (2, PSR, HR, XR)):
            si = oc * 3 + mi
            wb = WKV3[si % 3]
            S.waitt("pe", wkvr_wd[si])
            for kc in range(C):
                mm(dst[0][:, 0:512],
                   wb[:, kc * 128:(kc + 1) * 128],
                   mixsrc[:, kc * 1025 + 1: kc * 1025 + 1 + 512],
                   start=(kc == 0), stop=(kc == C - 1),
                   inc=("pe" if kc == C - 1 else None))
            pes.append(S.ctr["pe"])
            for kc in range(C):
                mm(dsth, wb[:, kc * 128:(kc + 1) * 128],
                   mixsrc[:, kc * 1025: kc * 1025 + 1],
                   start=(kc == 0), stop=(kc == C - 1),
                   inc=("pe" if kc == C - 1 else None))
            slice_pe[si % 3] = S.ctr["pe"]
            nsi = si + 3
            if nsi < 48:
                S.wait("pool", "pe", slice_pe[nsi % 3])
                wkvr_wd[nsi] = wkvr_slice(nsi)
        pe_halo = S.ctr["pe"]
        chain_dv0 = kvr_chain(oc, 0, pes[0], pes[1], pes[2])
        S.wait("act", "pe", pe_halo)
        act(HST[:, oc:oc + 1], HK, AF.Identity)
        act(HST[:, 16 + oc:16 + oc + 1], HV, AF.Identity)
        act(HST[:, 32 + oc:32 + oc + 1], HR, AF.Identity, inc="ac")
        halo_stage_ac = S.ctr["ac"]

    W16 = HSTX[:, 48:64]
    P16 = HSTX[:, 64:80]
    E16 = HSTX[:, 80:96]
    R16 = HSTX[:, 96:112]
    T16 = HSTX[:, 112:128]
    N16 = HSTX[:, 128:144]
    UH16 = HSTX[:, 144:160]
    S.wait("act", "ac", halo_stage_ac)
    act(SPC, SPC, AF.Identity, inc="ac")
    S.wait("dve", "ac", S.ctr["ac"])
    dve("tensor_add", W16, HST[:, 0:16], VEC[:, V_TMF:V_TMF + 16])
    dve("tensor_copy", SPC, SPC)
    dve("tensor_scalar", P16, W16, VEC[:, V_NEG:V_NEG + 1], None, ALU.max)
    dve("tensor_copy", SPC, SPC)
    dve("tensor_sub", W16, W16, P16)
    dve("tensor_copy", SPC, SPC, inc="dv")
    S.wait("act", "dv", S.ctr["dv"])
    act(E16, W16, AF.Exp)
    act(R16, W16, AF.Exp, scale=VEC[:, V_NONE:V_NONE + 1])
    act(T16, HST[:, 32:48], AF.Tanh, scale=VEC[:, V_HALF:V_HALF + 1])
    act(SPC, SPC, AF.Identity, inc="ac")
    S.wait("dve", "ac", S.ctr["ac"])
    dve("scalar_tensor_tensor", N16, HST[:, 16:32], 0.5, E16, ALU.mult, ALU.mult)
    dve("tensor_copy", SPC, SPC)
    dve("tensor_mul", N16, N16, R16)
    dve("tensor_copy", SPC, SPC)
    dve("scalar_tensor_tensor", UH16, T16, 1.0, N16, ALU.add, ALU.mult)
    dve("tensor_copy", SPC, SPC)
    for oc in range(C):
        dve("tensor_copy", U[:, oc * 1025: oc * 1025 + 1], UH16[:, oc:oc + 1])
    dve("tensor_copy", SPC, SPC, inc="dv")
    halo_done_dv = S.ctr["dv"]
    if "d_hkvr" in dbg_out:
        S.wait("sp", "dv", S.ctr["dv"])
        dma("sp", dbg_out["d_hkvr"], HSTX, "dbg")

    for si in range(48, 51):
        S.wait("pool", "pe", slice_pe[si % 3])
        wkvr_wd[si] = wkvr_slice(si)
    chain_dv1 = 0
    for oc in range(C):
        S.wait("pe", "dv", mix2_dv)
        S.wait("pe", "pl", mix2_pl)
        S.wait("pe", "dv", halo_done_dv)
        S.wait("pe", "dv", chain_dv1)
        pes = []
        for mi, dst, mixsrc in ((0, PSK, XK), (1, PSV, XV), (2, PSR, XR)):
            si = 48 + oc * 3 + mi
            wb = WKV3[si % 3]
            S.waitt("pe", wkvr_wd[si])
            for kc in range(C):
                mm(dst[1][:, 0:512],
                   wb[:, kc * 128:(kc + 1) * 128],
                   mixsrc[:, kc * 1025 + 513: kc * 1025 + 513 + 512],
                   start=(kc == 0), stop=(kc == C - 1),
                   inc=("pe" if kc == C - 1 else None))
            pes.append(S.ctr["pe"])
            slice_pe[si % 3] = S.ctr["pe"]
            nsi = si + 3
            if nsi < 96:
                S.wait("pool", "pe", slice_pe[nsi % 3])
                wkvr_wd[nsi] = wkvr_slice(nsi)
        chain_dv1 = kvr_chain(oc, 1, pes[0], pes[1], pes[2])
    u_done_dv = S.ctr["dv"]
    if "d_u" in dbg_out:
        S.wait("sp", "dv", u_done_dv)
        for fc in range(C):
            dma("sp", dbg_out["d_u"][fc], U[:, fc * 1025:(fc + 1) * 1025], "dbg")

    S.barrier()
    BC_A = sba("BC_A2", [128, 1026], F32, D0 + 165376)
    BC_B = sba("BC_B2", [128, 1026], F32, D0 + 169600)
    broadcast(stv(SL_A0), BC_A, 1026, a0_dv, PS[7])
    bc_ac = broadcast(stv(SL_B0), BC_B, 1026, a0_dv, PS[7])

    XCH = [sba(f"XCHb{i}", [128, 1026], F32, D0 + 139776 + i * 4224) for i in range(2)]
    WO = [sba(f"WO{i}", [128, 2048], BF16, D0 + 131584 + i * 4096) for i in range(2)]
    PSW = [[PS[0], PS[1]], [PS[2], PS[3]]]
    XMCH = [sba(f"XMCH{i}", [128, 1025], F32, D0 + 148224 + i * 4224) for i in range(2)]
    XMBF = [sba(f"XMBF{i}", [128, 1025], BF16, D0 + 156672 + i * 2176) for i in range(2)]
    SQ5 = [sba(f"SQ5{i}", [128, 1025], BF16, D0 + 161024 + i * 2176) for i in range(2)]
    pa, pb = PS[4], PS[5]
    tl5 = _tiles(1025)

    wo_wd = {}
    wo_wd[0] = dma("pool", WO[0], wo_h[0], "wwo0")
    wo_wd[1] = dma("pool", WO[1], wo_h[1], "wwo1")
    wo_pe_free = [0, 0]
    xm_halo_dv_last = 0
    xm_dv = [0, 0]
    xm_spilled = [None, None]
    xm_stats_pe = [0, 0]
    xch_free = [0, 0]
    sq5_ac = [0, 0]
    for oc in range(C):
        b = oc % 2
        S.waitt("pe", wo_wd[oc])
        S.wait("pe", "dv", u_done_dv)
        S.wait("pe", "dv", xm_dv[b])
        for tt in range(2):
            tok0 = 1 + tt * 512
            for kc in range(C):
                mm(PSW[b][tt][:, 0:512], WO[b][:, kc * 128:(kc + 1) * 128],
                   U[:, kc * 1025 + tok0: kc * 1025 + tok0 + 512],
                   start=(kc == 0), stop=(kc == C - 1),
                   inc=("pe" if kc == C - 1 else None))
        S.wait("pe", "dv", xm_halo_dv_last)
        for kc in range(C):
            mm(HW, WO[b][:, kc * 128:(kc + 1) * 128],
               U[:, kc * 1025: kc * 1025 + 1],
               start=(kc == 0), stop=(kc == C - 1),
               inc=("pe" if kc == C - 1 else None))
        pe_wo = S.ctr["pe"]
        wo_pe_free[b] = pe_wo
        if oc + 2 < C:
            S.wait("pool", "pe", pe_wo)
            wo_wd[oc + 2] = dma("pool", WO[b], wo_h[oc + 2], f"wwo{b}")
        S.wait("sp", "dv", xch_free[b])
        xd_tok = dma("sp", XCH[b], x_img[:, oc * 1026:(oc + 1) * 1026], f"xch{b}")
        S.waitt("dve", xd_tok)
        S.wait("dve", "ac", bc_ac)
        S.waitt("dve", xm_spilled[b])
        S.wait("dve", "ac", sq5_ac[b])
        dve("tensor_mul", XMCH[b], XCH[b][:, 1:1026], BC_A[:, 1:1026])
        dve("tensor_add", XMCH[b], XMCH[b], BC_B[:, 1:1026], inc="dv")
        xch_free[b] = S.ctr["dv"]
        S.wait("dve", "pe", pe_wo)
        dve("tensor_add", XMCH[b][:, 0:1], XMCH[b][:, 0:1], HW, inc="dv")
        xm_halo_dv_last = S.ctr["dv"]
        dve("tensor_add", XMCH[b][:, 1:513], XMCH[b][:, 1:513], PSW[b][0][:, 0:512])
        dve("tensor_add", XMCH[b][:, 513:1025], XMCH[b][:, 513:1025], PSW[b][1][:, 0:512])
        S.wait("dve", "pe", xm_stats_pe[b])
        dve("tensor_copy", XMBF[b], XMCH[b], inc="dv")
        xm_dv[b] = S.ctr["dv"]
        S.wait("act", "dv", xm_dv[b])
        S.wait("act", "pe", xm_stats_pe[b])
        act(SQ5[b], XMCH[b], AF.Square, inc="ac")
        ac_sq = S.ctr["ac"]
        sq5_ac[b] = ac_sq
        S.wait("pool", "dv", xm_dv[b])
        xm_spilled[b] = dma("pool", xm_spill[oc], XMCH[b], f"sxm{b}")
        S.wait("pe", "dv", xm_dv[b])
        S.wait("pe", "ac", ac_sq)
        for ti, (s, l) in enumerate(tl5):
            mm(pa[ROWS3[ti]:ROWS3[ti] + 1, 0:l], ONESB, XMBF[b][:, s:s + l],
               start=(oc == 0), stop=(oc == C - 1))
        for ti, (s, l) in enumerate(tl5):
            mm(pb[ROWS3[ti]:ROWS3[ti] + 1, 0:l], ONESB, SQ5[b][:, s:s + l],
               start=(oc == 0), stop=(oc == C - 1),
               inc=("pe" if ti == len(tl5) - 1 else None))
        xm_stats_pe[b] = S.ctr["pe"]
    a2_dv = stats_finish(pa, pb, 1025, SL_A, SL_B, S.ctr["pe"])
    broadcast(stv(SL_A, 1025), BC_A[:, 0:1025], 1025, a2_dv, PS[7])
    bc_ac = broadcast(stv(SL_B, 1025), BC_B[:, 0:1025], 1025, a2_dv, PS[7])

    S.barrier()
    CMK = sba("CMK", [128, C * T], BF16, D0)
    CMR = [sba(f"CMR{i}", [128, T], BF16, D0 + 32896 + i * 2048) for i in range(2)]
    SC_H2b = sba("SC_H2b", [128, 1026], F32, D0 + 73856)
    SC_H2c = sba("SC_H2c", [128, 1026], F32, D0 + 78080)
    XCHb3 = sba("XCHb3", [128, 1026], F32, D0 + 82304)
    CMR.append(sba("CMR2", [128, T], BF16, D0 + 36992))
    SC_HX = [SC_H, SC_H2b, SC_H2c]
    XCH6 = [XCH[0], XCH[1], XCHb3]
    hx_free = [0, 0, 0]
    S.wait("sp", "dv", max(xch_free))
    xch_free = [0, 0, 0]
    cmr_spilled = [None, None, None]
    for fc in range(C):
        b = fc % 3
        S.wait("sp", "pl", xch_free[b])
        xd_tok = dma("sp", XCH6[b][:, 0:1025], xm_spill[fc], f"xch{b}")
        hx = SC_HX[b]
        S.waitt("pool", xd_tok)
        S.wait("pool", "ac", bc_ac)
        S.wait("pool", "dv", hx_free[b])
        pool("tensor_mul", hx[:, 0:1025], XCH6[b][:, 0:1025], BC_A[:, 0:1025])
        pool("tensor_add", hx[:, 0:1025], hx[:, 0:1025], BC_B[:, 0:1025], inc="pl")
        xch_free[b] = S.ctr["pl"]
        S.wait("act", "pl", S.ctr["pl"])
        act(hx[:, 0:1025], hx[:, 0:1025], AF.Identity,
            scale=vcol(V_G2, fc), bias=vcol(V_B2, fc), inc="ac")
        S.wait("dve", "ac", S.ctr["ac"])
        dve("tensor_mul", hx[:, 0:511], hx[:, 0:511], M2[:, 1:512])
        dve("tensor_sub", SC_D[:, 0:T], hx[:, 1:1025], hx[:, 0:T])
        dve("scalar_tensor_tensor", CMK[:, fc * T:(fc + 1) * T], SC_D[:, 0:T],
            vcol(V_CMK, fc), hx[:, 0:T], ALU.mult, ALU.add)
        S.waitt("dve", cmr_spilled[b])
        dve("scalar_tensor_tensor", CMR[b], SC_D[:, 0:T],
            vcol(V_CMR, fc), hx[:, 0:T], ALU.mult, ALU.add, inc="dv")
        hx_free[b] = S.ctr["dv"]
        S.wait("pool", "dv", S.ctr["dv"])
        cmr_spilled[b] = dma("pool", cmr_spill[fc], CMR[b], f"scm{b}")
    cmk_done_dv = S.ctr["dv"]
    if "d_cmk" in dbg_out:
        S.wait("sp", "dv", cmk_done_dv)
        for fc in range(C):
            dma("sp", dbg_out["d_cmk"][fc], CMK[:, fc * T:(fc + 1) * T], "dbg")

    S.barrier()
    SILU = sba("SILU", [128, IC * T], BF16, D0 + 32896)
    WK3 = [sba(f"WK3{i}", [128, 2048], BF16, D0 + 163968 + i * 4096) for i in range(2)]
    PSA = [PS[0], PS[1]]
    PSB = [PS[2], PS[3]]
    wkey_wd = {}
    wk_pe_free = [0, 0]
    silu_ac = [0, 0]
    for i in range(2):
        wkey_wd[i] = dma("pool", WK3[i], wkey_h[i], f"wy{i}")
    for oi in range(IC):
        S.waitt("pe", wkey_wd[oi])
        S.wait("pe", "dv", cmk_done_dv)
        ps = (PSA, PSB)[oi % 2]
        S.wait("pe", "ac", silu_ac[oi % 2])
        for tt in range(2):
            for kc in range(C):
                mm(ps[tt][:, 0:512], WK3[oi % 2][:, kc * 128:(kc + 1) * 128],
                   CMK[:, kc * T + tt * 512: kc * T + tt * 512 + 512],
                   start=(kc == 0), stop=(kc == C - 1),
                   inc=("pe" if kc == C - 1 else None))
        pe_kk = S.ctr["pe"]
        wk_pe_free[oi % 2] = pe_kk
        if oi + 2 < IC:
            S.wait("pool", "pe", pe_kk)
            wkey_wd[oi + 2] = dma("pool", WK3[oi % 2], wkey_h[oi + 2], f"wy{oi % 2}")
        S.wait("act", "pe", pe_kk)
        act(SILU[:, oi * T: oi * T + 512], ps[0][:, 0:512], AF.Silu)
        silu_ac[oi % 2] = act(SILU[:, oi * T + 512: oi * T + 1024], ps[1][:, 0:512],
                              AF.Silu, inc="ac")
    silu_done_ac = S.ctr["ac"]
    if "d_silu" in dbg_out:
        S.wait("sp", "ac", silu_done_ac)
        for oi in range(IC):
            dma("sp", dbg_out["d_silu"][oi], SILU[:, oi * T:(oi + 1) * T], "dbg")

    S.barrier()
    WV2 = [sba(f"WV2{i}", [128, 2048], BF16, D0 + 163968 + i * 4096) for i in range(2)]
    PKV = [[PS[0], PS[1]], [PS[2], PS[3]]]
    KVCH = [SC_H[:, 0:T], SC_D[:, 0:T]]
    CMRR = sba("CMRR", [128, C * T], BF16, D0)
    cmr_re_xd = 0
    for fc in range(C):
        cmr_re_xd = dma("sp", CMRR[:, fc * T:(fc + 1) * T], cmr_spill[fc], "cmrr")
    wval_wd = {}
    wv_pe_free = [0, 0]
    kv_evac_ac = [0, 0]
    kv_spilled = [None, None]
    for i in range(2):
        wval_wd[i] = dma("pool", WV2[i], wval_h[i], f"wv{i}")
    for oc2 in range(C):
        b = oc2 % 2
        S.wait("pe", "ac", kv_evac_ac[b])
        for kcg in range(4):
            idx = oc2 * 4 + kcg
            S.waitt("pe", wval_wd[idx])
            S.wait("pe", "ac", silu_done_ac)
            for kc in range(C):
                ic = kcg * 16 + kc
                for tt in range(2):
                    mm(PKV[b][tt][:, 0:512], WV2[idx % 2][:, kc * 128:(kc + 1) * 128],
                       SILU[:, ic * T + tt * 512: ic * T + tt * 512 + 512],
                       start=(kcg == 0 and kc == 0), stop=(kcg == 3 and kc == C - 1),
                       inc=("pe" if (kc == C - 1 and tt == 1) else None))
            wv_pe_free[idx % 2] = S.ctr["pe"]
            if idx + 2 < IC:
                S.wait("pool", "pe", wv_pe_free[idx % 2])
                wval_wd[idx + 2] = dma("pool", WV2[idx % 2], wval_h[idx + 2], f"wv{idx % 2}")
        pe_kv = S.ctr["pe"]
        S.wait("act", "pe", pe_kv)
        S.waitt("act", kv_spilled[b])
        act(KVCH[b][:, 0:512], PKV[b][0][:, 0:512], AF.Identity)
        kv_evac_ac[b] = act(KVCH[b][:, 512:1024], PKV[b][1][:, 0:512], AF.Identity,
                            inc="ac")
        S.wait("sp", "ac", kv_evac_ac[b])
        kv_spilled[b] = dma("sp", kv_spill[oc2], KVCH[b], f"skv{b}")

    S.barrier()
    WC2 = [sba(f"WC2{i}", [128, 2048], BF16, D0 + 32896 + i * 4096) for i in range(2)]
    PCR = [[PS[0], PS[1]], [PS[2], PS[3]]]
    pa, pb = PS[4], PS[5]
    XCH = [sba(f"XCHc{i}", [128, T], F32, D0 + 41088 + i * 4096) for i in range(2)]
    XM2 = [sba(f"XM2{i}", [128, T], F32, D0 + 49280 + i * 4096) for i in range(2)]
    KV2 = [sba(f"KV2{i}", [128, T], F32, D0 + 57472 + i * 4096) for i in range(2)]
    X1CH = [sba(f"X1CH{i}", [128, T], F32, D0 + 65664 + i * 4096) for i in range(2)]
    X1BF = [sba(f"X1BF{i}", [128, T], BF16, D0 + 73856 + i * 2048) for i in range(2)]
    SQ8 = [sba(f"SQ8{i}", [128, T], BF16, D0 + 77952 + i * 2048) for i in range(2)]
    CH0 = SC_H[:, 0:512]
    CH1 = SC_H[:, 513:1025]
    wcr_wd = {}
    wcr_wd[0] = dma("pool", WC2[0], wcr_h[0], "wc0")
    wcr_wd[1] = dma("pool", WC2[1], wcr_h[1], "wc1")
    wc_pe_free = [0, 0]
    sq8_ac = [0, 0]
    last_asm_dv = 0
    x1_dv = [0, 0]
    x1_spilled = [None, None]
    x1_stats_pe = [0, 0]
    tl4 = _tiles(T)
    stats_ready = {}

    def emit_x1_stats(j):
        bj = j % 2
        dvv, acv = stats_ready[j]
        S.wait("pe", "dv", dvv)
        S.wait("pe", "ac", acv)
        for ti, (s, l) in enumerate(tl4):
            mm(pa[ROWS3[ti]:ROWS3[ti] + 1, 0:l], ONESB, X1BF[bj][:, s:s + l],
               start=(j == 0), stop=(j == C - 1),
               inc=("pe" if ti == len(tl4) - 1 else None))
        for ti, (s, l) in enumerate(tl4):
            mm(pb[ROWS3[ti]:ROWS3[ti] + 1, 0:l], ONESB, SQ8[bj][:, s:s + l],
               start=(j == 0), stop=(j == C - 1),
               inc=("pe" if ti == len(tl4) - 1 else None))
        x1_stats_pe[bj] = S.ctr["pe"]
    def p8b_loads(j):
        bj = j % 2
        S.wait("sp", "dv", x1_dv[bj])
        t1 = dma("sp", XCH[bj], x_img[:, j * 1026 + 2:(j + 1) * 1026], f"xch{bj}")
        t2 = dma("sp", XM2[bj], xm_spill[j][:, 1:1025], f"xm2{bj}")
        t3 = dma("sp", KV2[bj], kv_spill[j], f"kv2{bj}")
        return (t1, t2, t3)

    xd_loads_of = {0: p8b_loads(0), 1: p8b_loads(1)}
    for oc2 in range(C):
        b = oc2 % 2
        xd_loads = xd_loads_of[oc2]
        S.waitt("pe", wcr_wd[oc2])
        S.waitt("pe", cmr_re_xd)
        S.wait("pe", "dv", x1_dv[b])
        for tt in range(2):
            for kc in range(C):
                mm(PCR[b][tt][:, 0:512], WC2[b][:, kc * 128:(kc + 1) * 128],
                   CMRR[:, kc * T + tt * 512: kc * T + tt * 512 + 512],
                   start=(kc == 0), stop=(kc == C - 1),
                   inc=("pe" if (kc == C - 1 and tt == 1) else None))
        pe_cr = S.ctr["pe"]
        wc_pe_free[b] = pe_cr
        if oc2 + 2 < C:
            S.wait("pool", "pe", pe_cr)
            wcr_wd[oc2 + 2] = dma("pool", WC2[b], wcr_h[oc2 + 2], f"wc{b}")
        for tok_ in xd_loads:
            S.waitt("dve", tok_)
        S.waitt("dve", x1_spilled[b])
        S.wait("dve", "ac", sq8_ac[b])
        S.wait("dve", "pe", pe_cr)
        S.wait("act", "pe", pe_cr)
        for tt in range(2):
            sl = slice(tt * 512, tt * 512 + 512)
            S.wait("act", "dv", last_asm_dv)
            act(CH0, PCR[b][tt][:, 0:512], AF.Sigmoid, inc="ac")
            S.wait("dve", "ac", S.ctr["ac"])
            dve("tensor_add", X1CH[b][:, sl], XCH[b][:, sl], XM2[b][:, sl])
            dve("tensor_mul", CH1, CH0, KV2[b][:, sl])
            dve("tensor_add", X1CH[b][:, sl], X1CH[b][:, sl], CH1, inc="dv")
            last_asm_dv = S.ctr["dv"]
        S.wait("dve", "pe", x1_stats_pe[b])
        dve("tensor_copy", X1BF[b], X1CH[b], inc="dv")
        x1_dv[b] = S.ctr["dv"]
        S.wait("act", "dv", x1_dv[b])
        S.wait("act", "pe", x1_stats_pe[b])
        act(SQ8[b], X1CH[b], AF.Square, inc="ac")
        ac_sq = S.ctr["ac"]
        sq8_ac[b] = ac_sq
        S.wait("sp", "dv", x1_dv[b])
        x1_spilled[b] = dma("sp", x1_spill[oc2], X1CH[b], f"sx1{b}")
        if oc2 + 2 < C:
            xd_loads_of[oc2 + 2] = p8b_loads(oc2 + 2)
        stats_ready[oc2] = (x1_dv[b], ac_sq)
        if oc2 > 0:
            emit_x1_stats(oc2 - 1)
    emit_x1_stats(C - 1)
    a3_dv = stats_finish(pa, pb, T, SL_A, SL_B, S.ctr["pe"])

    S.barrier()
    H3 = sba("H3", [128, C * T], BF16, D0)
    BC_A = sba("BC_A3", [128, T], F32, D0 + 32896)
    BC_B = sba("BC_B3", [128, T], F32, D0 + 36992)
    broadcast(stv(SL_A, T), BC_A, T, a3_dv, PS[0])
    bc_ac = broadcast(stv(SL_B, T), BC_B, T, a3_dv, PS[0])
    XCH = [sba(f"XCHd{i}", [128, T], F32, D0 + 41088 + i * 4224) for i in range(4)]
    HXD = SC_H
    HXP = SC_D
    xch_free = [0, 0, 0, 0]
    for fc in range(C):
        b = fc % 4
        eng, hx = (dve, HXD) if fc % 2 == 0 else (pool, HXP)
        sem = "dv" if fc % 2 == 0 else "pl"
        ename = "dve" if fc % 2 == 0 else "pool"
        S.wait("sp", sem, xch_free[b])
        xd_tok = dma("sp", XCH[b], x1_spill[fc], f"xch{b}")
        S.waitt(ename, xd_tok)
        S.wait(ename, "ac", bc_ac)
        eng("tensor_mul", hx[:, 0:T], XCH[b], BC_A)
        eng("tensor_add", H3[:, fc * T:(fc + 1) * T], hx[:, 0:T], BC_B, inc=sem)
        xch_free[b] = S.ctr[sem]
    h3_dv = S.ctr["dv"]
    h3_pl = S.ctr["pl"]
    if "d_h3" in dbg_out:
        S.wait("sp", "dv", h3_dv)
        for fc in range(C):
            dma("sp", dbg_out["d_h3"][fc], H3[:, fc * T:(fc + 1) * T], "dbg")

    S.barrier()
    MID = sba("MID", [128, IC * T], BF16, D0 + 32896)
    WG = [sba(f"WG{i}", [128, 2048], BF16, D0 + 163968 + i * 4096) for i in range(2)]
    PG = [[PS[0], PS[1]], [PS[2], PS[3]]]
    PU = [[PS[4], PS[5]], [PS[6], PS[7]]]
    GS = [SC_H[:, 0:512], SC_H[:, 513:1025]]
    wgu_wd = {}
    piece_pe_free = [0, 0]
    mid_dv = [0, 0]
    gs_free = [0, 0]
    wgu_wd[0] = dma("pool", WG[0], wgu_h[0][:, 0:2048], "wg0")
    wgu_wd[1] = dma("pool", WG[1], wgu_h[0][:, 2048:4096], "wg1")
    for oi in range(IC):
        b = oi % 2
        S.wait("pe", "pl", h3_pl)
        S.wait("pe", "dv", h3_dv)
        S.wait("pe", "dv", mid_dv[b])
        grp_pe = []
        for part in range(2):
            ip = oi * 2 + part
            S.waitt("pe", wgu_wd[ip])
            ps_ = PG[b] if part == 0 else PU[b]
            for tt in range(2):
                for kc in range(C):
                    mm(ps_[tt][:, 0:512], WG[ip % 2][:, kc * 128:(kc + 1) * 128],
                       H3[:, kc * T + tt * 512: kc * T + tt * 512 + 512],
                       start=(kc == 0), stop=(kc == C - 1),
                       inc=("pe" if kc == C - 1 else None))
                grp_pe.append(S.ctr["pe"])
            piece_pe_free[ip % 2] = S.ctr["pe"]
            if ip + 2 < IC * 2:
                nip = ip + 2
                S.wait("pool", "pe", piece_pe_free[ip % 2])
                srcp = wgu_h[nip // 2][:, (nip % 2) * 2048:(nip % 2) * 2048 + 2048]
                wgu_wd[nip] = dma("pool", WG[nip % 2], srcp, f"wg{nip % 2}")
        for tt in range(2):
            S.wait("act", "pe", grp_pe[tt])
            S.wait("act", "dv", gs_free[tt])
            act(GS[tt], PG[b][tt][:, 0:512], AF.Silu, inc="ac")
            S.wait("dve", "ac", S.ctr["ac"])
            S.wait("dve", "pe", grp_pe[2 + tt])
            dve("tensor_mul", MID[:, oi * T + tt * 512: oi * T + tt * 512 + 512],
                GS[tt], PU[b][tt][:, 0:512], inc="dv")
            gs_free[tt] = S.ctr["dv"]
        mid_dv[b] = S.ctr["dv"]
    mid_done_dv = S.ctr["dv"]
    if "d_mid" in dbg_out:
        S.wait("sp", "dv", mid_done_dv)
        for oi in range(IC):
            dma("sp", dbg_out["d_mid"][oi], MID[:, oi * T:(oi + 1) * T], "dbg")

    S.barrier()
    WD2 = [sba("WD20", [128, 2048], BF16, D0 + 163968),
           sba("WD21", [128, 2048], BF16, D0 + 168064),
           sba("WD22", [128, 2048], BF16, D0)]
    PD = [[PS[0], PS[1]], [PS[2], PS[3]]]
    XCH = [sba(f"XCHe{i}", [128, T], F32, D0 + 4224 + i * 4096) for i in range(2)]
    OCH = [sba(f"OCH{i}", [128, T], F32, D0 + 12544 + i * 4096) for i in range(2)]
    wdn_wd = {}
    wd_pe_free = [0, 0, 0]
    out_dv = [0, 0]
    out_od = [None, None]
    for i in range(3):
        wdn_wd[i] = dma("pool", WD2[i], wdn_h[i], f"wn{i}")
    for oc2 in range(C):
        b = oc2 % 2
        S.wait("sp", "dv", out_dv[b])
        xd_tok = dma("sp", XCH[b], x1_spill[oc2], f"xch{b}")
        S.wait("pe", "dv", out_dv[b])
        for kcg in range(4):
            idx = oc2 * 4 + kcg
            S.waitt("pe", wdn_wd[idx])
            S.wait("pe", "dv", mid_done_dv)
            for kc in range(C):
                ic = kcg * 16 + kc
                for tt in range(2):
                    mm(PD[b][tt][:, 0:512], WD2[idx % 3][:, kc * 128:(kc + 1) * 128],
                       MID[:, ic * T + tt * 512: ic * T + tt * 512 + 512],
                       start=(kcg == 0 and kc == 0), stop=(kcg == 3 and kc == C - 1),
                       inc=("pe" if (kc == C - 1 and tt == 1) else None))
            wd_pe_free[idx % 3] = S.ctr["pe"]
            if idx + 3 < IC:
                S.wait("pool", "pe", wd_pe_free[idx % 3])
                wdn_wd[idx + 3] = dma("pool", WD2[idx % 3], wdn_h[idx + 3], f"wn{idx % 3}")
        pe_dn = S.ctr["pe"]
        S.waitt("dve", xd_tok)
        S.wait("dve", "pe", pe_dn)
        S.waitt("dve", out_od[b])
        dve("tensor_add", OCH[b][:, 0:512], XCH[b][:, 0:512], PD[b][0][:, 0:512])
        dve("tensor_add", OCH[b][:, 512:1024], XCH[b][:, 512:1024], PD[b][1][:, 0:512],
            inc="dv")
        out_dv[b] = S.ctr["dv"]
        S.wait("sp", "dv", out_dv[b])
        out_od[b] = dma("sp", out_h[oc2], OCH[b], f"od{b}")
    S.wait("sp", "od0", S.ctr.get("od0", 0))
    S.wait("sp", "od1", S.ctr.get("od1", 0))

    with contextlib.ExitStack() as st:
        for name in list(S.sems):
            S.sems[name] = st.enter_context(nc.semaphore(name))
        with nc.Block() as block:
            S.replay(block)
    return nc


def _prep(inputs):
    f32 = np.float32
    _ = inputs.get("L0_attn_tm_decay")
    x = np.asarray(inputs["L0_attn_x"], f32)[0]
    Wk = np.asarray(inputs["L0_attn_Wk"], f32)
    Wv = np.asarray(inputs["L0_attn_Wv"], f32)
    Wr = np.asarray(inputs["L0_attn_Wr"], f32)
    Wo = np.asarray(inputs["L0_attn_Wo"], f32)
    Wkey = np.asarray(inputs["L0_attn_Wkey"], f32)
    Wval = np.asarray(inputs["L0_attn_Wval"], f32)
    Wcr = np.asarray(inputs["L0_attn_Wcr"], f32)
    Wgu = np.asarray(inputs["L0_ffn_w_gate_up"], f32)
    Wdn = np.asarray(inputs["L0_ffn_w_down"], f32)

    A = np.stack([Wk, Wv, Wr])
    wkvr = np.ascontiguousarray(
        A.reshape(3, 16, 128, 16, 128).transpose(1, 4, 0, 3, 2).reshape(16, 128, 3 * 2048))
    wo = np.ascontiguousarray(
        Wo.reshape(16, 128, 16, 128).transpose(0, 3, 2, 1).reshape(16, 128, 2048))
    wkey = np.ascontiguousarray(
        Wkey.reshape(64, 128, 16, 128).transpose(0, 3, 2, 1).reshape(64, 128, 2048))
    wval = np.ascontiguousarray(
        Wval.reshape(16, 128, 4, 16, 128).transpose(0, 2, 4, 3, 1).reshape(64, 128, 2048))
    G = Wgu[:8192].reshape(64, 128, 16, 128)
    Uu = Wgu[8192:].reshape(64, 128, 16, 128)
    wgu = np.ascontiguousarray(
        np.stack([G, Uu]).transpose(1, 4, 0, 3, 2).reshape(64, 128, 4096))
    wdn = np.ascontiguousarray(
        Wdn.reshape(16, 128, 4, 16, 128).transpose(0, 2, 4, 3, 1).reshape(64, 128, 2048))
    wcr = np.ascontiguousarray(
        Wcr.reshape(16, 128, 16, 128).transpose(0, 3, 2, 1).reshape(16, 128, 2048))

    vec_list = [
        inputs["L0_attn_ln1_g"], inputs["L0_attn_ln1_b"],
        inputs["L0_attn_ln2_g"], inputs["L0_attn_ln2_b"],
        inputs["L0_attn_tm_first"],
        np.asarray(inputs["L0_attn_tm_mk"]).reshape(-1),
        np.asarray(inputs["L0_attn_tm_mv"]).reshape(-1),
        np.asarray(inputs["L0_attn_tm_mr"]).reshape(-1),
        np.asarray(inputs["L0_attn_cm_mk"]).reshape(-1),
        np.asarray(inputs["L0_attn_cm_mr"]).reshape(-1),
    ]
    V = np.stack([np.asarray(v, f32).reshape(2048) for v in vec_list])
    vecs = np.zeros((128, NVEC), f32)
    vecs[:, :160] = V.reshape(10, 16, 128).transpose(2, 0, 1).reshape(128, 160)
    vecs[:, V_EPS] = EPS
    vecs[:, V_NHALF] = -0.5
    vecs[:, V_NONE] = -1.0
    vecs[:, V_HALF] = 0.5
    vecs[:, V_NEG] = NEG
    vecs = np.ascontiguousarray(vecs)

    x_fm = np.ascontiguousarray(x.T)
    in_maps = []
    for ci in range(NCORES):
        s = ci * T
        sl = np.zeros((2048, 1026), f32)
        if s - 2 < 0:
            sl[:, 2:] = x_fm[:, s:s + T]
        else:
            sl[:, :] = x_fm[:, s - 2:s + T]
        ximg = np.ascontiguousarray(
            sl.reshape(16, 128, 1026).transpose(1, 0, 2).reshape(128, 16 * 1026))
        hm = np.full((128, 1), 0.0 if ci == 0 else 1.0, f32)
        in_maps.append(dict(
            x_img=ximg, wkvr=wkvr, wo=wo, wkey=wkey, wval=wval,
            wgu=wgu, wdn=wdn, wcr=wcr, vecs=vecs, hmask=hm))
    return in_maps


def kernel(**inputs):
    in_maps = _prep(inputs)
    nc = _build()
    res = run_bass_kernel_spmd(nc, in_maps, core_ids=list(range(NCORES)))
    parts = []
    for ci in range(NCORES):
        o = np.asarray(res.results[ci]["out"], np.float32)
        parts.append(o.reshape(2048, T))
    full = np.concatenate(parts, axis=1)
    return np.ascontiguousarray(full.T).reshape(1, 8192, 2048).astype(np.float32)


# version 08
# speedup vs baseline: 1.0387x; 1.0050x over previous
import contextlib
import numpy as np
import concourse.bass as bass
import concourse.mybir as mybir
from concourse.bass_utils import run_bass_kernel_spmd

F32 = mybir.dt.float32
BF16 = mybir.dt.bfloat16
AF = mybir.ActivationFunctionType
ALU = mybir.AluOpType

NCORES = 8
T = 1024
C = 16
IC = 64
H = 2048
EPS = 1e-5
NEG = -1e38

V_G1, V_B1, V_G2, V_B2, V_TMF, V_MK, V_MV, V_MR, V_CMK, V_CMR = (
    0, 16, 32, 48, 64, 80, 96, 112, 128, 144)
V_EPS, V_NHALF, V_NONE, V_HALF, V_NEG = 160, 161, 162, 163, 164
NVEC = 168

ROWS3 = [0, 32, 64]


def _tiles(n, cap=512):
    out, s = [], 0
    while s < n:
        l = min(cap, n - s)
        out.append((s, l))
        s += l
    return out


class Sched:
    ENGS = ("pe", "dve", "act", "sp", "pool")

    def __init__(self):
        self.items = {e: [] for e in self.ENGS}
        self.sems = {}
        self.ctr = {}
        self.hiw = {}

    def sem(self, name):
        if name not in self.sems:
            self.sems[name] = None
            self.ctr[name] = 0

    def emit(self, eng, fn, inc=None, incn=1):
        if inc is not None:
            self.ctr[inc] += incn
        self.items[eng].append((fn, inc, incn))
        return self.ctr[inc] if inc is not None else None

    def wait(self, eng, sem, val):
        if val is None or val <= 0:
            return
        key = (eng, sem)
        if self.hiw.get(key, 0) >= val:
            return
        self.hiw[key] = val

        def f(e, sem=sem, val=val):
            e.wait_ge(self.sems[sem], val)
            return None
        self.items[eng].append((f, None, 1))

    def barrier(self, engs=None):
        vals = [(s, self.ctr[s]) for s in self.ctr]
        for eng in (engs or self.ENGS):
            for s, v in vals:
                self.wait(eng, s, v)

    def waitt(self, eng, tok):
        if tok:
            self.wait(eng, tok[0], tok[1])

    def replay(self, block):
        def mk(name):
            def run(e):
                for fn, inc, incn in self.items[name]:
                    inst = fn(e)
                    if inc is not None:
                        inst.then_inc(self.sems[inc], incn)
            return run
        block.gpsimd(mk("pool"))
        block.sync(mk("sp"))
        block.tensor(mk("pe"))
        block.vector(mk("dve"))
        block.scalar(mk("act"))


def _build(dbg=()):
    nc = bass.Bass("TRN2", target_bir_lowering=False)
    dbg = set(dbg)

    x_img = nc.dram_tensor("x_img", [128, C * 1026], F32, kind="ExternalInput").ap()
    wkvr_h = nc.dram_tensor("wkvr", [C, 128, 3 * 2048], F32, kind="ExternalInput").ap()
    wo_h = nc.dram_tensor("wo", [C, 128, 2048], F32, kind="ExternalInput").ap()
    wkey_h = nc.dram_tensor("wkey", [IC, 128, 2048], F32, kind="ExternalInput").ap()
    wval_h = nc.dram_tensor("wval", [IC, 128, 2048], F32, kind="ExternalInput").ap()
    wgu_h = nc.dram_tensor("wgu", [IC, 128, 4096], F32, kind="ExternalInput").ap()
    wdn_h = nc.dram_tensor("wdn", [IC, 128, 2048], F32, kind="ExternalInput").ap()
    wcr_h = nc.dram_tensor("wcr", [C, 128, 2048], F32, kind="ExternalInput").ap()
    vecs_h = nc.dram_tensor("vecs", [128, NVEC], F32, kind="ExternalInput").ap()
    hmask_h = nc.dram_tensor("hmask", [128, 1], F32, kind="ExternalInput").ap()
    out_h = nc.dram_tensor("out", [C, 128, T], F32, kind="ExternalOutput").ap()

    def scratch(name, shape, dt):
        kind = "ExternalOutput" if name in dbg else "Internal"
        return nc.dram_tensor(name, shape, dt, kind=kind).ap()

    xm_spill = scratch("xm_sp", [C, 128, 1025], F32)
    cmr_spill = scratch("cmr_sp", [C, 128, T], BF16)
    x1_spill = scratch("x1_sp", [C, 128, T], F32)
    kv_spill = scratch("kv_sp", [C, 128, T], F32)

    dbg_out = {}
    for nm, shape, dt in [
        ("d_bca", [128, 1026], F32), ("d_bcb", [128, 1026], F32),
        ("d_xk", [C, 128, 1025], BF16), ("d_xv", [C, 128, 1025], BF16),
        ("d_xr", [C, 128, 1025], BF16), ("d_u", [C, 128, 1025], BF16),
        ("d_cmk", [C, 128, T], BF16), ("d_h3", [C, 128, T], BF16),
        ("d_silu", [IC, 128, T], BF16), ("d_mid", [IC, 128, T], BF16),
        ("d_k", [C, 128, T], F32), ("d_v", [C, 128, T], F32),
        ("d_r", [C, 128, T], F32), ("d_stv", [6, 1, 1026], F32),
        ("d_hkvr", [128, 192], F32),
    ]:
        if nm in dbg:
            dbg_out[nm] = nc.dram_tensor(nm, shape, dt, kind="ExternalOutput").ap()

    S = Sched()
    for s in ("pe", "dv", "ac", "pl"):
        S.sem(s)

    BASE = (nc.sbuf_base + 127) // 128 * 128
    TOP = nc.sbuf_base + nc.sbuf_bytes_remaining

    def sba(name, shape, dtype, off):
        a = BASE + off
        nb = int(np.prod(shape[1:])) * (2 if dtype == BF16 else 4)
        assert a + nb <= TOP, (name, a + nb, TOP)
        return nc.alloc_sbuf_tensor_at(name, shape, dtype, offset=a).ap()

    VEC = sba("VEC", [128, NVEC], F32, 0)
    HM = sba("HM", [128, 1], F32, 768)
    ONESB = sba("ONESB", [128, 1], BF16, 896)
    ONESC = sba("ONESC", [1, 128], F32, 1024)
    STV = [sba(f"STV{i}", [1, 1026], F32, 1536 + i * 4224) for i in range(6)]
    SL_A0, SL_B0, SL_A, SL_B, SL_T0, SL_T1 = range(6)
    SC_H = sba("SC_H", [128, 1026], F32, 26880)
    SC_D = sba("SC_D", [128, 1026], F32, 31104)
    D0 = 35328

    def stv(i, n=1026):
        return STV[i][0:1, 0:n]

    def vcol(base, fc):
        return VEC[:, base + fc: base + fc + 1]

    PS = [nc.alloc_psum_tensor(f"PSB{i}", [128, 512], F32).ap() for i in range(8)]

    def dma(eng, dst, src, sem):
        S.sem(sem)

        def f(e, dst=dst, src=src):
            return e.dma_start(dst, src)
        return (sem, S.emit(eng, f, inc=sem, incn=16))

    def mm(psum, lhsT, rhs, start, stop, inc=None):
        def f(e, psum=psum, lhsT=lhsT, rhs=rhs, start=start, stop=stop):
            return e.matmul(psum, lhsT, rhs, start=start, stop=stop)
        return S.emit("pe", f, inc=inc)

    def act(out, in_, func, bias=0.0, scale=1.0, inc=None):
        def f(e, out=out, in_=in_, func=func, bias=bias, scale=scale):
            return e.activation(out, in_, func, bias=bias, scale=scale)
        return S.emit("act", f, inc=inc)

    def dve(op, *args, inc=None):
        def f(e, op=op, args=args):
            return getattr(e, op)(*args)
        return S.emit("dve", f, inc=inc)

    def pool(op, *args, inc=None):
        def f(e, op=op, args=args):
            return getattr(e, op)(*args)
        return S.emit("pool", f, inc=inc)

    def stats_finish(pa, pb, ncols, slot_a, slot_b, pe_done):
        tl = _tiles(ncols)
        S.wait("act", "pe", pe_done)
        for ti, (s, l) in enumerate(tl):
            act(stv(SL_T0)[0:1, s:s + l], pa[ROWS3[ti]:ROWS3[ti] + 1, 0:l], AF.Identity)
        for ti, (s, l) in enumerate(tl):
            act(stv(SL_T1)[0:1, s:s + l], pb[ROWS3[ti]:ROWS3[ti] + 1, 0:l], AF.Identity,
                inc=("ac" if ti == len(tl) - 1 else None))
        S.wait("dve", "ac", S.ctr["ac"])
        m = stv(SL_T0, ncols)
        q = stv(SL_T1, ncols)
        a = stv(slot_a, ncols)
        b = stv(slot_b, ncols)
        dve("tensor_scalar", m, m, 1.0 / H, None, ALU.mult)
        dve("tensor_scalar", q, q, 1.0 / H, None, ALU.mult)
        dve("tensor_mul", b, m, m)
        dve("tensor_sub", q, q, b, inc="dv")
        S.wait("act", "dv", S.ctr["dv"])
        act(a, q, AF.Ln, bias=VEC[0:1, V_EPS:V_EPS + 1])
        act(a, a, AF.Exp, scale=VEC[0:1, V_NHALF:V_NHALF + 1], inc="ac")
        S.wait("dve", "ac", S.ctr["ac"])
        dve("scalar_tensor_tensor", b, m, -1.0, a, ALU.mult, ALU.mult, inc="dv")
        return S.ctr["dv"]

    def broadcast(vec_ap, dst, ncols, dv_ready, psb):
        S.wait("pe", "dv", dv_ready)
        last = None
        for (s, l) in _tiles(ncols):
            S.wait("pe", "ac", S.ctr["ac"])
            mm(psb[:, 0:l], ONESC, vec_ap[0:1, s:s + l], start=True, stop=True, inc="pe")
            S.wait("act", "pe", S.ctr["pe"])
            last = act(dst[:, s:s + l], psb[:, 0:l], AF.Identity, inc="ac")
        return last

    M2 = sba("M2", [128, 512], F32, 209920)
    dma("sp", VEC, vecs_h, "ld")
    hm_tok = dma("sp", HM, hmask_h, "ld")
    for e_ in ("act", "dve", "pool"):
        S.wait(e_, "ld", 32)
    dve("memset", ONESB, 1.0)
    dve("memset", ONESC, 1.0, inc="dv")
    dve("memset", M2, 1.0, inc="dv")
    S.wait("dve", "dv", S.ctr["dv"])
    dve("tensor_copy", M2[:, 0:1], HM, inc="dv")
    S.wait("dve", "dv", S.ctr["dv"])
    dve("tensor_copy", M2[:, 1:2], HM, inc="dv")

    XBF = sba("XBF", [128, C * 1026], BF16, D0)
    xbf_tok = dma("pool", XBF, x_img, "xbf")

    SQ = [sba(f"SQ{i}", [128, 1026], BF16, D0 + 33024 + i * 2176) for i in range(2)]
    pa, pb = PS[0], PS[1]
    tl6 = _tiles(1026)
    sq_pe_free = [0, 0]
    for fc in range(C):
        b = fc % 2
        xv = XBF[:, fc * 1026:(fc + 1) * 1026]
        S.waitt("act", xbf_tok)
        S.wait("act", "pe", sq_pe_free[b])
        act(SQ[b], xv, AF.Square, inc="ac")
        sq_ac = S.ctr["ac"]
        S.waitt("pe", xbf_tok)
        S.wait("pe", "ac", sq_ac)
        for ti, (s, l) in enumerate(tl6):
            mm(pa[ROWS3[ti]:ROWS3[ti] + 1, 0:l], ONESB, xv[:, s:s + l],
               start=(fc == 0), stop=(fc == C - 1))
        for ti, (s, l) in enumerate(tl6):
            mm(pb[ROWS3[ti]:ROWS3[ti] + 1, 0:l], ONESB, SQ[b][:, s:s + l],
               start=(fc == 0), stop=(fc == C - 1),
               inc=("pe" if ti == len(tl6) - 1 else None))
        sq_pe_free[b] = S.ctr["pe"]
    a0_dv = stats_finish(pa, pb, 1026, SL_A0, SL_B0, S.ctr["pe"])

    BC_A = sba("BC_A1", [128, 1026], F32, D0 + 165056)
    BC_B = sba("BC_B1", [128, 1026], F32, D0 + 169280)
    broadcast(stv(SL_A0), BC_A, 1026, a0_dv, PS[2])
    bc_ac = broadcast(stv(SL_B0), BC_B, 1026, a0_dv, PS[2])

    XCH = [sba(f"XCHa{i}", [128, 1026], F32, D0 + 131584 + i * 4224) for i in range(2)]
    XABF = [sba(f"XABF{i}", [128, 1026], BF16, D0 + 37376 + i * 2176) for i in range(2)]
    xch_free = [0, 0]
    st_pe_free = [0, 0]
    sch_ac_free = 0
    for fc in range(C):
        b = fc % 2
        S.wait("sp", "dv", xch_free[b])
        xd_tok = dma("sp", XCH[b], x_img[:, fc * 1026:(fc + 1) * 1026], f"xch{b}")
        S.waitt("dve", xd_tok)
        S.wait("dve", "ac", bc_ac)
        S.wait("dve", "pe", st_pe_free[b])
        S.wait("dve", "ac", sch_ac_free)
        dve("tensor_mul", SC_H, XCH[b], BC_A)
        dve("tensor_add", SC_H, SC_H, BC_B)
        dve("tensor_copy", XABF[b], SC_H, inc="dv")
        xch_free[b] = S.ctr["dv"]
        S.wait("act", "dv", S.ctr["dv"])
        S.wait("act", "pe", st_pe_free[b])
        act(SQ[b], SC_H, AF.Square, inc="ac")
        sch_ac_free = S.ctr["ac"]
        S.wait("pe", "dv", S.ctr["dv"])
        S.wait("pe", "ac", S.ctr["ac"])
        for ti, (s, l) in enumerate(tl6):
            mm(pa[ROWS3[ti]:ROWS3[ti] + 1, 0:l], ONESB, XABF[b][:, s:s + l],
               start=(fc == 0), stop=(fc == C - 1))
        for ti, (s, l) in enumerate(tl6):
            mm(pb[ROWS3[ti]:ROWS3[ti] + 1, 0:l], ONESB, SQ[b][:, s:s + l],
               start=(fc == 0), stop=(fc == C - 1),
               inc=("pe" if ti == len(tl6) - 1 else None))
        st_pe_free[b] = S.ctr["pe"]
    a1_dv = stats_finish(pa, pb, 1026, SL_A, SL_B, S.ctr["pe"])
    S.wait("dve", "dv", a1_dv)
    dve("tensor_mul", stv(SL_T0), stv(SL_B0), stv(SL_A))
    dve("tensor_add", stv(SL_T0), stv(SL_T0), stv(SL_B))
    dve("tensor_mul", stv(SL_T1), stv(SL_A0), stv(SL_A), inc="dv")
    a01_dv = S.ctr["dv"]
    broadcast(stv(SL_T1), BC_A, 1026, a01_dv, PS[2])
    bc_ac = broadcast(stv(SL_T0), BC_B, 1026, a01_dv, PS[2])
    if "d_stv" in dbg_out:
        S.wait("sp", "dv", a01_dv)
        for i in range(6):
            dma("sp", dbg_out["d_stv"][i], STV[i][0:1, 0:1026], "dbg")

    XK = sba("XK", [128, C * 1025], BF16, D0)
    XV = sba("XV", [128, C * 1025], BF16, D0 + 32896)
    XR = sba("XR", [128, C * 1025], BF16, D0 + 65792)
    WKV3 = [sba(f"WKV3{i}", [128, 2048], BF16, D0 + 148480 + i * 4096)
            for i in range(3)]
    wkvr_wd = {}

    def wkvr_slice(si):
        oc, mat = (si % 48) // 3, (si % 48) % 3
        return dma("pool", WKV3[si % 3], wkvr_h[oc][:, mat * 2048:(mat + 1) * 2048],
                   f"wk{si % 3}")

    for si in range(3):
        wkvr_wd[si] = wkvr_slice(si)

    XCH3 = [sba(f"XCH3{i}", [128, 1026], F32, D0 + 139776 + i * 4224) for i in range(2)]
    SC_H2 = sba("SC_H2", [128, 1026], F32, D0 + 160768)
    SC_HX = [SC_H, SC_H2]
    hx_free = [0, 0]
    S.wait("sp", "dv", max(xch_free))
    xch_free = [0, 0]

    def mix_chunk(fc, lo, hi, first):
            b = fc % 2
            S.wait("sp", "pl", xch_free[b])
            xd_tok = dma("sp", XCH3[b], x_img[:, fc * 1026:(fc + 1) * 1026], f"xch{b}")
            hx = SC_HX[b]
            S.waitt("pool", xd_tok)
            S.wait("pool", "ac", bc_ac)
            S.wait("pool", "dv", hx_free[b])
            pool("tensor_mul", hx, XCH3[b], BC_A)
            pool("tensor_add", hx, hx, BC_B, inc="pl")
            xch_free[b] = S.ctr["pl"]
            S.wait("act", "pl", S.ctr["pl"])
            act(hx, hx, AF.Identity, scale=vcol(V_G1, fc), bias=vcol(V_B1, fc), inc="ac")
            S.wait("dve", "ac", S.ctr["ac"])
            if first:
                dve("tensor_mul", hx[:, 0:512], hx[:, 0:512], M2)
            dve("tensor_sub", SC_D[:, lo:hi], hx[:, lo + 1:hi + 1], hx[:, lo:hi])
            o = fc * 1025
            dve("scalar_tensor_tensor", XK[:, o + lo:o + hi], SC_D[:, lo:hi],
                vcol(V_MK, fc), hx[:, lo:hi], ALU.mult, ALU.add)
            dve("scalar_tensor_tensor", XV[:, o + lo:o + hi], SC_D[:, lo:hi],
                vcol(V_MV, fc), hx[:, lo:hi], ALU.mult, ALU.add)
            dve("scalar_tensor_tensor", XR[:, o + lo:o + hi], SC_D[:, lo:hi],
                vcol(V_MR, fc), hx[:, lo:hi], ALU.mult, ALU.add, inc="dv")
            hx_free[b] = S.ctr["dv"]

    for fc_ in range(C):
        mix_chunk(fc_, 0, 513, True)
    mix1_dv = S.ctr["dv"]
    if "d_bca" in dbg_out:
        S.wait("sp", "ac", bc_ac)
        dma("sp", dbg_out["d_bca"], BC_A, "dbg")
        dma("sp", dbg_out["d_bcb"], BC_B, "dbg")

    U = sba("U", [128, C * 1025], BF16, D0 + 98688)
    CH = [sba(f"CH{i}", [128, 512], F32, D0 + 131584 + i * 2048) for i in range(4)]
    PSK = [PS[0], PS[1]]
    PSV = [PS[2], PS[3]]
    PSR = [PS[4], PS[5]]
    HK, HV, HR = PS[1][:, 0:1], PS[3][:, 0:1], PS[5][:, 0:1]
    HW = PS[6][:, 0:1]
    HSTX = sba("HSTX", [128, 192], F32, 209152)
    HST = HSTX[:, 0:48]
    SPC = SC_H2[:, 0:512]

    def kvr_chain(oc, tt, pe_k, pe_v, pe_r):
        c0, c1, c2, c3 = CH
        tok0 = 1 + tt * 512
        uo = U[:, oc * 1025 + tok0: oc * 1025 + tok0 + 512]
        S.wait("act", "pe", pe_k)
        act(c0, PSK[tt][:, 0:512], AF.Identity, bias=vcol(V_TMF, oc), inc="ac")
        S.wait("dve", "ac", S.ctr["ac"])
        dve("tensor_scalar", c1, c0, VEC[:, V_NEG:V_NEG + 1], None, ALU.max)
        dve("tensor_sub", c0, c0, c1, inc="dv")
        S.wait("act", "dv", S.ctr["dv"])
        act(c1, c0, AF.Exp)
        act(c2, c0, AF.Exp, scale=VEC[:, V_NONE:V_NONE + 1])
        S.wait("act", "pe", pe_r)
        act(c3, PSR[tt][:, 0:512], AF.Tanh, scale=VEC[:, V_HALF:V_HALF + 1], inc="ac")
        S.wait("dve", "ac", S.ctr["ac"])
        S.wait("dve", "pe", pe_v)
        dve("scalar_tensor_tensor", c0, PSV[tt][:, 0:512], 0.5, c1, ALU.mult, ALU.mult)
        dve("tensor_mul", c0, c0, c2)
        dve("scalar_tensor_tensor", uo, c3, 1.0, c0, ALU.add, ALU.mult, inc="dv")
        return S.ctr["dv"]

    chain_dv0 = 0
    halo_stage_ac = 0
    slice_pe = {}
    for oc in range(C):
        mix_chunk(oc, 513, 1025, False)
        S.wait("pe", "dv", mix1_dv)
        S.wait("pe", "dv", chain_dv0)
        S.wait("pe", "ac", halo_stage_ac)
        pes = []
        for mi, dst, dsth, mixsrc in ((0, PSK, HK, XK), (1, PSV, HV, XV),
                                      (2, PSR, HR, XR)):
            si = oc * 3 + mi
            wb = WKV3[si % 3]
            S.waitt("pe", wkvr_wd[si])
            for kc in range(C):
                mm(dst[0][:, 0:512],
                   wb[:, kc * 128:(kc + 1) * 128],
                   mixsrc[:, kc * 1025 + 1: kc * 1025 + 1 + 512],
                   start=(kc == 0), stop=(kc == C - 1),
                   inc=("pe" if kc == C - 1 else None))
            pes.append(S.ctr["pe"])
            for kc in range(C):
                mm(dsth, wb[:, kc * 128:(kc + 1) * 128],
                   mixsrc[:, kc * 1025: kc * 1025 + 1],
                   start=(kc == 0), stop=(kc == C - 1),
                   inc=("pe" if kc == C - 1 else None))
            slice_pe[si % 3] = S.ctr["pe"]
            nsi = si + 3
            if nsi < 48:
                S.wait("pool", "pe", slice_pe[nsi % 3])
                wkvr_wd[nsi] = wkvr_slice(nsi)
        pe_halo = S.ctr["pe"]
        chain_dv0 = kvr_chain(oc, 0, pes[0], pes[1], pes[2])
        S.wait("act", "pe", pe_halo)
        act(HST[:, oc:oc + 1], HK, AF.Identity)
        act(HST[:, 16 + oc:16 + oc + 1], HV, AF.Identity)
        act(HST[:, 32 + oc:32 + oc + 1], HR, AF.Identity, inc="ac")
        halo_stage_ac = S.ctr["ac"]
    mix2_dv = S.ctr["dv"]
    mix2_pl = S.ctr["pl"]
    if "d_xk" in dbg_out:
        S.wait("sp", "dv", mix2_dv)
        for fc in range(C):
            dma("sp", dbg_out["d_xk"][fc], XK[:, fc * 1025:(fc + 1) * 1025], "dbg")
            dma("sp", dbg_out["d_xv"][fc], XV[:, fc * 1025:(fc + 1) * 1025], "dbg")
            dma("sp", dbg_out["d_xr"][fc], XR[:, fc * 1025:(fc + 1) * 1025], "dbg")


    W16 = HSTX[:, 48:64]
    P16 = HSTX[:, 64:80]
    E16 = HSTX[:, 80:96]
    R16 = HSTX[:, 96:112]
    T16 = HSTX[:, 112:128]
    N16 = HSTX[:, 128:144]
    UH16 = HSTX[:, 144:160]
    S.wait("act", "ac", halo_stage_ac)
    act(SPC, SPC, AF.Identity, inc="ac")
    S.wait("dve", "ac", S.ctr["ac"])
    dve("tensor_add", W16, HST[:, 0:16], VEC[:, V_TMF:V_TMF + 16])
    dve("tensor_copy", SPC, SPC)
    dve("tensor_scalar", P16, W16, VEC[:, V_NEG:V_NEG + 1], None, ALU.max)
    dve("tensor_copy", SPC, SPC)
    dve("tensor_sub", W16, W16, P16)
    dve("tensor_copy", SPC, SPC, inc="dv")
    S.wait("act", "dv", S.ctr["dv"])
    act(E16, W16, AF.Exp)
    act(R16, W16, AF.Exp, scale=VEC[:, V_NONE:V_NONE + 1])
    act(T16, HST[:, 32:48], AF.Tanh, scale=VEC[:, V_HALF:V_HALF + 1])
    act(SPC, SPC, AF.Identity, inc="ac")
    S.wait("dve", "ac", S.ctr["ac"])
    dve("scalar_tensor_tensor", N16, HST[:, 16:32], 0.5, E16, ALU.mult, ALU.mult)
    dve("tensor_copy", SPC, SPC)
    dve("tensor_mul", N16, N16, R16)
    dve("tensor_copy", SPC, SPC)
    dve("scalar_tensor_tensor", UH16, T16, 1.0, N16, ALU.add, ALU.mult)
    dve("tensor_copy", SPC, SPC)
    for oc in range(C):
        dve("tensor_copy", U[:, oc * 1025: oc * 1025 + 1], UH16[:, oc:oc + 1])
    dve("tensor_copy", SPC, SPC, inc="dv")
    halo_done_dv = S.ctr["dv"]
    if "d_hkvr" in dbg_out:
        S.wait("sp", "dv", S.ctr["dv"])
        dma("sp", dbg_out["d_hkvr"], HSTX, "dbg")

    for si in range(48, 51):
        S.wait("pool", "pe", slice_pe[si % 3])
        wkvr_wd[si] = wkvr_slice(si)
    chain_dv1 = 0
    for oc in range(C):
        S.wait("pe", "dv", mix2_dv)
        S.wait("pe", "pl", mix2_pl)
        S.wait("pe", "dv", halo_done_dv)
        S.wait("pe", "dv", chain_dv1)
        pes = []
        for mi, dst, mixsrc in ((0, PSK, XK), (1, PSV, XV), (2, PSR, XR)):
            si = 48 + oc * 3 + mi
            wb = WKV3[si % 3]
            S.waitt("pe", wkvr_wd[si])
            for kc in range(C):
                mm(dst[1][:, 0:512],
                   wb[:, kc * 128:(kc + 1) * 128],
                   mixsrc[:, kc * 1025 + 513: kc * 1025 + 513 + 512],
                   start=(kc == 0), stop=(kc == C - 1),
                   inc=("pe" if kc == C - 1 else None))
            pes.append(S.ctr["pe"])
            slice_pe[si % 3] = S.ctr["pe"]
            nsi = si + 3
            if nsi < 96:
                S.wait("pool", "pe", slice_pe[nsi % 3])
                wkvr_wd[nsi] = wkvr_slice(nsi)
        chain_dv1 = kvr_chain(oc, 1, pes[0], pes[1], pes[2])
    u_done_dv = S.ctr["dv"]
    if "d_u" in dbg_out:
        S.wait("sp", "dv", u_done_dv)
        for fc in range(C):
            dma("sp", dbg_out["d_u"][fc], U[:, fc * 1025:(fc + 1) * 1025], "dbg")

    S.barrier()
    BC_A = sba("BC_A2", [128, 1026], F32, D0 + 165376)
    BC_B = sba("BC_B2", [128, 1026], F32, D0 + 169600)
    broadcast(stv(SL_A0), BC_A, 1026, a0_dv, PS[7])
    bc_ac = broadcast(stv(SL_B0), BC_B, 1026, a0_dv, PS[7])

    XCH = [sba(f"XCHb{i}", [128, 1026], F32, D0 + 139776 + i * 4224) for i in range(2)]
    WO = [sba(f"WO{i}", [128, 2048], BF16, D0 + 131584 + i * 4096) for i in range(2)]
    PSW = [[PS[0], PS[1]], [PS[2], PS[3]]]
    XMCH = [sba(f"XMCH{i}", [128, 1025], F32, D0 + 148224 + i * 4224) for i in range(2)]
    XMBF = [sba(f"XMBF{i}", [128, 1025], BF16, D0 + 156672 + i * 2176) for i in range(2)]
    SQ5 = [sba(f"SQ5{i}", [128, 1025], BF16, D0 + 161024 + i * 2176) for i in range(2)]
    pa, pb = PS[4], PS[5]
    tl5 = _tiles(1025)

    wo_wd = {}
    wo_wd[0] = dma("pool", WO[0], wo_h[0], "wwo0")
    wo_wd[1] = dma("pool", WO[1], wo_h[1], "wwo1")
    wo_pe_free = [0, 0]
    xm_halo_dv_last = 0
    xm_dv = [0, 0]
    xm_spilled = [None, None]
    xm_stats_pe = [0, 0]
    xch_free = [0, 0]
    sq5_ac = [0, 0]
    for oc in range(C):
        b = oc % 2
        S.waitt("pe", wo_wd[oc])
        S.wait("pe", "dv", u_done_dv)
        S.wait("pe", "dv", xm_dv[b])
        for tt in range(2):
            tok0 = 1 + tt * 512
            for kc in range(C):
                mm(PSW[b][tt][:, 0:512], WO[b][:, kc * 128:(kc + 1) * 128],
                   U[:, kc * 1025 + tok0: kc * 1025 + tok0 + 512],
                   start=(kc == 0), stop=(kc == C - 1),
                   inc=("pe" if kc == C - 1 else None))
        S.wait("pe", "dv", xm_halo_dv_last)
        for kc in range(C):
            mm(HW, WO[b][:, kc * 128:(kc + 1) * 128],
               U[:, kc * 1025: kc * 1025 + 1],
               start=(kc == 0), stop=(kc == C - 1),
               inc=("pe" if kc == C - 1 else None))
        pe_wo = S.ctr["pe"]
        wo_pe_free[b] = pe_wo
        if oc + 2 < C:
            S.wait("pool", "pe", pe_wo)
            wo_wd[oc + 2] = dma("pool", WO[b], wo_h[oc + 2], f"wwo{b}")
        S.wait("sp", "dv", xch_free[b])
        xd_tok = dma("sp", XCH[b], x_img[:, oc * 1026:(oc + 1) * 1026], f"xch{b}")
        S.waitt("dve", xd_tok)
        S.wait("dve", "ac", bc_ac)
        S.waitt("dve", xm_spilled[b])
        S.wait("dve", "ac", sq5_ac[b])
        dve("tensor_mul", XMCH[b], XCH[b][:, 1:1026], BC_A[:, 1:1026])
        dve("tensor_add", XMCH[b], XMCH[b], BC_B[:, 1:1026], inc="dv")
        xch_free[b] = S.ctr["dv"]
        S.wait("dve", "pe", pe_wo)
        dve("tensor_add", XMCH[b][:, 0:1], XMCH[b][:, 0:1], HW, inc="dv")
        xm_halo_dv_last = S.ctr["dv"]
        dve("tensor_add", XMCH[b][:, 1:513], XMCH[b][:, 1:513], PSW[b][0][:, 0:512])
        dve("tensor_add", XMCH[b][:, 513:1025], XMCH[b][:, 513:1025], PSW[b][1][:, 0:512])
        S.wait("dve", "pe", xm_stats_pe[b])
        dve("tensor_copy", XMBF[b], XMCH[b], inc="dv")
        xm_dv[b] = S.ctr["dv"]
        S.wait("act", "dv", xm_dv[b])
        S.wait("act", "pe", xm_stats_pe[b])
        act(SQ5[b], XMCH[b], AF.Square, inc="ac")
        ac_sq = S.ctr["ac"]
        sq5_ac[b] = ac_sq
        S.wait("pool", "dv", xm_dv[b])
        xm_spilled[b] = dma("pool", xm_spill[oc], XMCH[b], f"sxm{b}")
        S.wait("pe", "dv", xm_dv[b])
        S.wait("pe", "ac", ac_sq)
        for ti, (s, l) in enumerate(tl5):
            mm(pa[ROWS3[ti]:ROWS3[ti] + 1, 0:l], ONESB, XMBF[b][:, s:s + l],
               start=(oc == 0), stop=(oc == C - 1))
        for ti, (s, l) in enumerate(tl5):
            mm(pb[ROWS3[ti]:ROWS3[ti] + 1, 0:l], ONESB, SQ5[b][:, s:s + l],
               start=(oc == 0), stop=(oc == C - 1),
               inc=("pe" if ti == len(tl5) - 1 else None))
        xm_stats_pe[b] = S.ctr["pe"]
    a2_dv = stats_finish(pa, pb, 1025, SL_A, SL_B, S.ctr["pe"])
    broadcast(stv(SL_A, 1025), BC_A[:, 0:1025], 1025, a2_dv, PS[7])
    bc_ac = broadcast(stv(SL_B, 1025), BC_B[:, 0:1025], 1025, a2_dv, PS[7])

    S.barrier()
    CMK = sba("CMK", [128, C * T], BF16, D0)
    CMR = [sba(f"CMR{i}", [128, T], BF16, D0 + 32896 + i * 2048) for i in range(2)]
    SC_H2b = sba("SC_H2b", [128, 1026], F32, D0 + 73856)
    SC_H2c = sba("SC_H2c", [128, 1026], F32, D0 + 78080)
    XCHb3 = sba("XCHb3", [128, 1026], F32, D0 + 82304)
    CMR.append(sba("CMR2", [128, T], BF16, D0 + 36992))
    SC_HX = [SC_H, SC_H2b, SC_H2c]
    XCH6 = [XCH[0], XCH[1], XCHb3]
    hx_free = [0, 0, 0]
    S.wait("sp", "dv", max(xch_free))
    xch_free = [0, 0, 0]
    cmr_spilled = [None, None, None]
    for fc in range(C):
        b = fc % 3
        S.wait("sp", "pl", xch_free[b])
        xd_tok = dma("sp", XCH6[b][:, 0:1025], xm_spill[fc], f"xch{b}")
        hx = SC_HX[b]
        S.waitt("pool", xd_tok)
        S.wait("pool", "ac", bc_ac)
        S.wait("pool", "dv", hx_free[b])
        pool("tensor_mul", hx[:, 0:1025], XCH6[b][:, 0:1025], BC_A[:, 0:1025])
        pool("tensor_add", hx[:, 0:1025], hx[:, 0:1025], BC_B[:, 0:1025], inc="pl")
        xch_free[b] = S.ctr["pl"]
        S.wait("act", "pl", S.ctr["pl"])
        act(hx[:, 0:1025], hx[:, 0:1025], AF.Identity,
            scale=vcol(V_G2, fc), bias=vcol(V_B2, fc), inc="ac")
        S.wait("dve", "ac", S.ctr["ac"])
        dve("tensor_mul", hx[:, 0:511], hx[:, 0:511], M2[:, 1:512])
        dve("tensor_sub", SC_D[:, 0:T], hx[:, 1:1025], hx[:, 0:T])
        dve("scalar_tensor_tensor", CMK[:, fc * T:(fc + 1) * T], SC_D[:, 0:T],
            vcol(V_CMK, fc), hx[:, 0:T], ALU.mult, ALU.add)
        S.waitt("dve", cmr_spilled[b])
        dve("scalar_tensor_tensor", CMR[b], SC_D[:, 0:T],
            vcol(V_CMR, fc), hx[:, 0:T], ALU.mult, ALU.add, inc="dv")
        hx_free[b] = S.ctr["dv"]
        S.wait("pool", "dv", S.ctr["dv"])
        cmr_spilled[b] = dma("pool", cmr_spill[fc], CMR[b], f"scm{b}")
    cmk_done_dv = S.ctr["dv"]
    if "d_cmk" in dbg_out:
        S.wait("sp", "dv", cmk_done_dv)
        for fc in range(C):
            dma("sp", dbg_out["d_cmk"][fc], CMK[:, fc * T:(fc + 1) * T], "dbg")

    S.barrier()
    SILU = sba("SILU", [128, IC * T], BF16, D0 + 32896)
    WK3 = [sba(f"WK3{i}", [128, 2048], BF16, D0 + 163968 + i * 4096) for i in range(2)]
    PSA = [PS[0], PS[1]]
    PSB = [PS[2], PS[3]]
    wkey_wd = {}
    wk_pe_free = [0, 0]
    silu_ac = [0, 0]
    for i in range(2):
        wkey_wd[i] = dma("pool", WK3[i], wkey_h[i], f"wy{i}")
    for oi in range(IC):
        S.waitt("pe", wkey_wd[oi])
        S.wait("pe", "dv", cmk_done_dv)
        ps = (PSA, PSB)[oi % 2]
        S.wait("pe", "ac", silu_ac[oi % 2])
        for tt in range(2):
            for kc in range(C):
                mm(ps[tt][:, 0:512], WK3[oi % 2][:, kc * 128:(kc + 1) * 128],
                   CMK[:, kc * T + tt * 512: kc * T + tt * 512 + 512],
                   start=(kc == 0), stop=(kc == C - 1),
                   inc=("pe" if kc == C - 1 else None))
        pe_kk = S.ctr["pe"]
        wk_pe_free[oi % 2] = pe_kk
        if oi + 2 < IC:
            S.wait("pool", "pe", pe_kk)
            wkey_wd[oi + 2] = dma("pool", WK3[oi % 2], wkey_h[oi + 2], f"wy{oi % 2}")
        S.wait("act", "pe", pe_kk)
        act(SILU[:, oi * T: oi * T + 512], ps[0][:, 0:512], AF.Silu)
        silu_ac[oi % 2] = act(SILU[:, oi * T + 512: oi * T + 1024], ps[1][:, 0:512],
                              AF.Silu, inc="ac")
    silu_done_ac = S.ctr["ac"]
    if "d_silu" in dbg_out:
        S.wait("sp", "ac", silu_done_ac)
        for oi in range(IC):
            dma("sp", dbg_out["d_silu"][oi], SILU[:, oi * T:(oi + 1) * T], "dbg")

    S.barrier()
    WV2 = [sba(f"WV2{i}", [128, 2048], BF16, D0 + 163968 + i * 4096) for i in range(2)]
    PKV = [[PS[0], PS[1]], [PS[2], PS[3]]]
    KVCH = [SC_H[:, 0:T], SC_D[:, 0:T]]
    CMRR = sba("CMRR", [128, C * T], BF16, D0)
    cmr_re_xd = 0
    for fc in range(C):
        cmr_re_xd = dma("sp", CMRR[:, fc * T:(fc + 1) * T], cmr_spill[fc], "cmrr")
    wval_wd = {}
    wv_pe_free = [0, 0]
    kv_evac_ac = [0, 0]
    kv_spilled = [None, None]
    for i in range(2):
        wval_wd[i] = dma("pool", WV2[i], wval_h[i], f"wv{i}")
    for oc2 in range(C):
        b = oc2 % 2
        S.wait("pe", "ac", kv_evac_ac[b])
        for kcg in range(4):
            idx = oc2 * 4 + kcg
            S.waitt("pe", wval_wd[idx])
            S.wait("pe", "ac", silu_done_ac)
            for kc in range(C):
                ic = kcg * 16 + kc
                for tt in range(2):
                    mm(PKV[b][tt][:, 0:512], WV2[idx % 2][:, kc * 128:(kc + 1) * 128],
                       SILU[:, ic * T + tt * 512: ic * T + tt * 512 + 512],
                       start=(kcg == 0 and kc == 0), stop=(kcg == 3 and kc == C - 1),
                       inc=("pe" if (kc == C - 1 and tt == 1) else None))
            wv_pe_free[idx % 2] = S.ctr["pe"]
            if idx + 2 < IC:
                S.wait("pool", "pe", wv_pe_free[idx % 2])
                wval_wd[idx + 2] = dma("pool", WV2[idx % 2], wval_h[idx + 2], f"wv{idx % 2}")
        pe_kv = S.ctr["pe"]
        S.wait("act", "pe", pe_kv)
        S.waitt("act", kv_spilled[b])
        act(KVCH[b][:, 0:512], PKV[b][0][:, 0:512], AF.Identity)
        kv_evac_ac[b] = act(KVCH[b][:, 512:1024], PKV[b][1][:, 0:512], AF.Identity,
                            inc="ac")
        S.wait("sp", "ac", kv_evac_ac[b])
        kv_spilled[b] = dma("sp", kv_spill[oc2], KVCH[b], f"skv{b}")

    S.barrier()
    WC2 = [sba(f"WC2{i}", [128, 2048], BF16, D0 + 32896 + i * 4096) for i in range(2)]
    PCR = [[PS[0], PS[1]], [PS[2], PS[3]]]
    pa, pb = PS[4], PS[5]
    XCH = [sba(f"XCHc{i}", [128, T], F32, D0 + 41088 + i * 4096) for i in range(2)]
    XM2 = [sba(f"XM2{i}", [128, T], F32, D0 + 49280 + i * 4096) for i in range(2)]
    KV2 = [sba(f"KV2{i}", [128, T], F32, D0 + 57472 + i * 4096) for i in range(2)]
    X1CH = [sba(f"X1CH{i}", [128, T], F32, D0 + 65664 + i * 4096) for i in range(2)]
    X1BF = [sba(f"X1BF{i}", [128, T], BF16, D0 + 73856 + i * 2048) for i in range(2)]
    SQ8 = [sba(f"SQ8{i}", [128, T], BF16, D0 + 77952 + i * 2048) for i in range(2)]
    CH0 = SC_H[:, 0:512]
    CH1 = SC_H[:, 513:1025]
    wcr_wd = {}
    wcr_wd[0] = dma("pool", WC2[0], wcr_h[0], "wc0")
    wcr_wd[1] = dma("pool", WC2[1], wcr_h[1], "wc1")
    wc_pe_free = [0, 0]
    sq8_ac = [0, 0]
    last_asm_dv = 0
    x1_dv = [0, 0]
    x1_spilled = [None, None]
    x1_stats_pe = [0, 0]
    tl4 = _tiles(T)
    stats_ready = {}

    def emit_x1_stats(j):
        bj = j % 2
        dvv, acv = stats_ready[j]
        S.wait("pe", "dv", dvv)
        S.wait("pe", "ac", acv)
        for ti, (s, l) in enumerate(tl4):
            mm(pa[ROWS3[ti]:ROWS3[ti] + 1, 0:l], ONESB, X1BF[bj][:, s:s + l],
               start=(j == 0), stop=(j == C - 1),
               inc=("pe" if ti == len(tl4) - 1 else None))
        for ti, (s, l) in enumerate(tl4):
            mm(pb[ROWS3[ti]:ROWS3[ti] + 1, 0:l], ONESB, SQ8[bj][:, s:s + l],
               start=(j == 0), stop=(j == C - 1),
               inc=("pe" if ti == len(tl4) - 1 else None))
        x1_stats_pe[bj] = S.ctr["pe"]
    def p8b_loads(j):
        bj = j % 2
        S.wait("sp", "dv", x1_dv[bj])
        t1 = dma("sp", XCH[bj], x_img[:, j * 1026 + 2:(j + 1) * 1026], f"xch{bj}")
        t2 = dma("sp", XM2[bj], xm_spill[j][:, 1:1025], f"xm2{bj}")
        t3 = dma("sp", KV2[bj], kv_spill[j], f"kv2{bj}")
        return (t1, t2, t3)

    xd_loads_of = {0: p8b_loads(0), 1: p8b_loads(1)}
    for oc2 in range(C):
        b = oc2 % 2
        xd_loads = xd_loads_of[oc2]
        S.waitt("pe", wcr_wd[oc2])
        S.waitt("pe", cmr_re_xd)
        S.wait("pe", "dv", x1_dv[b])
        for tt in range(2):
            for kc in range(C):
                mm(PCR[b][tt][:, 0:512], WC2[b][:, kc * 128:(kc + 1) * 128],
                   CMRR[:, kc * T + tt * 512: kc * T + tt * 512 + 512],
                   start=(kc == 0), stop=(kc == C - 1),
                   inc=("pe" if (kc == C - 1 and tt == 1) else None))
        pe_cr = S.ctr["pe"]
        wc_pe_free[b] = pe_cr
        if oc2 + 2 < C:
            S.wait("pool", "pe", pe_cr)
            wcr_wd[oc2 + 2] = dma("pool", WC2[b], wcr_h[oc2 + 2], f"wc{b}")
        for tok_ in xd_loads:
            S.waitt("dve", tok_)
        S.waitt("dve", x1_spilled[b])
        S.wait("dve", "ac", sq8_ac[b])
        S.wait("dve", "pe", pe_cr)
        S.wait("act", "pe", pe_cr)
        for tt in range(2):
            sl = slice(tt * 512, tt * 512 + 512)
            S.wait("act", "dv", last_asm_dv)
            act(CH0, PCR[b][tt][:, 0:512], AF.Sigmoid, inc="ac")
            S.wait("dve", "ac", S.ctr["ac"])
            dve("tensor_add", X1CH[b][:, sl], XCH[b][:, sl], XM2[b][:, sl])
            dve("tensor_mul", CH1, CH0, KV2[b][:, sl])
            dve("tensor_add", X1CH[b][:, sl], X1CH[b][:, sl], CH1, inc="dv")
            last_asm_dv = S.ctr["dv"]
        S.wait("dve", "pe", x1_stats_pe[b])
        dve("tensor_copy", X1BF[b], X1CH[b], inc="dv")
        x1_dv[b] = S.ctr["dv"]
        S.wait("act", "dv", x1_dv[b])
        S.wait("act", "pe", x1_stats_pe[b])
        act(SQ8[b], X1CH[b], AF.Square, inc="ac")
        ac_sq = S.ctr["ac"]
        sq8_ac[b] = ac_sq
        S.wait("sp", "dv", x1_dv[b])
        x1_spilled[b] = dma("sp", x1_spill[oc2], X1CH[b], f"sx1{b}")
        if oc2 + 2 < C:
            xd_loads_of[oc2 + 2] = p8b_loads(oc2 + 2)
        stats_ready[oc2] = (x1_dv[b], ac_sq)
        if oc2 > 0:
            emit_x1_stats(oc2 - 1)
    emit_x1_stats(C - 1)
    a3_dv = stats_finish(pa, pb, T, SL_A, SL_B, S.ctr["pe"])

    S.barrier()
    H3 = sba("H3", [128, C * T], BF16, D0)
    BC_A = sba("BC_A3", [128, T], F32, D0 + 32896)
    BC_B = sba("BC_B3", [128, T], F32, D0 + 36992)
    broadcast(stv(SL_A, T), BC_A, T, a3_dv, PS[0])
    bc_ac = broadcast(stv(SL_B, T), BC_B, T, a3_dv, PS[0])
    XCH = [sba(f"XCHd{i}", [128, T], F32, D0 + 41088 + i * 4224) for i in range(4)]
    HXD = SC_H
    HXP = SC_D
    xch_free = [0, 0, 0, 0]
    for fc in range(C):
        b = fc % 4
        eng, hx = (dve, HXD) if fc % 2 == 0 else (pool, HXP)
        sem = "dv" if fc % 2 == 0 else "pl"
        ename = "dve" if fc % 2 == 0 else "pool"
        S.wait("sp", sem, xch_free[b])
        xd_tok = dma("sp", XCH[b], x1_spill[fc], f"xch{b}")
        S.waitt(ename, xd_tok)
        S.wait(ename, "ac", bc_ac)
        eng("tensor_mul", hx[:, 0:T], XCH[b], BC_A)
        eng("tensor_add", H3[:, fc * T:(fc + 1) * T], hx[:, 0:T], BC_B, inc=sem)
        xch_free[b] = S.ctr[sem]
    h3_dv = S.ctr["dv"]
    h3_pl = S.ctr["pl"]
    if "d_h3" in dbg_out:
        S.wait("sp", "dv", h3_dv)
        for fc in range(C):
            dma("sp", dbg_out["d_h3"][fc], H3[:, fc * T:(fc + 1) * T], "dbg")

    S.barrier()
    MID = sba("MID", [128, IC * T], BF16, D0 + 32896)
    WG = [sba(f"WG{i}", [128, 2048], BF16, D0 + 163968 + i * 4096) for i in range(2)]
    PG = [[PS[0], PS[1]], [PS[2], PS[3]]]
    PU = [[PS[4], PS[5]], [PS[6], PS[7]]]
    GS = [SC_H[:, 0:512], SC_H[:, 513:1025]]
    wgu_wd = {}
    piece_pe_free = [0, 0]
    mid_dv = [0, 0]
    gs_free = [0, 0]
    wgu_wd[0] = dma("pool", WG[0], wgu_h[0][:, 0:2048], "wg0")
    wgu_wd[1] = dma("pool", WG[1], wgu_h[0][:, 2048:4096], "wg1")
    for oi in range(IC):
        b = oi % 2
        S.wait("pe", "pl", h3_pl)
        S.wait("pe", "dv", h3_dv)
        S.wait("pe", "dv", mid_dv[b])
        grp_pe = []
        for part in range(2):
            ip = oi * 2 + part
            S.waitt("pe", wgu_wd[ip])
            ps_ = PG[b] if part == 0 else PU[b]
            for tt in range(2):
                for kc in range(C):
                    mm(ps_[tt][:, 0:512], WG[ip % 2][:, kc * 128:(kc + 1) * 128],
                       H3[:, kc * T + tt * 512: kc * T + tt * 512 + 512],
                       start=(kc == 0), stop=(kc == C - 1),
                       inc=("pe" if kc == C - 1 else None))
                grp_pe.append(S.ctr["pe"])
            piece_pe_free[ip % 2] = S.ctr["pe"]
            if ip + 2 < IC * 2:
                nip = ip + 2
                S.wait("pool", "pe", piece_pe_free[ip % 2])
                srcp = wgu_h[nip // 2][:, (nip % 2) * 2048:(nip % 2) * 2048 + 2048]
                wgu_wd[nip] = dma("pool", WG[nip % 2], srcp, f"wg{nip % 2}")
        for tt in range(2):
            S.wait("act", "pe", grp_pe[tt])
            S.wait("act", "dv", gs_free[tt])
            act(GS[tt], PG[b][tt][:, 0:512], AF.Silu, inc="ac")
            S.wait("dve", "ac", S.ctr["ac"])
            S.wait("dve", "pe", grp_pe[2 + tt])
            dve("tensor_mul", MID[:, oi * T + tt * 512: oi * T + tt * 512 + 512],
                GS[tt], PU[b][tt][:, 0:512], inc="dv")
            gs_free[tt] = S.ctr["dv"]
        mid_dv[b] = S.ctr["dv"]
    mid_done_dv = S.ctr["dv"]
    if "d_mid" in dbg_out:
        S.wait("sp", "dv", mid_done_dv)
        for oi in range(IC):
            dma("sp", dbg_out["d_mid"][oi], MID[:, oi * T:(oi + 1) * T], "dbg")

    S.barrier()
    WD2 = [sba("WD20", [128, 2048], BF16, D0 + 163968),
           sba("WD21", [128, 2048], BF16, D0 + 168064),
           sba("WD22", [128, 2048], BF16, D0)]
    PD = [[PS[0], PS[1]], [PS[2], PS[3]]]
    XCH = [sba(f"XCHe{i}", [128, T], F32, D0 + 4224 + i * 4096) for i in range(2)]
    OCH = [sba(f"OCH{i}", [128, T], F32, D0 + 12544 + i * 4096) for i in range(2)]
    wdn_wd = {}
    wd_pe_free = [0, 0, 0]
    out_dv = [0, 0]
    out_od = [None, None]
    for i in range(3):
        wdn_wd[i] = dma("pool", WD2[i], wdn_h[i], f"wn{i}")
    for oc2 in range(C):
        b = oc2 % 2
        S.wait("sp", "dv", out_dv[b])
        xd_tok = dma("sp", XCH[b], x1_spill[oc2], f"xch{b}")
        S.wait("pe", "dv", out_dv[b])
        for kcg in range(4):
            idx = oc2 * 4 + kcg
            S.waitt("pe", wdn_wd[idx])
            S.wait("pe", "dv", mid_done_dv)
            for kc in range(C):
                ic = kcg * 16 + kc
                for tt in range(2):
                    mm(PD[b][tt][:, 0:512], WD2[idx % 3][:, kc * 128:(kc + 1) * 128],
                       MID[:, ic * T + tt * 512: ic * T + tt * 512 + 512],
                       start=(kcg == 0 and kc == 0), stop=(kcg == 3 and kc == C - 1),
                       inc=("pe" if (kc == C - 1 and tt == 1) else None))
            wd_pe_free[idx % 3] = S.ctr["pe"]
            if idx + 3 < IC:
                S.wait("pool", "pe", wd_pe_free[idx % 3])
                wdn_wd[idx + 3] = dma("pool", WD2[idx % 3], wdn_h[idx + 3], f"wn{idx % 3}")
        pe_dn = S.ctr["pe"]
        S.waitt("dve", xd_tok)
        S.wait("dve", "pe", pe_dn)
        S.waitt("dve", out_od[b])
        dve("tensor_add", OCH[b][:, 0:512], XCH[b][:, 0:512], PD[b][0][:, 0:512])
        dve("tensor_add", OCH[b][:, 512:1024], XCH[b][:, 512:1024], PD[b][1][:, 0:512],
            inc="dv")
        out_dv[b] = S.ctr["dv"]
        S.wait("sp", "dv", out_dv[b])
        out_od[b] = dma("sp", out_h[oc2], OCH[b], f"od{b}")
    S.wait("sp", "od0", S.ctr.get("od0", 0))
    S.wait("sp", "od1", S.ctr.get("od1", 0))

    with contextlib.ExitStack() as st:
        for name in list(S.sems):
            S.sems[name] = st.enter_context(nc.semaphore(name))
        with nc.Block() as block:
            S.replay(block)
    return nc


def _prep(inputs):
    f32 = np.float32
    _ = inputs.get("L0_attn_tm_decay")
    x = np.asarray(inputs["L0_attn_x"], f32)[0]
    Wk = np.asarray(inputs["L0_attn_Wk"], f32)
    Wv = np.asarray(inputs["L0_attn_Wv"], f32)
    Wr = np.asarray(inputs["L0_attn_Wr"], f32)
    Wo = np.asarray(inputs["L0_attn_Wo"], f32)
    Wkey = np.asarray(inputs["L0_attn_Wkey"], f32)
    Wval = np.asarray(inputs["L0_attn_Wval"], f32)
    Wcr = np.asarray(inputs["L0_attn_Wcr"], f32)
    Wgu = np.asarray(inputs["L0_ffn_w_gate_up"], f32)
    Wdn = np.asarray(inputs["L0_ffn_w_down"], f32)

    A = np.stack([Wk, Wv, Wr])
    wkvr = np.ascontiguousarray(
        A.reshape(3, 16, 128, 16, 128).transpose(1, 4, 0, 3, 2).reshape(16, 128, 3 * 2048))
    wo = np.ascontiguousarray(
        Wo.reshape(16, 128, 16, 128).transpose(0, 3, 2, 1).reshape(16, 128, 2048))
    wkey = np.ascontiguousarray(
        Wkey.reshape(64, 128, 16, 128).transpose(0, 3, 2, 1).reshape(64, 128, 2048))
    wval = np.ascontiguousarray(
        Wval.reshape(16, 128, 4, 16, 128).transpose(0, 2, 4, 3, 1).reshape(64, 128, 2048))
    G = Wgu[:8192].reshape(64, 128, 16, 128)
    Uu = Wgu[8192:].reshape(64, 128, 16, 128)
    wgu = np.ascontiguousarray(
        np.stack([G, Uu]).transpose(1, 4, 0, 3, 2).reshape(64, 128, 4096))
    wdn = np.ascontiguousarray(
        Wdn.reshape(16, 128, 4, 16, 128).transpose(0, 2, 4, 3, 1).reshape(64, 128, 2048))
    wcr = np.ascontiguousarray(
        Wcr.reshape(16, 128, 16, 128).transpose(0, 3, 2, 1).reshape(16, 128, 2048))

    vec_list = [
        inputs["L0_attn_ln1_g"], inputs["L0_attn_ln1_b"],
        inputs["L0_attn_ln2_g"], inputs["L0_attn_ln2_b"],
        inputs["L0_attn_tm_first"],
        np.asarray(inputs["L0_attn_tm_mk"]).reshape(-1),
        np.asarray(inputs["L0_attn_tm_mv"]).reshape(-1),
        np.asarray(inputs["L0_attn_tm_mr"]).reshape(-1),
        np.asarray(inputs["L0_attn_cm_mk"]).reshape(-1),
        np.asarray(inputs["L0_attn_cm_mr"]).reshape(-1),
    ]
    V = np.stack([np.asarray(v, f32).reshape(2048) for v in vec_list])
    vecs = np.zeros((128, NVEC), f32)
    vecs[:, :160] = V.reshape(10, 16, 128).transpose(2, 0, 1).reshape(128, 160)
    vecs[:, V_EPS] = EPS
    vecs[:, V_NHALF] = -0.5
    vecs[:, V_NONE] = -1.0
    vecs[:, V_HALF] = 0.5
    vecs[:, V_NEG] = NEG
    vecs = np.ascontiguousarray(vecs)

    x_fm = np.ascontiguousarray(x.T)
    in_maps = []
    for ci in range(NCORES):
        s = ci * T
        sl = np.zeros((2048, 1026), f32)
        if s - 2 < 0:
            sl[:, 2:] = x_fm[:, s:s + T]
        else:
            sl[:, :] = x_fm[:, s - 2:s + T]
        ximg = np.ascontiguousarray(
            sl.reshape(16, 128, 1026).transpose(1, 0, 2).reshape(128, 16 * 1026))
        hm = np.full((128, 1), 0.0 if ci == 0 else 1.0, f32)
        in_maps.append(dict(
            x_img=ximg, wkvr=wkvr, wo=wo, wkey=wkey, wval=wval,
            wgu=wgu, wdn=wdn, wcr=wcr, vecs=vecs, hmask=hm))
    return in_maps


def kernel(**inputs):
    in_maps = _prep(inputs)
    nc = _build()
    res = run_bass_kernel_spmd(nc, in_maps, core_ids=list(range(NCORES)))
    parts = []
    for ci in range(NCORES):
        o = np.asarray(res.results[ci]["out"], np.float32)
        parts.append(o.reshape(2048, T))
    full = np.concatenate(parts, axis=1)
    return np.ascontiguousarray(full.T).reshape(1, 8192, 2048).astype(np.float32)


# version 09
# speedup vs baseline: 1.0598x; 1.0203x over previous
import contextlib
import numpy as np
import concourse.bass as bass
import concourse.mybir as mybir
from concourse.bass_utils import run_bass_kernel_spmd

F32 = mybir.dt.float32
BF16 = mybir.dt.bfloat16
AF = mybir.ActivationFunctionType
ALU = mybir.AluOpType

NCORES = 8
T = 1024
C = 16
IC = 64
H = 2048
EPS = 1e-5
NEG = -1e38

V_G1, V_B1, V_G2, V_B2, V_TMF, V_MK, V_MV, V_MR, V_CMK, V_CMR = (
    0, 16, 32, 48, 64, 80, 96, 112, 128, 144)
V_EPS, V_NHALF, V_NONE, V_HALF, V_NEG = 160, 161, 162, 163, 164
NVEC = 168

ROWS3 = [0, 32, 64]


def _tiles(n, cap=512):
    out, s = [], 0
    while s < n:
        l = min(cap, n - s)
        out.append((s, l))
        s += l
    return out


class Sched:
    ENGS = ("pe", "dve", "act", "sp", "pool")

    def __init__(self):
        self.items = {e: [] for e in self.ENGS}
        self.sems = {}
        self.ctr = {}
        self.hiw = {}

    def sem(self, name):
        if name not in self.sems:
            self.sems[name] = None
            self.ctr[name] = 0

    def emit(self, eng, fn, inc=None, incn=1):
        if inc is not None:
            self.ctr[inc] += incn
        self.items[eng].append((fn, inc, incn))
        return self.ctr[inc] if inc is not None else None

    def wait(self, eng, sem, val):
        if val is None or val <= 0:
            return
        key = (eng, sem)
        if self.hiw.get(key, 0) >= val:
            return
        self.hiw[key] = val

        def f(e, sem=sem, val=val):
            e.wait_ge(self.sems[sem], val)
            return None
        self.items[eng].append((f, None, 1))

    def barrier(self, engs=None):
        vals = [(s, self.ctr[s]) for s in self.ctr]
        for eng in (engs or self.ENGS):
            for s, v in vals:
                self.wait(eng, s, v)

    def waitt(self, eng, tok):
        if tok:
            self.wait(eng, tok[0], tok[1])

    def replay(self, block):
        def mk(name):
            def run(e):
                for fn, inc, incn in self.items[name]:
                    inst = fn(e)
                    if inc is not None:
                        inst.then_inc(self.sems[inc], incn)
            return run
        block.gpsimd(mk("pool"))
        block.sync(mk("sp"))
        block.tensor(mk("pe"))
        block.vector(mk("dve"))
        block.scalar(mk("act"))


def _build(dbg=()):
    nc = bass.Bass("TRN2", target_bir_lowering=False)
    dbg = set(dbg)

    x_img = nc.dram_tensor("x_img", [128, C * 1026], F32, kind="ExternalInput").ap()
    wkvr_h = nc.dram_tensor("wkvr", [C, 128, 3 * 2048], F32, kind="ExternalInput").ap()
    wo_h = nc.dram_tensor("wo", [C, 128, 2048], F32, kind="ExternalInput").ap()
    wkey_h = nc.dram_tensor("wkey", [IC, 128, 2048], F32, kind="ExternalInput").ap()
    wval_h = nc.dram_tensor("wval", [IC, 128, 2048], F32, kind="ExternalInput").ap()
    wgu_h = nc.dram_tensor("wgu", [IC, 128, 4096], F32, kind="ExternalInput").ap()
    wdn_h = nc.dram_tensor("wdn", [IC, 128, 2048], F32, kind="ExternalInput").ap()
    wcr_h = nc.dram_tensor("wcr", [C, 128, 2048], F32, kind="ExternalInput").ap()
    vecs_h = nc.dram_tensor("vecs", [128, NVEC], F32, kind="ExternalInput").ap()
    hmask_h = nc.dram_tensor("hmask", [128, 1], F32, kind="ExternalInput").ap()
    out_h = nc.dram_tensor("out", [C, 128, T], F32, kind="ExternalOutput").ap()

    def scratch(name, shape, dt):
        kind = "ExternalOutput" if name in dbg else "Internal"
        return nc.dram_tensor(name, shape, dt, kind=kind).ap()

    xm_spill = scratch("xm_sp", [C, 128, 1025], F32)
    cmr_spill = scratch("cmr_sp", [C, 128, T], BF16)
    x1_spill = scratch("x1_sp", [C, 128, T], F32)
    kv_spill = scratch("kv_sp", [C, 128, T], F32)

    dbg_out = {}
    for nm, shape, dt in [
        ("d_bca", [128, 1026], F32), ("d_bcb", [128, 1026], F32),
        ("d_xk", [C, 128, 1025], BF16), ("d_xv", [C, 128, 1025], BF16),
        ("d_xr", [C, 128, 1025], BF16), ("d_u", [C, 128, 1025], BF16),
        ("d_cmk", [C, 128, T], BF16), ("d_h3", [C, 128, T], BF16),
        ("d_silu", [IC, 128, T], BF16), ("d_mid", [IC, 128, T], BF16),
        ("d_k", [C, 128, T], F32), ("d_v", [C, 128, T], F32),
        ("d_r", [C, 128, T], F32), ("d_stv", [6, 1, 1026], F32),
        ("d_hkvr", [128, 192], F32),
    ]:
        if nm in dbg:
            dbg_out[nm] = nc.dram_tensor(nm, shape, dt, kind="ExternalOutput").ap()

    S = Sched()
    for s in ("pe", "dv", "ac", "pl"):
        S.sem(s)

    BASE = (nc.sbuf_base + 127) // 128 * 128
    TOP = nc.sbuf_base + nc.sbuf_bytes_remaining

    def sba(name, shape, dtype, off):
        a = BASE + off
        nb = int(np.prod(shape[1:])) * (2 if dtype == BF16 else 4)
        assert a + nb <= TOP, (name, a + nb, TOP)
        return nc.alloc_sbuf_tensor_at(name, shape, dtype, offset=a).ap()

    VEC = sba("VEC", [128, NVEC], F32, 0)
    HM = sba("HM", [128, 1], F32, 768)
    ONESB = sba("ONESB", [128, 1], BF16, 896)
    ONESC = sba("ONESC", [1, 128], F32, 1024)
    STV = [sba(f"STV{i}", [1, 1026], F32, 1536 + i * 4224) for i in range(6)]
    SL_A0, SL_B0, SL_A, SL_B, SL_T0, SL_T1 = range(6)
    SC_H = sba("SC_H", [128, 1026], F32, 26880)
    SC_D = sba("SC_D", [128, 1026], F32, 31104)
    D0 = 35328

    def stv(i, n=1026):
        return STV[i][0:1, 0:n]

    def vcol(base, fc):
        return VEC[:, base + fc: base + fc + 1]

    PS = [nc.alloc_psum_tensor(f"PSB{i}", [128, 512], F32).ap() for i in range(8)]

    def dma(eng, dst, src, sem):
        S.sem(sem)

        def f(e, dst=dst, src=src):
            return e.dma_start(dst, src)
        return (sem, S.emit(eng, f, inc=sem, incn=16))

    def mm(psum, lhsT, rhs, start, stop, inc=None):
        def f(e, psum=psum, lhsT=lhsT, rhs=rhs, start=start, stop=stop):
            return e.matmul(psum, lhsT, rhs, start=start, stop=stop)
        return S.emit("pe", f, inc=inc)

    def act(out, in_, func, bias=0.0, scale=1.0, inc=None):
        def f(e, out=out, in_=in_, func=func, bias=bias, scale=scale):
            return e.activation(out, in_, func, bias=bias, scale=scale)
        return S.emit("act", f, inc=inc)

    def dve(op, *args, inc=None):
        def f(e, op=op, args=args):
            return getattr(e, op)(*args)
        return S.emit("dve", f, inc=inc)

    def pool(op, *args, inc=None):
        def f(e, op=op, args=args):
            return getattr(e, op)(*args)
        return S.emit("pool", f, inc=inc)

    def stats_finish(pa, pb, ncols, slot_a, slot_b, pe_done):
        tl = _tiles(ncols)
        S.wait("act", "pe", pe_done)
        for ti, (s, l) in enumerate(tl):
            act(stv(SL_T0)[0:1, s:s + l], pa[ROWS3[ti]:ROWS3[ti] + 1, 0:l], AF.Identity)
        for ti, (s, l) in enumerate(tl):
            act(stv(SL_T1)[0:1, s:s + l], pb[ROWS3[ti]:ROWS3[ti] + 1, 0:l], AF.Identity,
                inc=("ac" if ti == len(tl) - 1 else None))
        S.wait("dve", "ac", S.ctr["ac"])
        m = stv(SL_T0, ncols)
        q = stv(SL_T1, ncols)
        a = stv(slot_a, ncols)
        b = stv(slot_b, ncols)
        dve("tensor_scalar", m, m, 1.0 / H, None, ALU.mult)
        dve("tensor_scalar", q, q, 1.0 / H, None, ALU.mult)
        dve("tensor_mul", b, m, m)
        dve("tensor_sub", q, q, b, inc="dv")
        S.wait("act", "dv", S.ctr["dv"])
        act(a, q, AF.Ln, bias=VEC[0:1, V_EPS:V_EPS + 1])
        act(a, a, AF.Exp, scale=VEC[0:1, V_NHALF:V_NHALF + 1], inc="ac")
        S.wait("dve", "ac", S.ctr["ac"])
        dve("scalar_tensor_tensor", b, m, -1.0, a, ALU.mult, ALU.mult, inc="dv")
        return S.ctr["dv"]

    def broadcast(vec_ap, dst, ncols, dv_ready, psb):
        S.wait("pe", "dv", dv_ready)
        last = None
        for (s, l) in _tiles(ncols):
            S.wait("pe", "ac", S.ctr["ac"])
            mm(psb[:, 0:l], ONESC, vec_ap[0:1, s:s + l], start=True, stop=True, inc="pe")
            S.wait("act", "pe", S.ctr["pe"])
            last = act(dst[:, s:s + l], psb[:, 0:l], AF.Identity, inc="ac")
        return last

    M2 = sba("M2", [128, 512], F32, 209920)
    dma("sp", VEC, vecs_h, "ld")
    hm_tok = dma("sp", HM, hmask_h, "ld")
    for e_ in ("act", "dve", "pool"):
        S.wait(e_, "ld", 32)
    dve("memset", ONESB, 1.0)
    dve("memset", ONESC, 1.0, inc="dv")
    dve("memset", M2, 1.0, inc="dv")
    S.wait("dve", "dv", S.ctr["dv"])
    dve("tensor_copy", M2[:, 0:1], HM, inc="dv")
    S.wait("dve", "dv", S.ctr["dv"])
    dve("tensor_copy", M2[:, 1:2], HM, inc="dv")

    XBF = sba("XBF", [128, C * 1026], BF16, D0)
    xbf_tok = dma("pool", XBF, x_img, "xbf")

    SQ = [sba(f"SQ{i}", [128, 1026], BF16, D0 + 33024 + i * 2176) for i in range(2)]
    pa, pb = PS[0], PS[1]
    tl6 = _tiles(1026)
    sq_pe_free = [0, 0]
    for fc in range(C):
        b = fc % 2
        xv = XBF[:, fc * 1026:(fc + 1) * 1026]
        S.waitt("act", xbf_tok)
        S.wait("act", "pe", sq_pe_free[b])
        act(SQ[b], xv, AF.Square, inc="ac")
        sq_ac = S.ctr["ac"]
        S.waitt("pe", xbf_tok)
        S.wait("pe", "ac", sq_ac)
        for ti, (s, l) in enumerate(tl6):
            mm(pa[ROWS3[ti]:ROWS3[ti] + 1, 0:l], ONESB, xv[:, s:s + l],
               start=(fc == 0), stop=(fc == C - 1))
        for ti, (s, l) in enumerate(tl6):
            mm(pb[ROWS3[ti]:ROWS3[ti] + 1, 0:l], ONESB, SQ[b][:, s:s + l],
               start=(fc == 0), stop=(fc == C - 1),
               inc=("pe" if ti == len(tl6) - 1 else None))
        sq_pe_free[b] = S.ctr["pe"]
    a0_dv = stats_finish(pa, pb, 1026, SL_A0, SL_B0, S.ctr["pe"])

    BC_A = sba("BC_A1", [128, 1026], F32, D0 + 140032)
    BC_B = sba("BC_B1", [128, 1026], F32, D0 + 144256)
    broadcast(stv(SL_A0), BC_A, 1026, a0_dv, PS[2])
    bc_ac = broadcast(stv(SL_B0), BC_B, 1026, a0_dv, PS[2])

    XCH = [sba(f"XCHa{i}", [128, 1026], F32, D0 + 131584 + i * 4224) for i in range(2)]
    XABF = [sba(f"XABF{i}", [128, 1026], BF16, D0 + 37376 + i * 2176) for i in range(2)]
    xch_free = [0, 0]
    st_pe_free = [0, 0]
    sch_ac_free = 0
    for fc in range(C):
        b = fc % 2
        S.wait("sp", "dv", xch_free[b])
        xd_tok = dma("sp", XCH[b], x_img[:, fc * 1026:(fc + 1) * 1026], f"xch{b}")
        S.waitt("dve", xd_tok)
        S.wait("dve", "ac", bc_ac)
        S.wait("dve", "pe", st_pe_free[b])
        S.wait("dve", "ac", sch_ac_free)
        dve("tensor_mul", SC_H, XCH[b], BC_A)
        dve("tensor_add", SC_H, SC_H, BC_B)
        dve("tensor_copy", XABF[b], SC_H, inc="dv")
        xch_free[b] = S.ctr["dv"]
        S.wait("act", "dv", S.ctr["dv"])
        S.wait("act", "pe", st_pe_free[b])
        act(SQ[b], SC_H, AF.Square, inc="ac")
        sch_ac_free = S.ctr["ac"]
        S.wait("pe", "dv", S.ctr["dv"])
        S.wait("pe", "ac", S.ctr["ac"])
        for ti, (s, l) in enumerate(tl6):
            mm(pa[ROWS3[ti]:ROWS3[ti] + 1, 0:l], ONESB, XABF[b][:, s:s + l],
               start=(fc == 0), stop=(fc == C - 1))
        for ti, (s, l) in enumerate(tl6):
            mm(pb[ROWS3[ti]:ROWS3[ti] + 1, 0:l], ONESB, SQ[b][:, s:s + l],
               start=(fc == 0), stop=(fc == C - 1),
               inc=("pe" if ti == len(tl6) - 1 else None))
        st_pe_free[b] = S.ctr["pe"]
    a1_dv = stats_finish(pa, pb, 1026, SL_A, SL_B, S.ctr["pe"])
    S.wait("dve", "dv", a1_dv)
    dve("tensor_mul", stv(SL_T0), stv(SL_B0), stv(SL_A))
    dve("tensor_add", stv(SL_T0), stv(SL_T0), stv(SL_B))
    dve("tensor_mul", stv(SL_T1), stv(SL_A0), stv(SL_A), inc="dv")
    a01_dv = S.ctr["dv"]
    broadcast(stv(SL_T1), BC_A, 1026, a01_dv, PS[2])
    bc_ac = broadcast(stv(SL_T0), BC_B, 1026, a01_dv, PS[2])
    if "d_stv" in dbg_out:
        S.wait("sp", "dv", a01_dv)
        for i in range(6):
            dma("sp", dbg_out["d_stv"][i], STV[i][0:1, 0:1026], "dbg")

    XK = sba("XK", [128, C * 1025], BF16, D0)
    XV = sba("XV", [128, C * 1025], BF16, D0 + 32896)
    XR = sba("XR", [128, C * 1025], BF16, D0 + 65792)
    WKVR = [sba(f"WKVR{i}", [128, 3 * 2048], BF16, D0 + 148480 + i * 12288)
            for i in range(2)]
    wkvr_wd = {}
    wkvr_wd[0] = dma("pool", WKVR[0], wkvr_h[0], "wk0")
    wkvr_wd[1] = dma("pool", WKVR[1], wkvr_h[1], "wk1")

    SC_H2 = sba("SC_H2", [128, 1026], F32, D0 + 98688)
    SC_HX = [SC_H, SC_H2]
    hx_free = [0, 0]
    S.wait("sp", "dv", max(xch_free))
    xch_free = [0, 0]
    for fc in range(C):
        b = fc % 2
        S.wait("sp", "pl", xch_free[b])
        xd_tok = dma("sp", XCH[b], x_img[:, fc * 1026:(fc + 1) * 1026], f"xch{b}")
        hx = SC_HX[b]
        S.waitt("pool", xd_tok)
        S.wait("pool", "ac", bc_ac)
        S.wait("pool", "dv", hx_free[b])
        pool("tensor_mul", hx, XCH[b], BC_A)
        pool("tensor_add", hx, hx, BC_B, inc="pl")
        xch_free[b] = S.ctr["pl"]
        S.wait("dve", "pl", S.ctr["pl"])
        dve("tensor_scalar", hx, hx, vcol(V_G1, fc), vcol(V_B1, fc), ALU.mult, ALU.add)
        dve("tensor_mul", hx[:, 0:512], hx[:, 0:512], M2)
        dve("tensor_sub", SC_D[:, 0:1025], hx[:, 1:1026], hx[:, 0:1025])
        o = fc * 1025
        dve("scalar_tensor_tensor", XK[:, o:o + 1025], SC_D[:, 0:1025], vcol(V_MK, fc),
            hx[:, 0:1025], ALU.mult, ALU.add)
        dve("scalar_tensor_tensor", XV[:, o:o + 1025], SC_D[:, 0:1025], vcol(V_MV, fc),
            hx[:, 0:1025], ALU.mult, ALU.add)
        dve("scalar_tensor_tensor", XR[:, o:o + 1025], SC_D[:, 0:1025], vcol(V_MR, fc),
            hx[:, 0:1025], ALU.mult, ALU.add, inc="dv")
        hx_free[b] = S.ctr["dv"]
    mixes_dv = S.ctr["dv"]
    mixes_pl = S.ctr["pl"]
    if "d_xk" in dbg_out:
        S.wait("sp", "dv", mixes_dv)
        for fc in range(C):
            dma("sp", dbg_out["d_xk"][fc], XK[:, fc * 1025:(fc + 1) * 1025], "dbg")
            dma("sp", dbg_out["d_xv"][fc], XV[:, fc * 1025:(fc + 1) * 1025], "dbg")
            dma("sp", dbg_out["d_xr"][fc], XR[:, fc * 1025:(fc + 1) * 1025], "dbg")
    if "d_bca" in dbg_out:
        S.wait("sp", "ac", bc_ac)
        dma("sp", dbg_out["d_bca"], BC_A, "dbg")
        dma("sp", dbg_out["d_bcb"], BC_B, "dbg")

    S.barrier()
    U = sba("U", [128, C * 1025], BF16, D0 + 98688)
    CH = [sba(f"CH{i}", [128, 512], F32, D0 + 131584 + i * 2048) for i in range(8)]
    PSK = [PS[0], PS[1]]
    PSV = [PS[2], PS[3]]
    PSR = [PS[4], PS[5]]
    HK, HV, HR = PS[0][:, 0:1], PS[2][:, 0:1], PS[4][:, 0:1]
    HW = PS[6][:, 0:1]
    HSTX = sba("HSTX", [128, 192], F32, 209152)
    HST = HSTX[:, 0:48]
    SPC = SC_H[:, 0:512]

    chain_dv_set = [0, 0]
    wkvr_pe_free = [0, 0]
    halo_stage_ac = 0
    for oc in range(C):
        wb = WKVR[oc % 2]
        S.waitt("pe", wkvr_wd[oc])
        S.wait("pe", "dv", mixes_dv)
        for tt in range(2):
            S.wait("pe", "dv", chain_dv_set[tt])
            if tt == 0:
                S.wait("pe", "ac", halo_stage_ac)
            tok0 = 1 + tt * 512
            for mi, dst, mixsrc in ((0, PSK, XK), (1, PSV, XV), (2, PSR, XR)):
                for kc in range(C):
                    mm(dst[tt][:, 0:512],
                       wb[:, (mi * 16 + kc) * 128:(mi * 16 + kc + 1) * 128],
                       mixsrc[:, kc * 1025 + tok0: kc * 1025 + tok0 + 512],
                       start=(kc == 0), stop=(kc == C - 1),
                       inc=("pe" if kc == C - 1 else None))
            pe_kvr = S.ctr["pe"]
            c0, c1, c2, c3 = CH[tt * 4: tt * 4 + 4]
            uo = U[:, oc * 1025 + tok0: oc * 1025 + tok0 + 512]
            S.wait("act", "pe", pe_kvr - 2)
            act(c0, PSK[tt][:, 0:512], AF.Identity, bias=vcol(V_TMF, oc), inc="ac")
            S.wait("dve", "ac", S.ctr["ac"])
            dve("tensor_scalar", c1, c0, VEC[:, V_NEG:V_NEG + 1], None, ALU.max)
            dve("tensor_sub", c0, c0, c1, inc="dv")
            S.wait("act", "dv", S.ctr["dv"])
            act(c1, c0, AF.Exp)
            act(c2, c0, AF.Exp, scale=VEC[:, V_NONE:V_NONE + 1])
            S.wait("act", "pe", pe_kvr)
            act(c3, PSR[tt][:, 0:512], AF.Tanh, scale=VEC[:, V_HALF:V_HALF + 1], inc="ac")
            S.wait("dve", "ac", S.ctr["ac"])
            S.wait("dve", "pe", pe_kvr)
            dve("scalar_tensor_tensor", c0, PSV[tt][:, 0:512], 0.5, c1, ALU.mult, ALU.mult)
            dve("tensor_mul", c0, c0, c2)
            dve("scalar_tensor_tensor", uo, c3, 1.0, c0, ALU.add, ALU.mult, inc="dv")
            chain_dv_set[tt] = S.ctr["dv"]
            if "d_k" in dbg_out:
                DBG1, DBG2, DBG3 = SC_H[:, 0:512], SC_H[:, 513:1025], SC_D[:, 514:1026]
                S.wait("act", "dv", chain_dv_set[tt])
                act(DBG1, PSK[tt][:, 0:512], AF.Identity)
                act(DBG2, PSV[tt][:, 0:512], AF.Identity)
                act(DBG3, PSR[tt][:, 0:512], AF.Identity, inc="ac")
                S.wait("sp", "ac", S.ctr["ac"])
                dma("sp", dbg_out["d_k"][oc][:, tt * 512:tt * 512 + 512], DBG1, "dbg")
                dma("sp", dbg_out["d_v"][oc][:, tt * 512:tt * 512 + 512], DBG2, "dbg")
                dv_dump = dma("sp", dbg_out["d_r"][oc][:, tt * 512:tt * 512 + 512], DBG3, "dbg")
                S.waitt("pe", dv_dump)
                S.waitt("dve", dv_dump)
        S.wait("pe", "ac", halo_stage_ac)
        S.wait("pe", "dv", chain_dv_set[0])
        for mi, dsth, mixsrc in ((0, HK, XK), (1, HV, XV), (2, HR, XR)):
            for kc in range(C):
                mm(dsth, wb[:, (mi * 16 + kc) * 128:(mi * 16 + kc + 1) * 128],
                   mixsrc[:, kc * 1025: kc * 1025 + 1],
                   start=(kc == 0), stop=(kc == C - 1),
                   inc=("pe" if (mi == 2 and kc == C - 1) else None))
        pe_halo = S.ctr["pe"]
        wkvr_pe_free[oc % 2] = pe_halo
        if oc + 2 < C:
            S.wait("pool", "pe", pe_halo)
            wkvr_wd[oc + 2] = dma("pool", WKVR[oc % 2], wkvr_h[oc + 2], f"wk{oc % 2}")
        S.wait("act", "pe", pe_halo)
        act(HST[:, oc:oc + 1], HK, AF.Identity)
        act(HST[:, 16 + oc:16 + oc + 1], HV, AF.Identity)
        act(HST[:, 32 + oc:32 + oc + 1], HR, AF.Identity, inc="ac")
        halo_stage_ac = S.ctr["ac"]
    W16 = HSTX[:, 48:64]
    P16 = HSTX[:, 64:80]
    E16 = HSTX[:, 80:96]
    R16 = HSTX[:, 96:112]
    T16 = HSTX[:, 112:128]
    N16 = HSTX[:, 128:144]
    UH16 = HSTX[:, 144:160]
    act(SPC, SPC, AF.Identity, inc="ac")
    S.wait("dve", "ac", S.ctr["ac"])
    dve("tensor_add", W16, HST[:, 0:16], VEC[:, V_TMF:V_TMF + 16])
    dve("tensor_copy", SPC, SPC)
    dve("tensor_scalar", P16, W16, VEC[:, V_NEG:V_NEG + 1], None, ALU.max)
    dve("tensor_copy", SPC, SPC)
    dve("tensor_sub", W16, W16, P16)
    dve("tensor_copy", SPC, SPC, inc="dv")
    S.wait("act", "dv", S.ctr["dv"])
    act(E16, W16, AF.Exp)
    act(R16, W16, AF.Exp, scale=VEC[:, V_NONE:V_NONE + 1])
    act(T16, HST[:, 32:48], AF.Tanh, scale=VEC[:, V_HALF:V_HALF + 1])
    act(SPC, SPC, AF.Identity, inc="ac")
    S.wait("dve", "ac", S.ctr["ac"])
    dve("scalar_tensor_tensor", N16, HST[:, 16:32], 0.5, E16, ALU.mult, ALU.mult)
    dve("tensor_copy", SPC, SPC)
    dve("tensor_mul", N16, N16, R16)
    dve("tensor_copy", SPC, SPC)
    dve("scalar_tensor_tensor", UH16, T16, 1.0, N16, ALU.add, ALU.mult)
    dve("tensor_copy", SPC, SPC)
    for oc in range(C):
        dve("tensor_copy", U[:, oc * 1025: oc * 1025 + 1], UH16[:, oc:oc + 1])
    dve("tensor_copy", SPC, SPC, inc="dv")
    if "d_hkvr" in dbg_out:
        S.wait("sp", "dv", S.ctr["dv"])
        dma("sp", dbg_out["d_hkvr"], HSTX, "dbg")
    u_done_dv = S.ctr["dv"]
    if "d_u" in dbg_out:
        S.wait("sp", "dv", u_done_dv)
        for fc in range(C):
            dma("sp", dbg_out["d_u"][fc], U[:, fc * 1025:(fc + 1) * 1025], "dbg")

    S.barrier()
    BC_A = sba("BC_A2", [128, 1026], F32, D0 + 165376)
    BC_B = sba("BC_B2", [128, 1026], F32, D0 + 169600)
    broadcast(stv(SL_A0), BC_A, 1026, a0_dv, PS[7])
    bc_ac = broadcast(stv(SL_B0), BC_B, 1026, a0_dv, PS[7])

    XCH = [sba(f"XCHb{i}", [128, 1026], F32, D0 + 139776 + i * 4224) for i in range(2)]
    WO = [sba(f"WO{i}", [128, 2048], BF16, D0 + 131584 + i * 4096) for i in range(2)]
    PSW = [[PS[0], PS[1]], [PS[2], PS[3]]]
    XMCH = [sba(f"XMCH{i}", [128, 1025], F32, D0 + 148224 + i * 4224) for i in range(2)]
    XMBF = [sba(f"XMBF{i}", [128, 1025], BF16, D0 + 156672 + i * 2176) for i in range(2)]
    SQ5 = [sba(f"SQ5{i}", [128, 1025], BF16, D0 + 161024 + i * 2176) for i in range(2)]
    pa, pb = PS[4], PS[5]
    tl5 = _tiles(1025)

    wo_wd = {}
    wo_wd[0] = dma("pool", WO[0], wo_h[0], "wwo0")
    wo_wd[1] = dma("pool", WO[1], wo_h[1], "wwo1")
    wo_pe_free = [0, 0]
    xm_halo_dv_last = 0
    xm_dv = [0, 0]
    xm_spilled = [None, None]
    xm_stats_pe = [0, 0]
    xch_free = [0, 0]
    sq5_ac = [0, 0]
    for oc in range(C):
        b = oc % 2
        S.waitt("pe", wo_wd[oc])
        S.wait("pe", "dv", u_done_dv)
        S.wait("pe", "dv", xm_dv[b])
        for tt in range(2):
            tok0 = 1 + tt * 512
            for kc in range(C):
                mm(PSW[b][tt][:, 0:512], WO[b][:, kc * 128:(kc + 1) * 128],
                   U[:, kc * 1025 + tok0: kc * 1025 + tok0 + 512],
                   start=(kc == 0), stop=(kc == C - 1),
                   inc=("pe" if kc == C - 1 else None))
        S.wait("pe", "dv", xm_halo_dv_last)
        for kc in range(C):
            mm(HW, WO[b][:, kc * 128:(kc + 1) * 128],
               U[:, kc * 1025: kc * 1025 + 1],
               start=(kc == 0), stop=(kc == C - 1),
               inc=("pe" if kc == C - 1 else None))
        pe_wo = S.ctr["pe"]
        wo_pe_free[b] = pe_wo
        if oc + 2 < C:
            S.wait("pool", "pe", pe_wo)
            wo_wd[oc + 2] = dma("pool", WO[b], wo_h[oc + 2], f"wwo{b}")
        S.wait("sp", "dv", xch_free[b])
        xd_tok = dma("sp", XCH[b], x_img[:, oc * 1026:(oc + 1) * 1026], f"xch{b}")
        S.waitt("dve", xd_tok)
        S.wait("dve", "ac", bc_ac)
        S.waitt("dve", xm_spilled[b])
        S.wait("dve", "ac", sq5_ac[b])
        dve("tensor_mul", XMCH[b], XCH[b][:, 1:1026], BC_A[:, 1:1026])
        dve("tensor_add", XMCH[b], XMCH[b], BC_B[:, 1:1026], inc="dv")
        xch_free[b] = S.ctr["dv"]
        S.wait("dve", "pe", pe_wo)
        dve("tensor_add", XMCH[b][:, 0:1], XMCH[b][:, 0:1], HW, inc="dv")
        xm_halo_dv_last = S.ctr["dv"]
        dve("tensor_add", XMCH[b][:, 1:513], XMCH[b][:, 1:513], PSW[b][0][:, 0:512])
        dve("tensor_add", XMCH[b][:, 513:1025], XMCH[b][:, 513:1025], PSW[b][1][:, 0:512])
        S.wait("dve", "pe", xm_stats_pe[b])
        dve("tensor_copy", XMBF[b], XMCH[b], inc="dv")
        xm_dv[b] = S.ctr["dv"]
        S.wait("act", "dv", xm_dv[b])
        S.wait("act", "pe", xm_stats_pe[b])
        act(SQ5[b], XMCH[b], AF.Square, inc="ac")
        ac_sq = S.ctr["ac"]
        sq5_ac[b] = ac_sq
        S.wait("pool", "dv", xm_dv[b])
        xm_spilled[b] = dma("pool", xm_spill[oc], XMCH[b], f"sxm{b}")
        S.wait("pe", "dv", xm_dv[b])
        S.wait("pe", "ac", ac_sq)
        for ti, (s, l) in enumerate(tl5):
            mm(pa[ROWS3[ti]:ROWS3[ti] + 1, 0:l], ONESB, XMBF[b][:, s:s + l],
               start=(oc == 0), stop=(oc == C - 1))
        for ti, (s, l) in enumerate(tl5):
            mm(pb[ROWS3[ti]:ROWS3[ti] + 1, 0:l], ONESB, SQ5[b][:, s:s + l],
               start=(oc == 0), stop=(oc == C - 1),
               inc=("pe" if ti == len(tl5) - 1 else None))
        xm_stats_pe[b] = S.ctr["pe"]
    a2_dv = stats_finish(pa, pb, 1025, SL_A, SL_B, S.ctr["pe"])
    broadcast(stv(SL_A, 1025), BC_A[:, 0:1025], 1025, a2_dv, PS[7])
    bc_ac = broadcast(stv(SL_B, 1025), BC_B[:, 0:1025], 1025, a2_dv, PS[7])

    S.barrier()
    CMK = sba("CMK", [128, C * T], BF16, D0)
    CMR = [sba(f"CMR{i}", [128, T], BF16, D0 + 32896 + i * 2048) for i in range(2)]
    SC_H2b = sba("SC_H2b", [128, 1026], F32, D0 + 73856)
    SC_H2c = sba("SC_H2c", [128, 1026], F32, D0 + 78080)
    XCHb3 = sba("XCHb3", [128, 1026], F32, D0 + 82304)
    CMR.append(sba("CMR2", [128, T], BF16, D0 + 36992))
    SC_HX = [SC_H, SC_H2b, SC_H2c]
    XCH6 = [XCH[0], XCH[1], XCHb3]
    hx_free = [0, 0, 0]
    S.wait("sp", "dv", max(xch_free))
    xch_free = [0, 0, 0]
    cmr_spilled = [None, None, None]
    for fc in range(C):
        b = fc % 3
        S.wait("sp", "pl", xch_free[b])
        xd_tok = dma("sp", XCH6[b][:, 0:1025], xm_spill[fc], f"xch{b}")
        hx = SC_HX[b]
        S.waitt("pool", xd_tok)
        S.wait("pool", "ac", bc_ac)
        S.wait("pool", "dv", hx_free[b])
        pool("tensor_mul", hx[:, 0:1025], XCH6[b][:, 0:1025], BC_A[:, 0:1025])
        pool("tensor_add", hx[:, 0:1025], hx[:, 0:1025], BC_B[:, 0:1025], inc="pl")
        xch_free[b] = S.ctr["pl"]
        S.wait("dve", "pl", S.ctr["pl"])
        dve("tensor_scalar", hx[:, 0:1025], hx[:, 0:1025], vcol(V_G2, fc),
            vcol(V_B2, fc), ALU.mult, ALU.add)
        dve("tensor_mul", hx[:, 0:511], hx[:, 0:511], M2[:, 1:512])
        dve("tensor_sub", SC_D[:, 0:T], hx[:, 1:1025], hx[:, 0:T])
        dve("scalar_tensor_tensor", CMK[:, fc * T:(fc + 1) * T], SC_D[:, 0:T],
            vcol(V_CMK, fc), hx[:, 0:T], ALU.mult, ALU.add)
        S.waitt("dve", cmr_spilled[b])
        dve("scalar_tensor_tensor", CMR[b], SC_D[:, 0:T],
            vcol(V_CMR, fc), hx[:, 0:T], ALU.mult, ALU.add, inc="dv")
        hx_free[b] = S.ctr["dv"]
        S.wait("pool", "dv", S.ctr["dv"])
        cmr_spilled[b] = dma("pool", cmr_spill[fc], CMR[b], f"scm{b}")
    cmk_done_dv = S.ctr["dv"]
    if "d_cmk" in dbg_out:
        S.wait("sp", "dv", cmk_done_dv)
        for fc in range(C):
            dma("sp", dbg_out["d_cmk"][fc], CMK[:, fc * T:(fc + 1) * T], "dbg")

    S.barrier()
    SILU = sba("SILU", [128, IC * T], BF16, D0 + 32896)
    WK3 = [sba(f"WK3{i}", [128, 2048], BF16, D0 + 163968 + i * 4096) for i in range(2)]
    PSA = [PS[0], PS[1]]
    PSB = [PS[2], PS[3]]
    wkey_wd = {}
    wk_pe_free = [0, 0]
    silu_ac = [0, 0]
    for i in range(2):
        wkey_wd[i] = dma("pool", WK3[i], wkey_h[i], f"wy{i}")
    for oi in range(IC):
        S.waitt("pe", wkey_wd[oi])
        S.wait("pe", "dv", cmk_done_dv)
        ps = (PSA, PSB)[oi % 2]
        S.wait("pe", "ac", silu_ac[oi % 2])
        for tt in range(2):
            for kc in range(C):
                mm(ps[tt][:, 0:512], WK3[oi % 2][:, kc * 128:(kc + 1) * 128],
                   CMK[:, kc * T + tt * 512: kc * T + tt * 512 + 512],
                   start=(kc == 0), stop=(kc == C - 1),
                   inc=("pe" if kc == C - 1 else None))
        pe_kk = S.ctr["pe"]
        wk_pe_free[oi % 2] = pe_kk
        if oi + 2 < IC:
            S.wait("pool", "pe", pe_kk)
            wkey_wd[oi + 2] = dma("pool", WK3[oi % 2], wkey_h[oi + 2], f"wy{oi % 2}")
        S.wait("act", "pe", pe_kk)
        act(SILU[:, oi * T: oi * T + 512], ps[0][:, 0:512], AF.Silu)
        silu_ac[oi % 2] = act(SILU[:, oi * T + 512: oi * T + 1024], ps[1][:, 0:512],
                              AF.Silu, inc="ac")
    silu_done_ac = S.ctr["ac"]
    if "d_silu" in dbg_out:
        S.wait("sp", "ac", silu_done_ac)
        for oi in range(IC):
            dma("sp", dbg_out["d_silu"][oi], SILU[:, oi * T:(oi + 1) * T], "dbg")

    S.barrier()
    WV2 = [sba(f"WV2{i}", [128, 2048], BF16, D0 + 163968 + i * 4096) for i in range(2)]
    PKV = [[PS[0], PS[1]], [PS[2], PS[3]]]
    KVCH = [SC_H[:, 0:T], SC_D[:, 0:T]]
    CMRR = sba("CMRR", [128, C * T], BF16, D0)
    cmr_re_xd = 0
    for fc in range(C):
        cmr_re_xd = dma("sp", CMRR[:, fc * T:(fc + 1) * T], cmr_spill[fc], "cmrr")
    wval_wd = {}
    wv_pe_free = [0, 0]
    kv_evac_ac = [0, 0]
    kv_spilled = [None, None]
    for i in range(2):
        wval_wd[i] = dma("pool", WV2[i], wval_h[i], f"wv{i}")
    for oc2 in range(C):
        b = oc2 % 2
        S.wait("pe", "ac", kv_evac_ac[b])
        for kcg in range(4):
            idx = oc2 * 4 + kcg
            S.waitt("pe", wval_wd[idx])
            S.wait("pe", "ac", silu_done_ac)
            for kc in range(C):
                ic = kcg * 16 + kc
                for tt in range(2):
                    mm(PKV[b][tt][:, 0:512], WV2[idx % 2][:, kc * 128:(kc + 1) * 128],
                       SILU[:, ic * T + tt * 512: ic * T + tt * 512 + 512],
                       start=(kcg == 0 and kc == 0), stop=(kcg == 3 and kc == C - 1),
                       inc=("pe" if (kc == C - 1 and tt == 1) else None))
            wv_pe_free[idx % 2] = S.ctr["pe"]
            if idx + 2 < IC:
                S.wait("pool", "pe", wv_pe_free[idx % 2])
                wval_wd[idx + 2] = dma("pool", WV2[idx % 2], wval_h[idx + 2], f"wv{idx % 2}")
        pe_kv = S.ctr["pe"]
        S.wait("act", "pe", pe_kv)
        S.waitt("act", kv_spilled[b])
        act(KVCH[b][:, 0:512], PKV[b][0][:, 0:512], AF.Identity)
        kv_evac_ac[b] = act(KVCH[b][:, 512:1024], PKV[b][1][:, 0:512], AF.Identity,
                            inc="ac")
        S.wait("sp", "ac", kv_evac_ac[b])
        kv_spilled[b] = dma("sp", kv_spill[oc2], KVCH[b], f"skv{b}")

    S.barrier()
    WC2 = [sba(f"WC2{i}", [128, 2048], BF16, D0 + 32896 + i * 4096) for i in range(2)]
    PCR = [[PS[0], PS[1]], [PS[2], PS[3]]]
    pa, pb = PS[4], PS[5]
    XCH = [sba(f"XCHc{i}", [128, T], F32, D0 + 41088 + i * 4096) for i in range(2)]
    XM2 = [sba(f"XM2{i}", [128, T], F32, D0 + 49280 + i * 4096) for i in range(2)]
    KV2 = [sba(f"KV2{i}", [128, T], F32, D0 + 57472 + i * 4096) for i in range(2)]
    X1CH = [sba(f"X1CH{i}", [128, T], F32, D0 + 65664 + i * 4096) for i in range(2)]
    X1BF = [sba(f"X1BF{i}", [128, T], BF16, D0 + 73856 + i * 2048) for i in range(2)]
    SQ8 = [sba(f"SQ8{i}", [128, T], BF16, D0 + 77952 + i * 2048) for i in range(2)]
    CH0 = SC_H[:, 0:512]
    CH1 = SC_H[:, 513:1025]
    wcr_wd = {}
    wcr_wd[0] = dma("pool", WC2[0], wcr_h[0], "wc0")
    wcr_wd[1] = dma("pool", WC2[1], wcr_h[1], "wc1")
    wc_pe_free = [0, 0]
    sq8_ac = [0, 0]
    last_asm_dv = 0
    x1_dv = [0, 0]
    x1_spilled = [None, None]
    x1_stats_pe = [0, 0]
    tl4 = _tiles(T)
    stats_ready = {}

    def emit_x1_stats(j):
        bj = j % 2
        dvv, acv = stats_ready[j]
        S.wait("pe", "dv", dvv)
        S.wait("pe", "ac", acv)
        for ti, (s, l) in enumerate(tl4):
            mm(pa[ROWS3[ti]:ROWS3[ti] + 1, 0:l], ONESB, X1BF[bj][:, s:s + l],
               start=(j == 0), stop=(j == C - 1),
               inc=("pe" if ti == len(tl4) - 1 else None))
        for ti, (s, l) in enumerate(tl4):
            mm(pb[ROWS3[ti]:ROWS3[ti] + 1, 0:l], ONESB, SQ8[bj][:, s:s + l],
               start=(j == 0), stop=(j == C - 1),
               inc=("pe" if ti == len(tl4) - 1 else None))
        x1_stats_pe[bj] = S.ctr["pe"]
    def p8b_loads(j):
        bj = j % 2
        S.wait("sp", "dv", x1_dv[bj])
        t1 = dma("sp", XCH[bj], x_img[:, j * 1026 + 2:(j + 1) * 1026], f"xch{bj}")
        t2 = dma("sp", XM2[bj], xm_spill[j][:, 1:1025], f"xm2{bj}")
        t3 = dma("sp", KV2[bj], kv_spill[j], f"kv2{bj}")
        return (t1, t2, t3)

    xd_loads_of = {0: p8b_loads(0), 1: p8b_loads(1)}
    for oc2 in range(C):
        b = oc2 % 2
        xd_loads = xd_loads_of[oc2]
        S.waitt("pe", wcr_wd[oc2])
        S.waitt("pe", cmr_re_xd)
        S.wait("pe", "dv", x1_dv[b])
        for tt in range(2):
            for kc in range(C):
                mm(PCR[b][tt][:, 0:512], WC2[b][:, kc * 128:(kc + 1) * 128],
                   CMRR[:, kc * T + tt * 512: kc * T + tt * 512 + 512],
                   start=(kc == 0), stop=(kc == C - 1),
                   inc=("pe" if (kc == C - 1 and tt == 1) else None))
        pe_cr = S.ctr["pe"]
        wc_pe_free[b] = pe_cr
        if oc2 + 2 < C:
            S.wait("pool", "pe", pe_cr)
            wcr_wd[oc2 + 2] = dma("pool", WC2[b], wcr_h[oc2 + 2], f"wc{b}")
        for tok_ in xd_loads:
            S.waitt("dve", tok_)
        S.waitt("dve", x1_spilled[b])
        S.wait("dve", "ac", sq8_ac[b])
        S.wait("dve", "pe", pe_cr)
        S.wait("act", "pe", pe_cr)
        for tt in range(2):
            sl = slice(tt * 512, tt * 512 + 512)
            S.wait("act", "dv", last_asm_dv)
            act(CH0, PCR[b][tt][:, 0:512], AF.Sigmoid, inc="ac")
            S.wait("dve", "ac", S.ctr["ac"])
            dve("tensor_add", X1CH[b][:, sl], XCH[b][:, sl], XM2[b][:, sl])
            dve("tensor_mul", CH1, CH0, KV2[b][:, sl])
            dve("tensor_add", X1CH[b][:, sl], X1CH[b][:, sl], CH1, inc="dv")
            last_asm_dv = S.ctr["dv"]
        S.wait("dve", "pe", x1_stats_pe[b])
        dve("tensor_copy", X1BF[b], X1CH[b], inc="dv")
        x1_dv[b] = S.ctr["dv"]
        S.wait("act", "dv", x1_dv[b])
        S.wait("act", "pe", x1_stats_pe[b])
        act(SQ8[b], X1CH[b], AF.Square, inc="ac")
        ac_sq = S.ctr["ac"]
        sq8_ac[b] = ac_sq
        S.wait("sp", "dv", x1_dv[b])
        x1_spilled[b] = dma("sp", x1_spill[oc2], X1CH[b], f"sx1{b}")
        if oc2 + 2 < C:
            xd_loads_of[oc2 + 2] = p8b_loads(oc2 + 2)
        stats_ready[oc2] = (x1_dv[b], ac_sq)
        if oc2 > 0:
            emit_x1_stats(oc2 - 1)
    emit_x1_stats(C - 1)
    a3_dv = stats_finish(pa, pb, T, SL_A, SL_B, S.ctr["pe"])

    S.barrier()
    H3 = sba("H3", [128, C * T], BF16, D0)
    BC_A = sba("BC_A3", [128, T], F32, D0 + 32896)
    BC_B = sba("BC_B3", [128, T], F32, D0 + 36992)
    broadcast(stv(SL_A, T), BC_A, T, a3_dv, PS[0])
    bc_ac = broadcast(stv(SL_B, T), BC_B, T, a3_dv, PS[0])
    XCH = [sba(f"XCHd{i}", [128, T], F32, D0 + 41088 + i * 4224) for i in range(4)]
    HXD = SC_H
    HXP = SC_D
    xch_free = [0, 0, 0, 0]
    for fc in range(C):
        b = fc % 4
        eng, hx = (dve, HXD) if fc % 2 == 0 else (pool, HXP)
        sem = "dv" if fc % 2 == 0 else "pl"
        ename = "dve" if fc % 2 == 0 else "pool"
        S.wait("sp", sem, xch_free[b])
        xd_tok = dma("sp", XCH[b], x1_spill[fc], f"xch{b}")
        S.waitt(ename, xd_tok)
        S.wait(ename, "ac", bc_ac)
        eng("tensor_mul", hx[:, 0:T], XCH[b], BC_A)
        eng("tensor_add", H3[:, fc * T:(fc + 1) * T], hx[:, 0:T], BC_B, inc=sem)
        xch_free[b] = S.ctr[sem]
    h3_dv = S.ctr["dv"]
    h3_pl = S.ctr["pl"]
    if "d_h3" in dbg_out:
        S.wait("sp", "dv", h3_dv)
        for fc in range(C):
            dma("sp", dbg_out["d_h3"][fc], H3[:, fc * T:(fc + 1) * T], "dbg")

    S.barrier()
    MID = sba("MID", [128, IC * T], BF16, D0 + 32896)
    WG = [sba(f"WG{i}", [128, 2048], BF16, D0 + 163968 + i * 4096) for i in range(2)]
    PG = [[PS[0], PS[1]], [PS[2], PS[3]]]
    PU = [[PS[4], PS[5]], [PS[6], PS[7]]]
    GS = [SC_H[:, 0:512], SC_H[:, 513:1025]]
    wgu_wd = {}
    piece_pe_free = [0, 0]
    mid_dv = [0, 0]
    gs_free = [0, 0]
    wgu_wd[0] = dma("pool", WG[0], wgu_h[0][:, 0:2048], "wg0")
    wgu_wd[1] = dma("pool", WG[1], wgu_h[0][:, 2048:4096], "wg1")
    for oi in range(IC):
        b = oi % 2
        S.wait("pe", "pl", h3_pl)
        S.wait("pe", "dv", h3_dv)
        S.wait("pe", "dv", mid_dv[b])
        grp_pe = []
        for part in range(2):
            ip = oi * 2 + part
            S.waitt("pe", wgu_wd[ip])
            ps_ = PG[b] if part == 0 else PU[b]
            for tt in range(2):
                for kc in range(C):
                    mm(ps_[tt][:, 0:512], WG[ip % 2][:, kc * 128:(kc + 1) * 128],
                       H3[:, kc * T + tt * 512: kc * T + tt * 512 + 512],
                       start=(kc == 0), stop=(kc == C - 1),
                       inc=("pe" if kc == C - 1 else None))
                grp_pe.append(S.ctr["pe"])
            piece_pe_free[ip % 2] = S.ctr["pe"]
            if ip + 2 < IC * 2:
                nip = ip + 2
                S.wait("pool", "pe", piece_pe_free[ip % 2])
                srcp = wgu_h[nip // 2][:, (nip % 2) * 2048:(nip % 2) * 2048 + 2048]
                wgu_wd[nip] = dma("pool", WG[nip % 2], srcp, f"wg{nip % 2}")
        for tt in range(2):
            S.wait("act", "pe", grp_pe[tt])
            S.wait("act", "dv", gs_free[tt])
            act(GS[tt], PG[b][tt][:, 0:512], AF.Silu, inc="ac")
            S.wait("dve", "ac", S.ctr["ac"])
            S.wait("dve", "pe", grp_pe[2 + tt])
            dve("tensor_mul", MID[:, oi * T + tt * 512: oi * T + tt * 512 + 512],
                GS[tt], PU[b][tt][:, 0:512], inc="dv")
            gs_free[tt] = S.ctr["dv"]
        mid_dv[b] = S.ctr["dv"]
    mid_done_dv = S.ctr["dv"]
    if "d_mid" in dbg_out:
        S.wait("sp", "dv", mid_done_dv)
        for oi in range(IC):
            dma("sp", dbg_out["d_mid"][oi], MID[:, oi * T:(oi + 1) * T], "dbg")

    S.barrier()
    WD2 = [sba("WD20", [128, 2048], BF16, D0 + 163968),
           sba("WD21", [128, 2048], BF16, D0 + 168064),
           sba("WD22", [128, 2048], BF16, D0)]
    PD = [[PS[0], PS[1]], [PS[2], PS[3]]]
    XCH = [sba(f"XCHe{i}", [128, T], F32, D0 + 4224 + i * 4096) for i in range(2)]
    OCH = [sba(f"OCH{i}", [128, T], F32, D0 + 12544 + i * 4096) for i in range(2)]
    wdn_wd = {}
    wd_pe_free = [0, 0, 0]
    out_dv = [0, 0]
    out_od = [None, None]
    for i in range(3):
        wdn_wd[i] = dma("pool", WD2[i], wdn_h[i], f"wn{i}")
    for oc2 in range(C):
        b = oc2 % 2
        S.wait("sp", "dv", out_dv[b])
        xd_tok = dma("sp", XCH[b], x1_spill[oc2], f"xch{b}")
        S.wait("pe", "dv", out_dv[b])
        for kcg in range(4):
            idx = oc2 * 4 + kcg
            S.waitt("pe", wdn_wd[idx])
            S.wait("pe", "dv", mid_done_dv)
            for kc in range(C):
                ic = kcg * 16 + kc
                for tt in range(2):
                    mm(PD[b][tt][:, 0:512], WD2[idx % 3][:, kc * 128:(kc + 1) * 128],
                       MID[:, ic * T + tt * 512: ic * T + tt * 512 + 512],
                       start=(kcg == 0 and kc == 0), stop=(kcg == 3 and kc == C - 1),
                       inc=("pe" if (kc == C - 1 and tt == 1) else None))
            wd_pe_free[idx % 3] = S.ctr["pe"]
            if idx + 3 < IC:
                S.wait("pool", "pe", wd_pe_free[idx % 3])
                wdn_wd[idx + 3] = dma("pool", WD2[idx % 3], wdn_h[idx + 3], f"wn{idx % 3}")
        pe_dn = S.ctr["pe"]
        S.waitt("dve", xd_tok)
        S.wait("dve", "pe", pe_dn)
        S.waitt("dve", out_od[b])
        dve("tensor_add", OCH[b][:, 0:512], XCH[b][:, 0:512], PD[b][0][:, 0:512])
        dve("tensor_add", OCH[b][:, 512:1024], XCH[b][:, 512:1024], PD[b][1][:, 0:512],
            inc="dv")
        out_dv[b] = S.ctr["dv"]
        S.wait("sp", "dv", out_dv[b])
        out_od[b] = dma("sp", out_h[oc2], OCH[b], f"od{b}")
    S.wait("sp", "od0", S.ctr.get("od0", 0))
    S.wait("sp", "od1", S.ctr.get("od1", 0))

    with contextlib.ExitStack() as st:
        for name in list(S.sems):
            S.sems[name] = st.enter_context(nc.semaphore(name))
        with nc.Block() as block:
            S.replay(block)
    return nc


def _prep(inputs):
    f32 = np.float32
    _ = inputs.get("L0_attn_tm_decay")
    x = np.asarray(inputs["L0_attn_x"], f32)[0]
    Wk = np.asarray(inputs["L0_attn_Wk"], f32)
    Wv = np.asarray(inputs["L0_attn_Wv"], f32)
    Wr = np.asarray(inputs["L0_attn_Wr"], f32)
    Wo = np.asarray(inputs["L0_attn_Wo"], f32)
    Wkey = np.asarray(inputs["L0_attn_Wkey"], f32)
    Wval = np.asarray(inputs["L0_attn_Wval"], f32)
    Wcr = np.asarray(inputs["L0_attn_Wcr"], f32)
    Wgu = np.asarray(inputs["L0_ffn_w_gate_up"], f32)
    Wdn = np.asarray(inputs["L0_ffn_w_down"], f32)

    A = np.stack([Wk, Wv, Wr])
    wkvr = np.ascontiguousarray(
        A.reshape(3, 16, 128, 16, 128).transpose(1, 4, 0, 3, 2).reshape(16, 128, 3 * 2048))
    wo = np.ascontiguousarray(
        Wo.reshape(16, 128, 16, 128).transpose(0, 3, 2, 1).reshape(16, 128, 2048))
    wkey = np.ascontiguousarray(
        Wkey.reshape(64, 128, 16, 128).transpose(0, 3, 2, 1).reshape(64, 128, 2048))
    wval = np.ascontiguousarray(
        Wval.reshape(16, 128, 4, 16, 128).transpose(0, 2, 4, 3, 1).reshape(64, 128, 2048))
    G = Wgu[:8192].reshape(64, 128, 16, 128)
    Uu = Wgu[8192:].reshape(64, 128, 16, 128)
    wgu = np.ascontiguousarray(
        np.stack([G, Uu]).transpose(1, 4, 0, 3, 2).reshape(64, 128, 4096))
    wdn = np.ascontiguousarray(
        Wdn.reshape(16, 128, 4, 16, 128).transpose(0, 2, 4, 3, 1).reshape(64, 128, 2048))
    wcr = np.ascontiguousarray(
        Wcr.reshape(16, 128, 16, 128).transpose(0, 3, 2, 1).reshape(16, 128, 2048))

    vec_list = [
        inputs["L0_attn_ln1_g"], inputs["L0_attn_ln1_b"],
        inputs["L0_attn_ln2_g"], inputs["L0_attn_ln2_b"],
        inputs["L0_attn_tm_first"],
        np.asarray(inputs["L0_attn_tm_mk"]).reshape(-1),
        np.asarray(inputs["L0_attn_tm_mv"]).reshape(-1),
        np.asarray(inputs["L0_attn_tm_mr"]).reshape(-1),
        np.asarray(inputs["L0_attn_cm_mk"]).reshape(-1),
        np.asarray(inputs["L0_attn_cm_mr"]).reshape(-1),
    ]
    V = np.stack([np.asarray(v, f32).reshape(2048) for v in vec_list])
    vecs = np.zeros((128, NVEC), f32)
    vecs[:, :160] = V.reshape(10, 16, 128).transpose(2, 0, 1).reshape(128, 160)
    vecs[:, V_EPS] = EPS
    vecs[:, V_NHALF] = -0.5
    vecs[:, V_NONE] = -1.0
    vecs[:, V_HALF] = 0.5
    vecs[:, V_NEG] = NEG
    vecs = np.ascontiguousarray(vecs)

    x_fm = np.ascontiguousarray(x.T)
    in_maps = []
    for ci in range(NCORES):
        s = ci * T
        sl = np.zeros((2048, 1026), f32)
        if s - 2 < 0:
            sl[:, 2:] = x_fm[:, s:s + T]
        else:
            sl[:, :] = x_fm[:, s - 2:s + T]
        ximg = np.ascontiguousarray(
            sl.reshape(16, 128, 1026).transpose(1, 0, 2).reshape(128, 16 * 1026))
        hm = np.full((128, 1), 0.0 if ci == 0 else 1.0, f32)
        in_maps.append(dict(
            x_img=ximg, wkvr=wkvr, wo=wo, wkey=wkey, wval=wval,
            wgu=wgu, wdn=wdn, wcr=wcr, vecs=vecs, hmask=hm))
    return in_maps


def kernel(**inputs):
    in_maps = _prep(inputs)
    nc = _build()
    res = run_bass_kernel_spmd(nc, in_maps, core_ids=list(range(NCORES)))
    parts = []
    for ci in range(NCORES):
        o = np.asarray(res.results[ci]["out"], np.float32)
        parts.append(o.reshape(2048, T))
    full = np.concatenate(parts, axis=1)
    return np.ascontiguousarray(full.T).reshape(1, 8192, 2048).astype(np.float32)


# version 10
# speedup vs baseline: 1.0964x; 1.0345x over previous
import contextlib
import numpy as np
import concourse.bass as bass
import concourse.mybir as mybir
from concourse.bass_utils import run_bass_kernel_spmd

F32 = mybir.dt.float32
BF16 = mybir.dt.bfloat16
AF = mybir.ActivationFunctionType
ALU = mybir.AluOpType

NCORES = 8
T = 1024
C = 16
IC = 64
H = 2048
EPS = 1e-5
NEG = -1e38

V_G1, V_B1, V_G2, V_B2, V_TMF, V_MK, V_MV, V_MR, V_CMK, V_CMR = (
    0, 16, 32, 48, 64, 80, 96, 112, 128, 144)
V_EPS, V_NHALF, V_NONE, V_HALF, V_NEG = 160, 161, 162, 163, 164
NVEC = 168

ROWS3 = [0, 32, 64]


def _tiles(n, cap=512):
    out, s = [], 0
    while s < n:
        l = min(cap, n - s)
        out.append((s, l))
        s += l
    return out


class Sched:
    ENGS = ("pe", "dve", "act", "sp", "pool")

    def __init__(self):
        self.items = {e: [] for e in self.ENGS}
        self.sems = {}
        self.ctr = {}
        self.hiw = {}

    def sem(self, name):
        if name not in self.sems:
            self.sems[name] = None
            self.ctr[name] = 0

    def emit(self, eng, fn, inc=None, incn=1):
        if inc is not None:
            self.ctr[inc] += incn
        self.items[eng].append((fn, inc, incn))
        return self.ctr[inc] if inc is not None else None

    def wait(self, eng, sem, val):
        if val is None or val <= 0:
            return
        key = (eng, sem)
        if self.hiw.get(key, 0) >= val:
            return
        self.hiw[key] = val

        def f(e, sem=sem, val=val):
            e.wait_ge(self.sems[sem], val)
            return None
        self.items[eng].append((f, None, 1))

    def barrier(self, engs=None):
        vals = [(s, self.ctr[s]) for s in self.ctr]
        for eng in (engs or self.ENGS):
            for s, v in vals:
                self.wait(eng, s, v)

    def waitt(self, eng, tok):
        if tok:
            self.wait(eng, tok[0], tok[1])

    def replay(self, block):
        def mk(name):
            def run(e):
                for fn, inc, incn in self.items[name]:
                    inst = fn(e)
                    if inc is not None:
                        inst.then_inc(self.sems[inc], incn)
            return run
        block.gpsimd(mk("pool"))
        block.sync(mk("sp"))
        block.tensor(mk("pe"))
        block.vector(mk("dve"))
        block.scalar(mk("act"))


def _build(dbg=()):
    nc = bass.Bass("TRN2", target_bir_lowering=False)
    dbg = set(dbg)

    x_img = nc.dram_tensor("x_img", [128, C * 1026], F32, kind="ExternalInput").ap()
    wkvr_h = nc.dram_tensor("wkvr", [C, 128, 3 * 2048], F32, kind="ExternalInput").ap()
    wo_h = nc.dram_tensor("wo", [C, 128, 2048], F32, kind="ExternalInput").ap()
    wkey_h = nc.dram_tensor("wkey", [IC, 128, 2048], F32, kind="ExternalInput").ap()
    wval_h = nc.dram_tensor("wval", [IC, 128, 2048], F32, kind="ExternalInput").ap()
    wgu_h = nc.dram_tensor("wgu", [IC, 128, 4096], F32, kind="ExternalInput").ap()
    wdn_h = nc.dram_tensor("wdn", [IC, 128, 2048], F32, kind="ExternalInput").ap()
    wcr_h = nc.dram_tensor("wcr", [C, 128, 2048], F32, kind="ExternalInput").ap()
    vecs_h = nc.dram_tensor("vecs", [128, NVEC], F32, kind="ExternalInput").ap()
    hmask_h = nc.dram_tensor("hmask", [128, 1], F32, kind="ExternalInput").ap()
    out_h = nc.dram_tensor("out", [C, 128, T], F32, kind="ExternalOutput").ap()

    def scratch(name, shape, dt):
        kind = "ExternalOutput" if name in dbg else "Internal"
        return nc.dram_tensor(name, shape, dt, kind=kind).ap()

    xm_spill = scratch("xm_sp", [C, 128, 1025], F32)
    cmr_spill = scratch("cmr_sp", [C, 128, T], BF16)
    x1_spill = scratch("x1_sp", [C, 128, T], F32)
    kv_spill = scratch("kv_sp", [C, 128, T], F32)

    dbg_out = {}
    for nm, shape, dt in [
        ("d_bca", [128, 1026], F32), ("d_bcb", [128, 1026], F32),
        ("d_xk", [C, 128, 1025], BF16), ("d_xv", [C, 128, 1025], BF16),
        ("d_xr", [C, 128, 1025], BF16), ("d_u", [C, 128, 1025], BF16),
        ("d_cmk", [C, 128, T], BF16), ("d_h3", [C, 128, T], BF16),
        ("d_silu", [IC, 128, T], BF16), ("d_mid", [IC, 128, T], BF16),
        ("d_k", [C, 128, T], F32), ("d_v", [C, 128, T], F32),
        ("d_r", [C, 128, T], F32), ("d_stv", [6, 1, 1026], F32),
        ("d_hkvr", [128, 192], F32),
    ]:
        if nm in dbg:
            dbg_out[nm] = nc.dram_tensor(nm, shape, dt, kind="ExternalOutput").ap()

    S = Sched()
    for s in ("pe", "dv", "ac", "pl"):
        S.sem(s)

    BASE = (nc.sbuf_base + 127) // 128 * 128
    TOP = nc.sbuf_base + nc.sbuf_bytes_remaining

    def sba(name, shape, dtype, off):
        a = BASE + off
        nb = int(np.prod(shape[1:])) * (2 if dtype == BF16 else 4)
        assert a + nb <= TOP, (name, a + nb, TOP)
        return nc.alloc_sbuf_tensor_at(name, shape, dtype, offset=a).ap()

    VEC = sba("VEC", [128, NVEC], F32, 0)
    HM = sba("HM", [128, 1], F32, 768)
    ONESB = sba("ONESB", [128, 1], BF16, 896)
    ONESC = sba("ONESC", [1, 128], F32, 1024)
    STV = [sba(f"STV{i}", [1, 1026], F32, 1536 + i * 4224) for i in range(6)]
    SL_A0, SL_B0, SL_A, SL_B, SL_T0, SL_T1 = range(6)
    SC_H = sba("SC_H", [128, 1026], F32, 26880)
    SC_D = sba("SC_D", [128, 1026], F32, 31104)
    D0 = 35328

    def stv(i, n=1026):
        return STV[i][0:1, 0:n]

    def vcol(base, fc):
        return VEC[:, base + fc: base + fc + 1]

    PS = [nc.alloc_psum_tensor(f"PSB{i}", [128, 512], F32).ap() for i in range(8)]

    def dma(eng, dst, src, sem):
        S.sem(sem)

        def f(e, dst=dst, src=src):
            return e.dma_start(dst, src)
        return (sem, S.emit(eng, f, inc=sem, incn=16))

    def mm(psum, lhsT, rhs, start, stop, inc=None):
        def f(e, psum=psum, lhsT=lhsT, rhs=rhs, start=start, stop=stop):
            return e.matmul(psum, lhsT, rhs, start=start, stop=stop)
        return S.emit("pe", f, inc=inc)

    def act(out, in_, func, bias=0.0, scale=1.0, inc=None):
        def f(e, out=out, in_=in_, func=func, bias=bias, scale=scale):
            return e.activation(out, in_, func, bias=bias, scale=scale)
        return S.emit("act", f, inc=inc)

    def dve(op, *args, inc=None):
        def f(e, op=op, args=args):
            return getattr(e, op)(*args)
        return S.emit("dve", f, inc=inc)

    def pool(op, *args, inc=None):
        def f(e, op=op, args=args):
            return getattr(e, op)(*args)
        return S.emit("pool", f, inc=inc)

    def stats_finish(pa, pb, ncols, slot_a, slot_b, pe_done):
        tl = _tiles(ncols)
        S.wait("act", "pe", pe_done)
        for ti, (s, l) in enumerate(tl):
            act(stv(SL_T0)[0:1, s:s + l], pa[ROWS3[ti]:ROWS3[ti] + 1, 0:l], AF.Identity)
        for ti, (s, l) in enumerate(tl):
            act(stv(SL_T1)[0:1, s:s + l], pb[ROWS3[ti]:ROWS3[ti] + 1, 0:l], AF.Identity,
                inc=("ac" if ti == len(tl) - 1 else None))
        S.wait("dve", "ac", S.ctr["ac"])
        m = stv(SL_T0, ncols)
        q = stv(SL_T1, ncols)
        a = stv(slot_a, ncols)
        b = stv(slot_b, ncols)
        dve("tensor_scalar", m, m, 1.0 / H, None, ALU.mult)
        dve("tensor_scalar", q, q, 1.0 / H, None, ALU.mult)
        dve("tensor_mul", b, m, m)
        dve("tensor_sub", q, q, b, inc="dv")
        S.wait("act", "dv", S.ctr["dv"])
        act(a, q, AF.Ln, bias=VEC[0:1, V_EPS:V_EPS + 1])
        act(a, a, AF.Exp, scale=VEC[0:1, V_NHALF:V_NHALF + 1], inc="ac")
        S.wait("dve", "ac", S.ctr["ac"])
        dve("scalar_tensor_tensor", b, m, -1.0, a, ALU.mult, ALU.mult, inc="dv")
        return S.ctr["dv"]

    def broadcast(vec_ap, dst, ncols, dv_ready, psb):
        S.wait("pe", "dv", dv_ready)
        last = None
        for (s, l) in _tiles(ncols):
            S.wait("pe", "ac", S.ctr["ac"])
            mm(psb[:, 0:l], ONESC, vec_ap[0:1, s:s + l], start=True, stop=True, inc="pe")
            S.wait("act", "pe", S.ctr["pe"])
            last = act(dst[:, s:s + l], psb[:, 0:l], AF.Identity, inc="ac")
        return last

    M2 = sba("M2", [128, 512], F32, 209920)
    dma("sp", VEC, vecs_h, "ld")
    hm_tok = dma("sp", HM, hmask_h, "ld")
    for e_ in ("act", "dve", "pool"):
        S.wait(e_, "ld", 32)
    dve("memset", ONESB, 1.0)
    dve("memset", ONESC, 1.0, inc="dv")
    dve("memset", M2, 1.0, inc="dv")
    S.wait("dve", "dv", S.ctr["dv"])
    dve("tensor_copy", M2[:, 0:1], HM, inc="dv")
    S.wait("dve", "dv", S.ctr["dv"])
    dve("tensor_copy", M2[:, 1:2], HM, inc="dv")

    XBF = sba("XBF", [128, C * 1026], BF16, D0)
    xbf_tok = dma("pool", XBF, x_img, "xbf")

    SQ = [sba(f"SQ{i}", [128, 1026], BF16, D0 + 33024 + i * 2176) for i in range(2)]
    pa, pb = PS[0], PS[1]
    tl6 = _tiles(1026)
    sq_pe_free = [0, 0]
    for fc in range(C):
        b = fc % 2
        xv = XBF[:, fc * 1026:(fc + 1) * 1026]
        S.waitt("act", xbf_tok)
        S.wait("act", "pe", sq_pe_free[b])
        act(SQ[b], xv, AF.Square, inc="ac")
        sq_ac = S.ctr["ac"]
        S.waitt("pe", xbf_tok)
        S.wait("pe", "ac", sq_ac)
        for ti, (s, l) in enumerate(tl6):
            mm(pa[ROWS3[ti]:ROWS3[ti] + 1, 0:l], ONESB, xv[:, s:s + l],
               start=(fc == 0), stop=(fc == C - 1))
        for ti, (s, l) in enumerate(tl6):
            mm(pb[ROWS3[ti]:ROWS3[ti] + 1, 0:l], ONESB, SQ[b][:, s:s + l],
               start=(fc == 0), stop=(fc == C - 1),
               inc=("pe" if ti == len(tl6) - 1 else None))
        sq_pe_free[b] = S.ctr["pe"]
    a0_dv = stats_finish(pa, pb, 1026, SL_A0, SL_B0, S.ctr["pe"])

    BC_A = sba("BC_A1", [128, 1026], F32, D0 + 140032)
    BC_B = sba("BC_B1", [128, 1026], F32, D0 + 144256)
    broadcast(stv(SL_A0), BC_A, 1026, a0_dv, PS[2])
    bc_ac = broadcast(stv(SL_B0), BC_B, 1026, a0_dv, PS[2])

    XCH = [sba(f"XCHa{i}", [128, 1026], F32, D0 + 131584 + i * 4224) for i in range(2)]
    XABF = [sba(f"XABF{i}", [128, 1026], BF16, D0 + 37376 + i * 2176) for i in range(2)]
    xch_free = [0, 0]
    st_pe_free = [0, 0]
    sch_ac_free = 0
    for fc in range(C):
        b = fc % 2
        S.wait("sp", "dv", xch_free[b])
        xd_tok = dma("sp", XCH[b], x_img[:, fc * 1026:(fc + 1) * 1026], f"xch{b}")
        S.waitt("dve", xd_tok)
        S.wait("dve", "ac", bc_ac)
        S.wait("dve", "pe", st_pe_free[b])
        S.wait("dve", "ac", sch_ac_free)
        dve("tensor_mul", SC_H, XCH[b], BC_A)
        dve("tensor_add", SC_H, SC_H, BC_B)
        dve("tensor_copy", XABF[b], SC_H, inc="dv")
        xch_free[b] = S.ctr["dv"]
        S.wait("act", "dv", S.ctr["dv"])
        S.wait("act", "pe", st_pe_free[b])
        act(SQ[b], SC_H, AF.Square, inc="ac")
        sch_ac_free = S.ctr["ac"]
        S.wait("pe", "dv", S.ctr["dv"])
        S.wait("pe", "ac", S.ctr["ac"])
        for ti, (s, l) in enumerate(tl6):
            mm(pa[ROWS3[ti]:ROWS3[ti] + 1, 0:l], ONESB, XABF[b][:, s:s + l],
               start=(fc == 0), stop=(fc == C - 1))
        for ti, (s, l) in enumerate(tl6):
            mm(pb[ROWS3[ti]:ROWS3[ti] + 1, 0:l], ONESB, SQ[b][:, s:s + l],
               start=(fc == 0), stop=(fc == C - 1),
               inc=("pe" if ti == len(tl6) - 1 else None))
        st_pe_free[b] = S.ctr["pe"]
    a1_dv = stats_finish(pa, pb, 1026, SL_A, SL_B, S.ctr["pe"])
    S.wait("dve", "dv", a1_dv)
    dve("tensor_mul", stv(SL_T0), stv(SL_B0), stv(SL_A))
    dve("tensor_add", stv(SL_T0), stv(SL_T0), stv(SL_B))
    dve("tensor_mul", stv(SL_T1), stv(SL_A0), stv(SL_A), inc="dv")
    a01_dv = S.ctr["dv"]
    broadcast(stv(SL_T1), BC_A, 1026, a01_dv, PS[2])
    bc_ac = broadcast(stv(SL_T0), BC_B, 1026, a01_dv, PS[2])
    if "d_stv" in dbg_out:
        S.wait("sp", "dv", a01_dv)
        for i in range(6):
            dma("sp", dbg_out["d_stv"][i], STV[i][0:1, 0:1026], "dbg")

    XK = sba("XK", [128, C * 1025], BF16, D0)
    XV = sba("XV", [128, C * 1025], BF16, D0 + 32896)
    XR = sba("XR", [128, C * 1025], BF16, D0 + 65792)
    WKVR = [sba(f"WKVR{i}", [128, 3 * 2048], BF16, D0 + 148480 + i * 12288)
            for i in range(2)]
    wkvr_wd = {}
    wkvr_wd[0] = dma("pool", WKVR[0], wkvr_h[0], "wk0")
    wkvr_wd[1] = dma("pool", WKVR[1], wkvr_h[1], "wk1")

    SC_H2 = sba("SC_H2", [128, 1026], F32, D0 + 98688)
    SC_HX = [SC_H, SC_H2]
    hx_free = [0, 0]
    S.wait("sp", "dv", max(xch_free))
    xch_free = [0, 0]
    for fc in range(C):
        b = fc % 2
        S.wait("sp", "pl", xch_free[b])
        xd_tok = dma("sp", XCH[b], x_img[:, fc * 1026:(fc + 1) * 1026], f"xch{b}")
        hx = SC_HX[b]
        S.waitt("pool", xd_tok)
        S.wait("pool", "ac", bc_ac)
        S.wait("pool", "dv", hx_free[b])
        pool("tensor_mul", hx, XCH[b], BC_A)
        pool("tensor_add", hx, hx, BC_B, inc="pl")
        xch_free[b] = S.ctr["pl"]
        S.wait("dve", "pl", S.ctr["pl"])
        dve("tensor_scalar", hx, hx, vcol(V_G1, fc), vcol(V_B1, fc), ALU.mult, ALU.add)
        dve("tensor_mul", hx[:, 0:512], hx[:, 0:512], M2)
        dve("tensor_sub", SC_D[:, 0:1025], hx[:, 1:1026], hx[:, 0:1025])
        o = fc * 1025
        dve("scalar_tensor_tensor", XK[:, o:o + 1025], SC_D[:, 0:1025], vcol(V_MK, fc),
            hx[:, 0:1025], ALU.mult, ALU.add)
        dve("scalar_tensor_tensor", XV[:, o:o + 1025], SC_D[:, 0:1025], vcol(V_MV, fc),
            hx[:, 0:1025], ALU.mult, ALU.add)
        dve("scalar_tensor_tensor", XR[:, o:o + 1025], SC_D[:, 0:1025], vcol(V_MR, fc),
            hx[:, 0:1025], ALU.mult, ALU.add, inc="dv")
        hx_free[b] = S.ctr["dv"]
    mixes_dv = S.ctr["dv"]
    mixes_pl = S.ctr["pl"]
    if "d_xk" in dbg_out:
        S.wait("sp", "dv", mixes_dv)
        for fc in range(C):
            dma("sp", dbg_out["d_xk"][fc], XK[:, fc * 1025:(fc + 1) * 1025], "dbg")
            dma("sp", dbg_out["d_xv"][fc], XV[:, fc * 1025:(fc + 1) * 1025], "dbg")
            dma("sp", dbg_out["d_xr"][fc], XR[:, fc * 1025:(fc + 1) * 1025], "dbg")
    if "d_bca" in dbg_out:
        S.wait("sp", "ac", bc_ac)
        dma("sp", dbg_out["d_bca"], BC_A, "dbg")
        dma("sp", dbg_out["d_bcb"], BC_B, "dbg")

    S.barrier()
    U = sba("U", [128, C * 1025], BF16, D0 + 98688)
    CH = [sba(f"CH{i}", [128, 512], F32, D0 + 131584 + i * 2048) for i in range(8)]
    PSK = [PS[0], PS[1]]
    PSV = [PS[2], PS[3]]
    PSR = [PS[4], PS[5]]
    HK, HV, HR = PS[0][:, 0:1], PS[2][:, 0:1], PS[4][:, 0:1]
    HW = PS[6][:, 0:1]
    HSTX = sba("HSTX", [128, 192], F32, 209152)
    HST = HSTX[:, 0:48]
    SPC = SC_H[:, 0:512]

    chain_dv_set = [0, 0]
    wkvr_pe_free = [0, 0]
    halo_stage_ac = 0
    for oc in range(C):
        wb = WKVR[oc % 2]
        S.waitt("pe", wkvr_wd[oc])
        S.wait("pe", "dv", mixes_dv)
        for tt in range(2):
            S.wait("pe", "dv", chain_dv_set[tt])
            if tt == 0:
                S.wait("pe", "ac", halo_stage_ac)
            tok0 = 1 + tt * 512
            for mi, dst, mixsrc in ((0, PSK, XK), (1, PSV, XV), (2, PSR, XR)):
                for kc in range(C):
                    mm(dst[tt][:, 0:512],
                       wb[:, (mi * 16 + kc) * 128:(mi * 16 + kc + 1) * 128],
                       mixsrc[:, kc * 1025 + tok0: kc * 1025 + tok0 + 512],
                       start=(kc == 0), stop=(kc == C - 1),
                       inc=("pe" if kc == C - 1 else None))
            pe_kvr = S.ctr["pe"]
            c0, c1, c2, c3 = CH[tt * 4: tt * 4 + 4]
            uo = U[:, oc * 1025 + tok0: oc * 1025 + tok0 + 512]
            S.wait("act", "pe", pe_kvr - 2)
            act(c0, PSK[tt][:, 0:512], AF.Identity, bias=vcol(V_TMF, oc), inc="ac")
            S.wait("dve", "ac", S.ctr["ac"])
            dve("tensor_scalar", c1, c0, VEC[:, V_NEG:V_NEG + 1], None, ALU.max)
            dve("tensor_sub", c0, c0, c1, inc="dv")
            S.wait("act", "dv", S.ctr["dv"])
            act(c1, c0, AF.Exp)
            act(c2, c0, AF.Exp, scale=VEC[:, V_NONE:V_NONE + 1])
            S.wait("act", "pe", pe_kvr)
            act(c3, PSR[tt][:, 0:512], AF.Tanh, scale=VEC[:, V_HALF:V_HALF + 1], inc="ac")
            S.wait("dve", "ac", S.ctr["ac"])
            S.wait("dve", "pe", pe_kvr)
            dve("scalar_tensor_tensor", c0, PSV[tt][:, 0:512], 0.5, c1, ALU.mult, ALU.mult)
            dve("tensor_mul", c0, c0, c2)
            dve("scalar_tensor_tensor", uo, c3, 1.0, c0, ALU.add, ALU.mult, inc="dv")
            chain_dv_set[tt] = S.ctr["dv"]
            if "d_k" in dbg_out:
                DBG1, DBG2, DBG3 = SC_H[:, 0:512], SC_H[:, 513:1025], SC_D[:, 514:1026]
                S.wait("act", "dv", chain_dv_set[tt])
                act(DBG1, PSK[tt][:, 0:512], AF.Identity)
                act(DBG2, PSV[tt][:, 0:512], AF.Identity)
                act(DBG3, PSR[tt][:, 0:512], AF.Identity, inc="ac")
                S.wait("sp", "ac", S.ctr["ac"])
                dma("sp", dbg_out["d_k"][oc][:, tt * 512:tt * 512 + 512], DBG1, "dbg")
                dma("sp", dbg_out["d_v"][oc][:, tt * 512:tt * 512 + 512], DBG2, "dbg")
                dv_dump = dma("sp", dbg_out["d_r"][oc][:, tt * 512:tt * 512 + 512], DBG3, "dbg")
                S.waitt("pe", dv_dump)
                S.waitt("dve", dv_dump)
        S.wait("pe", "ac", halo_stage_ac)
        S.wait("pe", "dv", chain_dv_set[0])
        for mi, dsth, mixsrc in ((0, HK, XK), (1, HV, XV), (2, HR, XR)):
            for kc in range(C):
                mm(dsth, wb[:, (mi * 16 + kc) * 128:(mi * 16 + kc + 1) * 128],
                   mixsrc[:, kc * 1025: kc * 1025 + 1],
                   start=(kc == 0), stop=(kc == C - 1),
                   inc=("pe" if (mi == 2 and kc == C - 1) else None))
        pe_halo = S.ctr["pe"]
        wkvr_pe_free[oc % 2] = pe_halo
        if oc + 2 < C:
            S.wait("pool", "pe", pe_halo)
            wkvr_wd[oc + 2] = dma("pool", WKVR[oc % 2], wkvr_h[oc + 2], f"wk{oc % 2}")
        S.wait("act", "pe", pe_halo)
        act(HST[:, oc:oc + 1], HK, AF.Identity)
        act(HST[:, 16 + oc:16 + oc + 1], HV, AF.Identity)
        act(HST[:, 32 + oc:32 + oc + 1], HR, AF.Identity, inc="ac")
        halo_stage_ac = S.ctr["ac"]
    W16 = HSTX[:, 48:64]
    P16 = HSTX[:, 64:80]
    E16 = HSTX[:, 80:96]
    R16 = HSTX[:, 96:112]
    T16 = HSTX[:, 112:128]
    N16 = HSTX[:, 128:144]
    UH16 = HSTX[:, 144:160]
    act(SPC, SPC, AF.Identity, inc="ac")
    S.wait("dve", "ac", S.ctr["ac"])
    dve("tensor_add", W16, HST[:, 0:16], VEC[:, V_TMF:V_TMF + 16])
    dve("tensor_copy", SPC, SPC)
    dve("tensor_scalar", P16, W16, VEC[:, V_NEG:V_NEG + 1], None, ALU.max)
    dve("tensor_copy", SPC, SPC)
    dve("tensor_sub", W16, W16, P16)
    dve("tensor_copy", SPC, SPC, inc="dv")
    S.wait("act", "dv", S.ctr["dv"])
    act(E16, W16, AF.Exp)
    act(R16, W16, AF.Exp, scale=VEC[:, V_NONE:V_NONE + 1])
    act(T16, HST[:, 32:48], AF.Tanh, scale=VEC[:, V_HALF:V_HALF + 1])
    act(SPC, SPC, AF.Identity, inc="ac")
    S.wait("dve", "ac", S.ctr["ac"])
    dve("scalar_tensor_tensor", N16, HST[:, 16:32], 0.5, E16, ALU.mult, ALU.mult)
    dve("tensor_copy", SPC, SPC)
    dve("tensor_mul", N16, N16, R16)
    dve("tensor_copy", SPC, SPC)
    dve("scalar_tensor_tensor", UH16, T16, 1.0, N16, ALU.add, ALU.mult)
    dve("tensor_copy", SPC, SPC)
    for oc in range(C):
        dve("tensor_copy", U[:, oc * 1025: oc * 1025 + 1], UH16[:, oc:oc + 1])
    dve("tensor_copy", SPC, SPC, inc="dv")
    if "d_hkvr" in dbg_out:
        S.wait("sp", "dv", S.ctr["dv"])
        dma("sp", dbg_out["d_hkvr"], HSTX, "dbg")
    u_done_dv = S.ctr["dv"]
    if "d_u" in dbg_out:
        S.wait("sp", "dv", u_done_dv)
        for fc in range(C):
            dma("sp", dbg_out["d_u"][fc], U[:, fc * 1025:(fc + 1) * 1025], "dbg")

    S.barrier()
    BC_A = sba("BC_A2", [128, 1026], F32, D0 + 165376)
    BC_B = sba("BC_B2", [128, 1026], F32, D0 + 169600)
    broadcast(stv(SL_A0), BC_A, 1026, a0_dv, PS[7])
    bc_ac = broadcast(stv(SL_B0), BC_B, 1026, a0_dv, PS[7])

    XCH = [sba(f"XCHb{i}", [128, 1026], F32, D0 + 139776 + i * 4224) for i in range(2)]
    WO = [sba(f"WO{i}", [128, 2048], BF16, D0 + 131584 + i * 4096) for i in range(2)]
    PSW = [[PS[0], PS[1]], [PS[2], PS[3]]]
    XMCH = [sba(f"XMCH{i}", [128, 1025], F32, D0 + 148224 + i * 4224) for i in range(2)]
    XMBF = [sba(f"XMBF{i}", [128, 1025], BF16, D0 + 156672 + i * 2176) for i in range(2)]
    SQ5 = [sba(f"SQ5{i}", [128, 1025], BF16, D0 + 161024 + i * 2176) for i in range(2)]
    pa, pb = PS[4], PS[5]
    tl5 = _tiles(1025)

    wo_wd = {}
    wo_wd[0] = dma("pool", WO[0], wo_h[0], "wwo0")
    wo_wd[1] = dma("pool", WO[1], wo_h[1], "wwo1")
    wo_pe_free = [0, 0]
    xm_halo_dv_last = 0
    xm_dv = [0, 0]
    xm_spilled = [None, None]
    xm_stats_pe = [0, 0]
    xch_free = [0, 0]
    sq5_ac = [0, 0]
    for oc in range(C):
        b = oc % 2
        S.waitt("pe", wo_wd[oc])
        S.wait("pe", "dv", u_done_dv)
        S.wait("pe", "dv", xm_dv[b])
        for tt in range(2):
            tok0 = 1 + tt * 512
            for kc in range(C):
                mm(PSW[b][tt][:, 0:512], WO[b][:, kc * 128:(kc + 1) * 128],
                   U[:, kc * 1025 + tok0: kc * 1025 + tok0 + 512],
                   start=(kc == 0), stop=(kc == C - 1),
                   inc=("pe" if kc == C - 1 else None))
        S.wait("pe", "dv", xm_halo_dv_last)
        for kc in range(C):
            mm(HW, WO[b][:, kc * 128:(kc + 1) * 128],
               U[:, kc * 1025: kc * 1025 + 1],
               start=(kc == 0), stop=(kc == C - 1),
               inc=("pe" if kc == C - 1 else None))
        pe_wo = S.ctr["pe"]
        wo_pe_free[b] = pe_wo
        if oc + 2 < C:
            S.wait("pool", "pe", pe_wo)
            wo_wd[oc + 2] = dma("pool", WO[b], wo_h[oc + 2], f"wwo{b}")
        S.wait("sp", "dv", xch_free[b])
        xd_tok = dma("sp", XCH[b], x_img[:, oc * 1026:(oc + 1) * 1026], f"xch{b}")
        S.waitt("dve", xd_tok)
        S.wait("dve", "ac", bc_ac)
        S.waitt("dve", xm_spilled[b])
        S.wait("dve", "ac", sq5_ac[b])
        dve("tensor_mul", XMCH[b], XCH[b][:, 1:1026], BC_A[:, 1:1026])
        dve("tensor_add", XMCH[b], XMCH[b], BC_B[:, 1:1026], inc="dv")
        xch_free[b] = S.ctr["dv"]
        S.wait("dve", "pe", pe_wo)
        dve("tensor_add", XMCH[b][:, 0:1], XMCH[b][:, 0:1], HW, inc="dv")
        xm_halo_dv_last = S.ctr["dv"]
        dve("tensor_add", XMCH[b][:, 1:513], XMCH[b][:, 1:513], PSW[b][0][:, 0:512])
        dve("tensor_add", XMCH[b][:, 513:1025], XMCH[b][:, 513:1025], PSW[b][1][:, 0:512])
        S.wait("dve", "pe", xm_stats_pe[b])
        dve("tensor_copy", XMBF[b], XMCH[b], inc="dv")
        xm_dv[b] = S.ctr["dv"]
        S.wait("act", "dv", xm_dv[b])
        S.wait("act", "pe", xm_stats_pe[b])
        act(SQ5[b], XMCH[b], AF.Square, inc="ac")
        ac_sq = S.ctr["ac"]
        sq5_ac[b] = ac_sq
        S.wait("act", "dv", xm_dv[b])
        xm_spilled[b] = dma("act", xm_spill[oc], XMCH[b], f"sxm{b}")
        S.wait("pe", "dv", xm_dv[b])
        S.wait("pe", "ac", ac_sq)
        for ti, (s, l) in enumerate(tl5):
            mm(pa[ROWS3[ti]:ROWS3[ti] + 1, 0:l], ONESB, XMBF[b][:, s:s + l],
               start=(oc == 0), stop=(oc == C - 1))
        for ti, (s, l) in enumerate(tl5):
            mm(pb[ROWS3[ti]:ROWS3[ti] + 1, 0:l], ONESB, SQ5[b][:, s:s + l],
               start=(oc == 0), stop=(oc == C - 1),
               inc=("pe" if ti == len(tl5) - 1 else None))
        xm_stats_pe[b] = S.ctr["pe"]
    a2_dv = stats_finish(pa, pb, 1025, SL_A, SL_B, S.ctr["pe"])
    broadcast(stv(SL_A, 1025), BC_A[:, 0:1025], 1025, a2_dv, PS[7])
    bc_ac = broadcast(stv(SL_B, 1025), BC_B[:, 0:1025], 1025, a2_dv, PS[7])

    S.barrier()
    CMK = sba("CMK", [128, C * T], BF16, D0)
    CMR = [sba(f"CMR{i}", [128, T], BF16, D0 + 32896 + i * 2048) for i in range(2)]
    SC_H2b = sba("SC_H2b", [128, 1026], F32, D0 + 73856)
    SC_H2c = sba("SC_H2c", [128, 1026], F32, D0 + 78080)
    XCHb3 = sba("XCHb3", [128, 1026], F32, D0 + 82304)
    CMR.append(sba("CMR2", [128, T], BF16, D0 + 36992))
    SC_HX = [SC_H, SC_H2b, SC_H2c]
    XCH6 = [XCH[0], XCH[1], XCHb3]
    hx_free = [0, 0, 0]
    S.wait("sp", "dv", max(xch_free))
    xch_free = [0, 0, 0]
    cmr_spilled = [None, None, None]
    for fc in range(C):
        b = fc % 3
        S.wait("sp", "pl", xch_free[b])
        xd_tok = dma("sp", XCH6[b][:, 0:1025], xm_spill[fc], f"xch{b}")
        hx = SC_HX[b]
        S.waitt("pool", xd_tok)
        S.wait("pool", "ac", bc_ac)
        S.wait("pool", "dv", hx_free[b])
        pool("tensor_mul", hx[:, 0:1025], XCH6[b][:, 0:1025], BC_A[:, 0:1025])
        pool("tensor_add", hx[:, 0:1025], hx[:, 0:1025], BC_B[:, 0:1025], inc="pl")
        xch_free[b] = S.ctr["pl"]
        S.wait("dve", "pl", S.ctr["pl"])
        dve("tensor_scalar", hx[:, 0:1025], hx[:, 0:1025], vcol(V_G2, fc),
            vcol(V_B2, fc), ALU.mult, ALU.add)
        dve("tensor_mul", hx[:, 0:511], hx[:, 0:511], M2[:, 1:512])
        dve("tensor_sub", SC_D[:, 0:T], hx[:, 1:1025], hx[:, 0:T])
        dve("scalar_tensor_tensor", CMK[:, fc * T:(fc + 1) * T], SC_D[:, 0:T],
            vcol(V_CMK, fc), hx[:, 0:T], ALU.mult, ALU.add)
        S.waitt("dve", cmr_spilled[b])
        dve("scalar_tensor_tensor", CMR[b], SC_D[:, 0:T],
            vcol(V_CMR, fc), hx[:, 0:T], ALU.mult, ALU.add, inc="dv")
        hx_free[b] = S.ctr["dv"]
        S.wait("act", "dv", S.ctr["dv"])
        cmr_spilled[b] = dma("act", cmr_spill[fc], CMR[b], f"scm{b}")
    cmk_done_dv = S.ctr["dv"]
    if "d_cmk" in dbg_out:
        S.wait("sp", "dv", cmk_done_dv)
        for fc in range(C):
            dma("sp", dbg_out["d_cmk"][fc], CMK[:, fc * T:(fc + 1) * T], "dbg")

    S.barrier()
    SILU = sba("SILU", [128, IC * T], BF16, D0 + 32896)
    WK3 = [sba(f"WK3{i}", [128, 2048], BF16, D0 + 163968 + i * 4096) for i in range(2)]
    PSA = [PS[0], PS[1]]
    PSB = [PS[2], PS[3]]
    wkey_wd = {}
    wk_pe_free = [0, 0]
    silu_ac = [0, 0]
    for i in range(2):
        wkey_wd[i] = dma("pool", WK3[i], wkey_h[i], f"wy{i}")
    for oi in range(IC):
        S.waitt("pe", wkey_wd[oi])
        S.wait("pe", "dv", cmk_done_dv)
        ps = (PSA, PSB)[oi % 2]
        S.wait("pe", "ac", silu_ac[oi % 2])
        for tt in range(2):
            for kc in range(C):
                mm(ps[tt][:, 0:512], WK3[oi % 2][:, kc * 128:(kc + 1) * 128],
                   CMK[:, kc * T + tt * 512: kc * T + tt * 512 + 512],
                   start=(kc == 0), stop=(kc == C - 1),
                   inc=("pe" if kc == C - 1 else None))
        pe_kk = S.ctr["pe"]
        wk_pe_free[oi % 2] = pe_kk
        if oi + 2 < IC:
            S.wait("pool", "pe", pe_kk)
            wkey_wd[oi + 2] = dma("pool", WK3[oi % 2], wkey_h[oi + 2], f"wy{oi % 2}")
        S.wait("act", "pe", pe_kk)
        act(SILU[:, oi * T: oi * T + 512], ps[0][:, 0:512], AF.Silu)
        silu_ac[oi % 2] = act(SILU[:, oi * T + 512: oi * T + 1024], ps[1][:, 0:512],
                              AF.Silu, inc="ac")
    silu_done_ac = S.ctr["ac"]
    if "d_silu" in dbg_out:
        S.wait("sp", "ac", silu_done_ac)
        for oi in range(IC):
            dma("sp", dbg_out["d_silu"][oi], SILU[:, oi * T:(oi + 1) * T], "dbg")

    S.barrier()
    WV2 = [sba(f"WV2{i}", [128, 2048], BF16, D0 + 163968 + i * 4096) for i in range(2)]
    PKV = [[PS[0], PS[1]], [PS[2], PS[3]]]
    KVCH = [SC_H[:, 0:T], SC_D[:, 0:T]]
    CMRR = sba("CMRR", [128, C * T], BF16, D0)
    cmr_re_xd = 0
    for fc in range(C):
        cmr_re_xd = dma("sp", CMRR[:, fc * T:(fc + 1) * T], cmr_spill[fc], "cmrr")
    wval_wd = {}
    wv_pe_free = [0, 0]
    kv_evac_ac = [0, 0]
    kv_spilled = [None, None]
    for i in range(2):
        wval_wd[i] = dma("pool", WV2[i], wval_h[i], f"wv{i}")
    for oc2 in range(C):
        b = oc2 % 2
        S.wait("pe", "ac", kv_evac_ac[b])
        for kcg in range(4):
            idx = oc2 * 4 + kcg
            S.waitt("pe", wval_wd[idx])
            S.wait("pe", "ac", silu_done_ac)
            for kc in range(C):
                ic = kcg * 16 + kc
                for tt in range(2):
                    mm(PKV[b][tt][:, 0:512], WV2[idx % 2][:, kc * 128:(kc + 1) * 128],
                       SILU[:, ic * T + tt * 512: ic * T + tt * 512 + 512],
                       start=(kcg == 0 and kc == 0), stop=(kcg == 3 and kc == C - 1),
                       inc=("pe" if (kc == C - 1 and tt == 1) else None))
            wv_pe_free[idx % 2] = S.ctr["pe"]
            if idx + 2 < IC:
                S.wait("pool", "pe", wv_pe_free[idx % 2])
                wval_wd[idx + 2] = dma("pool", WV2[idx % 2], wval_h[idx + 2], f"wv{idx % 2}")
        pe_kv = S.ctr["pe"]
        S.wait("act", "pe", pe_kv)
        S.waitt("act", kv_spilled[b])
        act(KVCH[b][:, 0:512], PKV[b][0][:, 0:512], AF.Identity)
        kv_evac_ac[b] = act(KVCH[b][:, 512:1024], PKV[b][1][:, 0:512], AF.Identity,
                            inc="ac")
        S.wait("sp", "ac", kv_evac_ac[b])
        kv_spilled[b] = dma("sp", kv_spill[oc2], KVCH[b], f"skv{b}")

    S.barrier()
    WC2 = [sba(f"WC2{i}", [128, 2048], BF16, D0 + 32896 + i * 4096) for i in range(2)]
    PCR = [[PS[0], PS[1]], [PS[2], PS[3]]]
    pa, pb = PS[4], PS[5]
    XCH = [sba(f"XCHc{i}", [128, T], F32, D0 + 41088 + i * 4096) for i in range(2)]
    XM2 = [sba(f"XM2{i}", [128, T], F32, D0 + 49280 + i * 4096) for i in range(2)]
    KV2 = [sba(f"KV2{i}", [128, T], F32, D0 + 57472 + i * 4096) for i in range(2)]
    X1CH = [sba(f"X1CH{i}", [128, T], F32, D0 + 65664 + i * 4096) for i in range(2)]
    X1BF = [sba(f"X1BF{i}", [128, T], BF16, D0 + 73856 + i * 2048) for i in range(2)]
    SQ8 = [sba(f"SQ8{i}", [128, T], BF16, D0 + 77952 + i * 2048) for i in range(2)]
    CH0 = SC_H[:, 0:512]
    CH1 = SC_H[:, 513:1025]
    wcr_wd = {}
    wcr_wd[0] = dma("pool", WC2[0], wcr_h[0], "wc0")
    wcr_wd[1] = dma("pool", WC2[1], wcr_h[1], "wc1")
    wc_pe_free = [0, 0]
    sq8_ac = [0, 0]
    last_asm_dv = 0
    x1_dv = [0, 0]
    x1_spilled = [None, None]
    x1_stats_pe = [0, 0]
    tl4 = _tiles(T)
    stats_ready = {}

    def emit_x1_stats(j):
        bj = j % 2
        dvv, acv = stats_ready[j]
        S.wait("pe", "dv", dvv)
        S.wait("pe", "ac", acv)
        for ti, (s, l) in enumerate(tl4):
            mm(pa[ROWS3[ti]:ROWS3[ti] + 1, 0:l], ONESB, X1BF[bj][:, s:s + l],
               start=(j == 0), stop=(j == C - 1),
               inc=("pe" if ti == len(tl4) - 1 else None))
        for ti, (s, l) in enumerate(tl4):
            mm(pb[ROWS3[ti]:ROWS3[ti] + 1, 0:l], ONESB, SQ8[bj][:, s:s + l],
               start=(j == 0), stop=(j == C - 1),
               inc=("pe" if ti == len(tl4) - 1 else None))
        x1_stats_pe[bj] = S.ctr["pe"]
    def p8b_loads(j):
        bj = j % 2
        S.wait("sp", "dv", x1_dv[bj])
        t1 = dma("sp", XCH[bj], x_img[:, j * 1026 + 2:(j + 1) * 1026], f"xch{bj}")
        t2 = dma("sp", XM2[bj], xm_spill[j][:, 1:1025], f"xm2{bj}")
        t3 = dma("sp", KV2[bj], kv_spill[j], f"kv2{bj}")
        return (t1, t2, t3)

    xd_loads_of = {0: p8b_loads(0), 1: p8b_loads(1)}
    for oc2 in range(C):
        b = oc2 % 2
        xd_loads = xd_loads_of[oc2]
        S.waitt("pe", wcr_wd[oc2])
        S.waitt("pe", cmr_re_xd)
        S.wait("pe", "dv", x1_dv[b])
        for tt in range(2):
            for kc in range(C):
                mm(PCR[b][tt][:, 0:512], WC2[b][:, kc * 128:(kc + 1) * 128],
                   CMRR[:, kc * T + tt * 512: kc * T + tt * 512 + 512],
                   start=(kc == 0), stop=(kc == C - 1),
                   inc=("pe" if (kc == C - 1 and tt == 1) else None))
        pe_cr = S.ctr["pe"]
        wc_pe_free[b] = pe_cr
        if oc2 + 2 < C:
            S.wait("pool", "pe", pe_cr)
            wcr_wd[oc2 + 2] = dma("pool", WC2[b], wcr_h[oc2 + 2], f"wc{b}")
        for tok_ in xd_loads:
            S.waitt("dve", tok_)
        S.waitt("dve", x1_spilled[b])
        S.wait("dve", "ac", sq8_ac[b])
        S.wait("dve", "pe", pe_cr)
        S.wait("act", "pe", pe_cr)
        for tt in range(2):
            sl = slice(tt * 512, tt * 512 + 512)
            S.wait("act", "dv", last_asm_dv)
            act(CH0, PCR[b][tt][:, 0:512], AF.Sigmoid, inc="ac")
            S.wait("dve", "ac", S.ctr["ac"])
            dve("tensor_add", X1CH[b][:, sl], XCH[b][:, sl], XM2[b][:, sl])
            dve("tensor_mul", CH1, CH0, KV2[b][:, sl])
            dve("tensor_add", X1CH[b][:, sl], X1CH[b][:, sl], CH1, inc="dv")
            last_asm_dv = S.ctr["dv"]
        S.wait("dve", "pe", x1_stats_pe[b])
        dve("tensor_copy", X1BF[b], X1CH[b], inc="dv")
        x1_dv[b] = S.ctr["dv"]
        S.wait("act", "dv", x1_dv[b])
        S.wait("act", "pe", x1_stats_pe[b])
        act(SQ8[b], X1CH[b], AF.Square, inc="ac")
        ac_sq = S.ctr["ac"]
        sq8_ac[b] = ac_sq
        S.wait("sp", "dv", x1_dv[b])
        x1_spilled[b] = dma("sp", x1_spill[oc2], X1CH[b], f"sx1{b}")
        if oc2 + 2 < C:
            xd_loads_of[oc2 + 2] = p8b_loads(oc2 + 2)
        stats_ready[oc2] = (x1_dv[b], ac_sq)
        if oc2 > 0:
            emit_x1_stats(oc2 - 1)
    emit_x1_stats(C - 1)
    a3_dv = stats_finish(pa, pb, T, SL_A, SL_B, S.ctr["pe"])

    S.barrier()
    H3 = sba("H3", [128, C * T], BF16, D0)
    BC_A = sba("BC_A3", [128, T], F32, D0 + 32896)
    BC_B = sba("BC_B3", [128, T], F32, D0 + 36992)
    broadcast(stv(SL_A, T), BC_A, T, a3_dv, PS[0])
    bc_ac = broadcast(stv(SL_B, T), BC_B, T, a3_dv, PS[0])
    XCH = [sba(f"XCHd{i}", [128, T], F32, D0 + 41088 + i * 4224) for i in range(4)]
    HXD = SC_H
    HXP = SC_D
    xch_free = [0, 0, 0, 0]
    for fc in range(C):
        b = fc % 4
        eng, hx = (dve, HXD) if fc % 2 == 0 else (pool, HXP)
        sem = "dv" if fc % 2 == 0 else "pl"
        ename = "dve" if fc % 2 == 0 else "pool"
        S.wait("sp", sem, xch_free[b])
        xd_tok = dma("sp", XCH[b], x1_spill[fc], f"xch{b}")
        S.waitt(ename, xd_tok)
        S.wait(ename, "ac", bc_ac)
        eng("tensor_mul", hx[:, 0:T], XCH[b], BC_A)
        eng("tensor_add", H3[:, fc * T:(fc + 1) * T], hx[:, 0:T], BC_B, inc=sem)
        xch_free[b] = S.ctr[sem]
    h3_dv = S.ctr["dv"]
    h3_pl = S.ctr["pl"]
    if "d_h3" in dbg_out:
        S.wait("sp", "dv", h3_dv)
        for fc in range(C):
            dma("sp", dbg_out["d_h3"][fc], H3[:, fc * T:(fc + 1) * T], "dbg")

    S.barrier()
    MID = sba("MID", [128, IC * T], BF16, D0 + 32896)
    WG = [sba(f"WG{i}", [128, 2048], BF16, D0 + 163968 + i * 4096) for i in range(2)]
    PG = [[PS[0], PS[1]], [PS[2], PS[3]]]
    PU = [[PS[4], PS[5]], [PS[6], PS[7]]]
    GS = [SC_H[:, 0:512], SC_H[:, 513:1025]]
    wgu_wd = {}
    piece_pe_free = [0, 0]
    mid_dv = [0, 0]
    gs_free = [0, 0]
    wgu_wd[0] = dma("pool", WG[0], wgu_h[0][:, 0:2048], "wg0")
    wgu_wd[1] = dma("pool", WG[1], wgu_h[0][:, 2048:4096], "wg1")
    for oi in range(IC):
        b = oi % 2
        S.wait("pe", "pl", h3_pl)
        S.wait("pe", "dv", h3_dv)
        S.wait("pe", "dv", mid_dv[b])
        grp_pe = []
        for part in range(2):
            ip = oi * 2 + part
            S.waitt("pe", wgu_wd[ip])
            ps_ = PG[b] if part == 0 else PU[b]
            for tt in range(2):
                for kc in range(C):
                    mm(ps_[tt][:, 0:512], WG[ip % 2][:, kc * 128:(kc + 1) * 128],
                       H3[:, kc * T + tt * 512: kc * T + tt * 512 + 512],
                       start=(kc == 0), stop=(kc == C - 1),
                       inc=("pe" if kc == C - 1 else None))
                grp_pe.append(S.ctr["pe"])
            piece_pe_free[ip % 2] = S.ctr["pe"]
            if ip + 2 < IC * 2:
                nip = ip + 2
                S.wait("pool", "pe", piece_pe_free[ip % 2])
                srcp = wgu_h[nip // 2][:, (nip % 2) * 2048:(nip % 2) * 2048 + 2048]
                wgu_wd[nip] = dma("pool", WG[nip % 2], srcp, f"wg{nip % 2}")
        for tt in range(2):
            S.wait("act", "pe", grp_pe[tt])
            S.wait("act", "dv", gs_free[tt])
            act(GS[tt], PG[b][tt][:, 0:512], AF.Silu, inc="ac")
            S.wait("dve", "ac", S.ctr["ac"])
            S.wait("dve", "pe", grp_pe[2 + tt])
            dve("tensor_mul", MID[:, oi * T + tt * 512: oi * T + tt * 512 + 512],
                GS[tt], PU[b][tt][:, 0:512], inc="dv")
            gs_free[tt] = S.ctr["dv"]
        mid_dv[b] = S.ctr["dv"]
    mid_done_dv = S.ctr["dv"]
    if "d_mid" in dbg_out:
        S.wait("sp", "dv", mid_done_dv)
        for oi in range(IC):
            dma("sp", dbg_out["d_mid"][oi], MID[:, oi * T:(oi + 1) * T], "dbg")

    S.barrier()
    WD2 = [sba("WD20", [128, 2048], BF16, D0 + 163968),
           sba("WD21", [128, 2048], BF16, D0 + 168064),
           sba("WD22", [128, 2048], BF16, D0)]
    PD = [[PS[0], PS[1]], [PS[2], PS[3]]]
    XCH = [sba(f"XCHe{i}", [128, T], F32, D0 + 4224 + i * 4096) for i in range(2)]
    OCH = [sba(f"OCH{i}", [128, T], F32, D0 + 12544 + i * 4096) for i in range(2)]
    wdn_wd = {}
    wd_pe_free = [0, 0, 0]
    out_dv = [0, 0]
    out_od = [None, None]
    for i in range(3):
        wdn_wd[i] = dma("pool", WD2[i], wdn_h[i], f"wn{i}")
    for oc2 in range(C):
        b = oc2 % 2
        S.wait("sp", "dv", out_dv[b])
        xd_tok = dma("sp", XCH[b], x1_spill[oc2], f"xch{b}")
        S.wait("pe", "dv", out_dv[b])
        for kcg in range(4):
            idx = oc2 * 4 + kcg
            S.waitt("pe", wdn_wd[idx])
            S.wait("pe", "dv", mid_done_dv)
            for kc in range(C):
                ic = kcg * 16 + kc
                for tt in range(2):
                    mm(PD[b][tt][:, 0:512], WD2[idx % 3][:, kc * 128:(kc + 1) * 128],
                       MID[:, ic * T + tt * 512: ic * T + tt * 512 + 512],
                       start=(kcg == 0 and kc == 0), stop=(kcg == 3 and kc == C - 1),
                       inc=("pe" if (kc == C - 1 and tt == 1) else None))
            wd_pe_free[idx % 3] = S.ctr["pe"]
            if idx + 3 < IC:
                S.wait("pool", "pe", wd_pe_free[idx % 3])
                wdn_wd[idx + 3] = dma("pool", WD2[idx % 3], wdn_h[idx + 3], f"wn{idx % 3}")
        pe_dn = S.ctr["pe"]
        S.waitt("dve", xd_tok)
        S.wait("dve", "pe", pe_dn)
        S.waitt("dve", out_od[b])
        dve("tensor_add", OCH[b][:, 0:512], XCH[b][:, 0:512], PD[b][0][:, 0:512])
        dve("tensor_add", OCH[b][:, 512:1024], XCH[b][:, 512:1024], PD[b][1][:, 0:512],
            inc="dv")
        out_dv[b] = S.ctr["dv"]
        S.wait("sp", "dv", out_dv[b])
        out_od[b] = dma("sp", out_h[oc2], OCH[b], f"od{b}")
    S.wait("sp", "od0", S.ctr.get("od0", 0))
    S.wait("sp", "od1", S.ctr.get("od1", 0))

    with contextlib.ExitStack() as st:
        for name in list(S.sems):
            S.sems[name] = st.enter_context(nc.semaphore(name))
        with nc.Block() as block:
            S.replay(block)
    return nc


def _prep(inputs):
    f32 = np.float32
    _ = inputs.get("L0_attn_tm_decay")
    x = np.asarray(inputs["L0_attn_x"], f32)[0]
    Wk = np.asarray(inputs["L0_attn_Wk"], f32)
    Wv = np.asarray(inputs["L0_attn_Wv"], f32)
    Wr = np.asarray(inputs["L0_attn_Wr"], f32)
    Wo = np.asarray(inputs["L0_attn_Wo"], f32)
    Wkey = np.asarray(inputs["L0_attn_Wkey"], f32)
    Wval = np.asarray(inputs["L0_attn_Wval"], f32)
    Wcr = np.asarray(inputs["L0_attn_Wcr"], f32)
    Wgu = np.asarray(inputs["L0_ffn_w_gate_up"], f32)
    Wdn = np.asarray(inputs["L0_ffn_w_down"], f32)

    A = np.stack([Wk, Wv, Wr])
    wkvr = np.ascontiguousarray(
        A.reshape(3, 16, 128, 16, 128).transpose(1, 4, 0, 3, 2).reshape(16, 128, 3 * 2048))
    wo = np.ascontiguousarray(
        Wo.reshape(16, 128, 16, 128).transpose(0, 3, 2, 1).reshape(16, 128, 2048))
    wkey = np.ascontiguousarray(
        Wkey.reshape(64, 128, 16, 128).transpose(0, 3, 2, 1).reshape(64, 128, 2048))
    wval = np.ascontiguousarray(
        Wval.reshape(16, 128, 4, 16, 128).transpose(0, 2, 4, 3, 1).reshape(64, 128, 2048))
    G = Wgu[:8192].reshape(64, 128, 16, 128)
    Uu = Wgu[8192:].reshape(64, 128, 16, 128)
    wgu = np.ascontiguousarray(
        np.stack([G, Uu]).transpose(1, 4, 0, 3, 2).reshape(64, 128, 4096))
    wdn = np.ascontiguousarray(
        Wdn.reshape(16, 128, 4, 16, 128).transpose(0, 2, 4, 3, 1).reshape(64, 128, 2048))
    wcr = np.ascontiguousarray(
        Wcr.reshape(16, 128, 16, 128).transpose(0, 3, 2, 1).reshape(16, 128, 2048))

    vec_list = [
        inputs["L0_attn_ln1_g"], inputs["L0_attn_ln1_b"],
        inputs["L0_attn_ln2_g"], inputs["L0_attn_ln2_b"],
        inputs["L0_attn_tm_first"],
        np.asarray(inputs["L0_attn_tm_mk"]).reshape(-1),
        np.asarray(inputs["L0_attn_tm_mv"]).reshape(-1),
        np.asarray(inputs["L0_attn_tm_mr"]).reshape(-1),
        np.asarray(inputs["L0_attn_cm_mk"]).reshape(-1),
        np.asarray(inputs["L0_attn_cm_mr"]).reshape(-1),
    ]
    V = np.stack([np.asarray(v, f32).reshape(2048) for v in vec_list])
    vecs = np.zeros((128, NVEC), f32)
    vecs[:, :160] = V.reshape(10, 16, 128).transpose(2, 0, 1).reshape(128, 160)
    vecs[:, V_EPS] = EPS
    vecs[:, V_NHALF] = -0.5
    vecs[:, V_NONE] = -1.0
    vecs[:, V_HALF] = 0.5
    vecs[:, V_NEG] = NEG
    vecs = np.ascontiguousarray(vecs)

    x_fm = np.ascontiguousarray(x.T)
    in_maps = []
    for ci in range(NCORES):
        s = ci * T
        sl = np.zeros((2048, 1026), f32)
        if s - 2 < 0:
            sl[:, 2:] = x_fm[:, s:s + T]
        else:
            sl[:, :] = x_fm[:, s - 2:s + T]
        ximg = np.ascontiguousarray(
            sl.reshape(16, 128, 1026).transpose(1, 0, 2).reshape(128, 16 * 1026))
        hm = np.full((128, 1), 0.0 if ci == 0 else 1.0, f32)
        in_maps.append(dict(
            x_img=ximg, wkvr=wkvr, wo=wo, wkey=wkey, wval=wval,
            wgu=wgu, wdn=wdn, wcr=wcr, vecs=vecs, hmask=hm))
    return in_maps


def kernel(**inputs):
    in_maps = _prep(inputs)
    nc = _build()
    res = run_bass_kernel_spmd(nc, in_maps, core_ids=list(range(NCORES)))
    parts = []
    for ci in range(NCORES):
        o = np.asarray(res.results[ci]["out"], np.float32)
        parts.append(o.reshape(2048, T))
    full = np.concatenate(parts, axis=1)
    return np.ascontiguousarray(full.T).reshape(1, 8192, 2048).astype(np.float32)


# version 12 (final)
# speedup vs baseline: 1.1307x; 1.0313x over previous
import contextlib
import numpy as np
import concourse.bass as bass
import concourse.mybir as mybir
from concourse.bass_utils import run_bass_kernel_spmd

F32 = mybir.dt.float32
BF16 = mybir.dt.bfloat16
AF = mybir.ActivationFunctionType
ALU = mybir.AluOpType

NCORES = 8
T = 1024
C = 16
IC = 64
H = 2048
EPS = 1e-5
NEG = -1e38

V_G1, V_B1, V_G2, V_B2, V_TMF, V_MK, V_MV, V_MR, V_CMK, V_CMR = (
    0, 16, 32, 48, 64, 80, 96, 112, 128, 144)
V_EPS, V_NHALF, V_NONE, V_HALF, V_NEG = 160, 161, 162, 163, 164
NVEC = 168

ROWS3 = [0, 32, 64]


def _tiles(n, cap=512):
    out, s = [], 0
    while s < n:
        l = min(cap, n - s)
        out.append((s, l))
        s += l
    return out


class Sched:
    ENGS = ("pe", "dve", "act", "sp", "pool")

    def __init__(self):
        self.items = {e: [] for e in self.ENGS}
        self.sems = {}
        self.ctr = {}
        self.hiw = {}

    def sem(self, name):
        if name not in self.sems:
            self.sems[name] = None
            self.ctr[name] = 0

    def emit(self, eng, fn, inc=None, incn=1):
        if inc is not None:
            self.ctr[inc] += incn
        self.items[eng].append((fn, inc, incn))
        return self.ctr[inc] if inc is not None else None

    def wait(self, eng, sem, val):
        if val is None or val <= 0:
            return
        key = (eng, sem)
        if self.hiw.get(key, 0) >= val:
            return
        self.hiw[key] = val

        def f(e, sem=sem, val=val):
            e.wait_ge(self.sems[sem], val)
            return None
        self.items[eng].append((f, None, 1))

    def barrier(self, engs=None):
        vals = [(s, self.ctr[s]) for s in self.ctr]
        for eng in (engs or self.ENGS):
            for s, v in vals:
                self.wait(eng, s, v)

    def waitt(self, eng, tok):
        if tok:
            self.wait(eng, tok[0], tok[1])

    def replay(self, block):
        def mk(name):
            def run(e):
                for fn, inc, incn in self.items[name]:
                    inst = fn(e)
                    if inc is not None:
                        inst.then_inc(self.sems[inc], incn)
            return run
        block.gpsimd(mk("pool"))
        block.sync(mk("sp"))
        block.tensor(mk("pe"))
        block.vector(mk("dve"))
        block.scalar(mk("act"))


def _build(dbg=()):
    nc = bass.Bass("TRN2", target_bir_lowering=False)
    dbg = set(dbg)

    x_img = nc.dram_tensor("x_img", [128, C * 1026], F32, kind="ExternalInput").ap()
    wkvr_h = nc.dram_tensor("wkvr", [C, 128, 3 * 2048], F32, kind="ExternalInput").ap()
    wo_h = nc.dram_tensor("wo", [C, 128, 2048], F32, kind="ExternalInput").ap()
    wkey_h = nc.dram_tensor("wkey", [IC, 128, 2048], F32, kind="ExternalInput").ap()
    wval_h = nc.dram_tensor("wval", [IC, 128, 2048], F32, kind="ExternalInput").ap()
    wgu_h = nc.dram_tensor("wgu", [IC, 128, 4096], F32, kind="ExternalInput").ap()
    wdn_h = nc.dram_tensor("wdn", [IC, 128, 2048], F32, kind="ExternalInput").ap()
    wcr_h = nc.dram_tensor("wcr", [C, 128, 2048], F32, kind="ExternalInput").ap()
    vecs_h = nc.dram_tensor("vecs", [128, NVEC], F32, kind="ExternalInput").ap()
    hmask_h = nc.dram_tensor("hmask", [128, 1], F32, kind="ExternalInput").ap()
    out_h = nc.dram_tensor("out", [C, 128, T], F32, kind="ExternalOutput").ap()

    def scratch(name, shape, dt):
        kind = "ExternalOutput" if name in dbg else "Internal"
        return nc.dram_tensor(name, shape, dt, kind=kind).ap()

    xm_spill = scratch("xm_sp", [C, 128, 1025], F32)
    cmr_spill = scratch("cmr_sp", [C, 128, T], BF16)
    x1_spill = scratch("x1_sp", [C, 128, T], F32)
    kv_spill = scratch("kv_sp", [C, 128, T], F32)

    dbg_out = {}
    for nm, shape, dt in [
        ("d_bca", [128, 1026], F32), ("d_bcb", [128, 1026], F32),
        ("d_xk", [C, 128, 1025], BF16), ("d_xv", [C, 128, 1025], BF16),
        ("d_xr", [C, 128, 1025], BF16), ("d_u", [C, 128, 1025], BF16),
        ("d_cmk", [C, 128, T], BF16), ("d_h3", [C, 128, T], BF16),
        ("d_silu", [IC, 128, T], BF16), ("d_mid", [IC, 128, T], BF16),
        ("d_k", [C, 128, T], F32), ("d_v", [C, 128, T], F32),
        ("d_r", [C, 128, T], F32), ("d_stv", [6, 1, 1026], F32),
        ("d_hkvr", [128, 192], F32),
    ]:
        if nm in dbg:
            dbg_out[nm] = nc.dram_tensor(nm, shape, dt, kind="ExternalOutput").ap()

    S = Sched()
    for s in ("pe", "dv", "ac", "pl"):
        S.sem(s)

    BASE = (nc.sbuf_base + 127) // 128 * 128
    TOP = nc.sbuf_base + nc.sbuf_bytes_remaining

    def sba(name, shape, dtype, off):
        a = BASE + off
        nb = int(np.prod(shape[1:])) * (2 if dtype == BF16 else 4)
        assert a + nb <= TOP, (name, a + nb, TOP)
        return nc.alloc_sbuf_tensor_at(name, shape, dtype, offset=a).ap()

    VEC = sba("VEC", [128, NVEC], F32, 0)
    HM = sba("HM", [128, 1], F32, 768)
    ONESB = sba("ONESB", [128, 1], BF16, 896)
    ONESC = sba("ONESC", [1, 128], F32, 1024)
    STV = [sba(f"STV{i}", [1, 1026], F32, 1536 + i * 4224) for i in range(6)]
    SL_A0, SL_B0, SL_A, SL_B, SL_T0, SL_T1 = range(6)
    SC_H = sba("SC_H", [128, 1026], F32, 26880)
    SC_D = sba("SC_D", [128, 1026], F32, 31104)
    D0 = 35328

    def stv(i, n=1026):
        return STV[i][0:1, 0:n]

    def vcol(base, fc):
        return VEC[:, base + fc: base + fc + 1]

    PS = [nc.alloc_psum_tensor(f"PSB{i}", [128, 512], F32).ap() for i in range(8)]

    def dma(eng, dst, src, sem):
        S.sem(sem)

        def f(e, dst=dst, src=src):
            return e.dma_start(dst, src)
        return (sem, S.emit(eng, f, inc=sem, incn=16))

    def mm(psum, lhsT, rhs, start, stop, inc=None):
        def f(e, psum=psum, lhsT=lhsT, rhs=rhs, start=start, stop=stop):
            return e.matmul(psum, lhsT, rhs, start=start, stop=stop)
        return S.emit("pe", f, inc=inc)

    def act(out, in_, func, bias=0.0, scale=1.0, inc=None):
        def f(e, out=out, in_=in_, func=func, bias=bias, scale=scale):
            return e.activation(out, in_, func, bias=bias, scale=scale)
        return S.emit("act", f, inc=inc)

    def dve(op, *args, inc=None):
        def f(e, op=op, args=args):
            return getattr(e, op)(*args)
        return S.emit("dve", f, inc=inc)

    def pool(op, *args, inc=None):
        def f(e, op=op, args=args):
            return getattr(e, op)(*args)
        return S.emit("pool", f, inc=inc)

    def stats_finish(pa, pb, ncols, slot_a, slot_b, pe_done):
        tl = _tiles(ncols)
        S.wait("act", "pe", pe_done)
        for ti, (s, l) in enumerate(tl):
            act(stv(SL_T0)[0:1, s:s + l], pa[ROWS3[ti]:ROWS3[ti] + 1, 0:l], AF.Identity)
        for ti, (s, l) in enumerate(tl):
            act(stv(SL_T1)[0:1, s:s + l], pb[ROWS3[ti]:ROWS3[ti] + 1, 0:l], AF.Identity,
                inc=("ac" if ti == len(tl) - 1 else None))
        S.wait("dve", "ac", S.ctr["ac"])
        m = stv(SL_T0, ncols)
        q = stv(SL_T1, ncols)
        a = stv(slot_a, ncols)
        b = stv(slot_b, ncols)
        dve("tensor_scalar", m, m, 1.0 / H, None, ALU.mult)
        dve("tensor_scalar", q, q, 1.0 / H, None, ALU.mult)
        dve("tensor_mul", b, m, m)
        dve("tensor_sub", q, q, b, inc="dv")
        S.wait("act", "dv", S.ctr["dv"])
        act(a, q, AF.Ln, bias=VEC[0:1, V_EPS:V_EPS + 1])
        act(a, a, AF.Exp, scale=VEC[0:1, V_NHALF:V_NHALF + 1], inc="ac")
        S.wait("dve", "ac", S.ctr["ac"])
        dve("scalar_tensor_tensor", b, m, -1.0, a, ALU.mult, ALU.mult, inc="dv")
        return S.ctr["dv"]

    def broadcast(vec_ap, dst, ncols, dv_ready, psb):
        S.wait("pe", "dv", dv_ready)
        last = None
        for (s, l) in _tiles(ncols):
            S.wait("pe", "ac", S.ctr["ac"])
            mm(psb[:, 0:l], ONESC, vec_ap[0:1, s:s + l], start=True, stop=True, inc="pe")
            S.wait("act", "pe", S.ctr["pe"])
            last = act(dst[:, s:s + l], psb[:, 0:l], AF.Identity, inc="ac")
        return last

    M2 = sba("M2", [128, 512], F32, 209920)
    dma("sp", VEC, vecs_h, "ld")
    hm_tok = dma("sp", HM, hmask_h, "ld")
    for e_ in ("act", "dve", "pool"):
        S.wait(e_, "ld", 32)
    dve("memset", ONESB, 1.0)
    dve("memset", ONESC, 1.0, inc="dv")
    dve("memset", M2, 1.0, inc="dv")
    S.wait("dve", "dv", S.ctr["dv"])
    dve("tensor_copy", M2[:, 0:1], HM, inc="dv")
    S.wait("dve", "dv", S.ctr["dv"])
    dve("tensor_copy", M2[:, 1:2], HM, inc="dv")

    XBF = sba("XBF", [128, C * 1026], BF16, D0)
    xbf_tok = dma("pool", XBF, x_img, "xbf")

    SQ = [sba(f"SQ{i}", [128, 1026], BF16, D0 + 33024 + i * 2176) for i in range(2)]
    pa, pb = PS[0], PS[1]
    tl6 = _tiles(1026)
    sq_pe_free = [0, 0]
    for fc in range(C):
        b = fc % 2
        xv = XBF[:, fc * 1026:(fc + 1) * 1026]
        S.waitt("act", xbf_tok)
        S.wait("act", "pe", sq_pe_free[b])
        act(SQ[b], xv, AF.Square, inc="ac")
        sq_ac = S.ctr["ac"]
        S.waitt("pe", xbf_tok)
        S.wait("pe", "ac", sq_ac)
        for ti, (s, l) in enumerate(tl6):
            mm(pa[ROWS3[ti]:ROWS3[ti] + 1, 0:l], ONESB, xv[:, s:s + l],
               start=(fc == 0), stop=(fc == C - 1))
        for ti, (s, l) in enumerate(tl6):
            mm(pb[ROWS3[ti]:ROWS3[ti] + 1, 0:l], ONESB, SQ[b][:, s:s + l],
               start=(fc == 0), stop=(fc == C - 1),
               inc=("pe" if ti == len(tl6) - 1 else None))
        sq_pe_free[b] = S.ctr["pe"]
    a0_dv = stats_finish(pa, pb, 1026, SL_A0, SL_B0, S.ctr["pe"])

    BC_A = sba("BC_A1", [128, 1026], F32, D0 + 140032)
    BC_B = sba("BC_B1", [128, 1026], F32, D0 + 144256)
    XCH = [sba(f"XCHa{i}", [128, 1026], F32, D0 + 131584 + i * 4224) for i in range(2)]
    xch_free = [0, 0]
    S.wait("dve", "dv", a0_dv)
    dve("tensor_mul", stv(SL_A), stv(SL_T1), stv(SL_A0))
    dve("tensor_mul", stv(SL_A), stv(SL_A), stv(SL_A0), inc="dv")
    S.wait("act", "dv", S.ctr["dv"])
    act(stv(SL_A), stv(SL_A), AF.Ln, bias=VEC[0:1, V_EPS:V_EPS + 1])
    act(stv(SL_A), stv(SL_A), AF.Exp, scale=VEC[0:1, V_NHALF:V_NHALF + 1], inc="ac")
    S.wait("dve", "ac", S.ctr["ac"])
    dve("tensor_mul", stv(SL_T1), stv(SL_A0), stv(SL_A))
    dve("tensor_mul", stv(SL_T0), stv(SL_B0), stv(SL_A), inc="dv")
    a01_dv = S.ctr["dv"]
    broadcast(stv(SL_T1), BC_A, 1026, a01_dv, PS[2])
    bc_ac = broadcast(stv(SL_T0), BC_B, 1026, a01_dv, PS[2])

    XK = sba("XK", [128, C * 1025], BF16, D0)
    XV = sba("XV", [128, C * 1025], BF16, D0 + 32896)
    XR = sba("XR", [128, C * 1025], BF16, D0 + 65792)
    WKVR = [sba(f"WKVR{i}", [128, 3 * 2048], BF16, D0 + 148480 + i * 12288)
            for i in range(2)]
    wkvr_wd = {}
    wkvr_wd[0] = dma("pool", WKVR[0], wkvr_h[0], "wk0")
    wkvr_wd[1] = dma("pool", WKVR[1], wkvr_h[1], "wk1")

    SC_H2 = sba("SC_H2", [128, 1026], F32, D0 + 98688)
    SC_HX = [SC_H, SC_H2]
    hx_free = [0, 0]
    S.wait("sp", "dv", max(xch_free))
    xch_free = [0, 0]
    for fc in range(C):
        b = fc % 2
        S.wait("sp", "pl", xch_free[b])
        xd_tok = dma("sp", XCH[b], x_img[:, fc * 1026:(fc + 1) * 1026], f"xch{b}")
        hx = SC_HX[b]
        S.waitt("pool", xd_tok)
        S.wait("pool", "ac", bc_ac)
        S.wait("pool", "dv", hx_free[b])
        pool("tensor_mul", hx, XCH[b], BC_A)
        pool("tensor_add", hx, hx, BC_B, inc="pl")
        xch_free[b] = S.ctr["pl"]
        S.wait("act", "pl", S.ctr["pl"])
        act(hx, hx, AF.Identity, scale=vcol(V_G1, fc), bias=vcol(V_B1, fc), inc="ac")
        S.wait("dve", "ac", S.ctr["ac"])
        dve("tensor_mul", hx[:, 0:512], hx[:, 0:512], M2)
        dve("tensor_sub", SC_D[:, 0:1025], hx[:, 1:1026], hx[:, 0:1025])
        o = fc * 1025
        dve("scalar_tensor_tensor", XK[:, o:o + 1025], SC_D[:, 0:1025], vcol(V_MK, fc),
            hx[:, 0:1025], ALU.mult, ALU.add)
        dve("scalar_tensor_tensor", XV[:, o:o + 1025], SC_D[:, 0:1025], vcol(V_MV, fc),
            hx[:, 0:1025], ALU.mult, ALU.add)
        dve("scalar_tensor_tensor", XR[:, o:o + 1025], SC_D[:, 0:1025], vcol(V_MR, fc),
            hx[:, 0:1025], ALU.mult, ALU.add, inc="dv")
        hx_free[b] = S.ctr["dv"]
    mixes_dv = S.ctr["dv"]
    mixes_pl = S.ctr["pl"]
    if "d_xk" in dbg_out:
        S.wait("sp", "dv", mixes_dv)
        for fc in range(C):
            dma("sp", dbg_out["d_xk"][fc], XK[:, fc * 1025:(fc + 1) * 1025], "dbg")
            dma("sp", dbg_out["d_xv"][fc], XV[:, fc * 1025:(fc + 1) * 1025], "dbg")
            dma("sp", dbg_out["d_xr"][fc], XR[:, fc * 1025:(fc + 1) * 1025], "dbg")
    if "d_bca" in dbg_out:
        S.wait("sp", "ac", bc_ac)
        dma("sp", dbg_out["d_bca"], BC_A, "dbg")
        dma("sp", dbg_out["d_bcb"], BC_B, "dbg")

    S.barrier()
    U = sba("U", [128, C * 1025], BF16, D0 + 98688)
    CH = [sba(f"CH{i}", [128, 512], F32, D0 + 131584 + i * 2048) for i in range(8)]
    PSK = [PS[0], PS[1]]
    PSV = [PS[2], PS[3]]
    PSR = [PS[4], PS[5]]
    HK, HV, HR = PS[0][:, 0:1], PS[2][:, 0:1], PS[4][:, 0:1]
    HW = PS[6][:, 0:1]
    HSTX = sba("HSTX", [128, 192], F32, 209152)
    HST = HSTX[:, 0:48]
    SPC = SC_H[:, 0:512]

    chain_dv_set = [0, 0]
    wkvr_pe_free = [0, 0]
    halo_stage_ac = 0
    for oc in range(C):
        wb = WKVR[oc % 2]
        S.waitt("pe", wkvr_wd[oc])
        S.wait("pe", "dv", mixes_dv)
        for tt in range(2):
            S.wait("pe", "dv", chain_dv_set[tt])
            if tt == 0:
                S.wait("pe", "ac", halo_stage_ac)
            tok0 = 1 + tt * 512
            for mi, dst, mixsrc in ((0, PSK, XK), (1, PSV, XV), (2, PSR, XR)):
                for kc in range(C):
                    mm(dst[tt][:, 0:512],
                       wb[:, (mi * 16 + kc) * 128:(mi * 16 + kc + 1) * 128],
                       mixsrc[:, kc * 1025 + tok0: kc * 1025 + tok0 + 512],
                       start=(kc == 0), stop=(kc == C - 1),
                       inc=("pe" if kc == C - 1 else None))
            pe_kvr = S.ctr["pe"]
            c0, c1, c2, c3 = CH[tt * 4: tt * 4 + 4]
            uo = U[:, oc * 1025 + tok0: oc * 1025 + tok0 + 512]
            S.wait("act", "pe", pe_kvr - 2)
            act(c0, PSK[tt][:, 0:512], AF.Identity, bias=vcol(V_TMF, oc), inc="ac")
            S.wait("dve", "ac", S.ctr["ac"])
            dve("tensor_scalar", c1, c0, VEC[:, V_NEG:V_NEG + 1], None, ALU.max)
            dve("tensor_sub", c0, c0, c1, inc="dv")
            S.wait("act", "dv", S.ctr["dv"])
            act(c1, c0, AF.Exp)
            act(c2, c0, AF.Exp, scale=VEC[:, V_NONE:V_NONE + 1])
            S.wait("act", "pe", pe_kvr)
            act(c3, PSR[tt][:, 0:512], AF.Tanh, scale=VEC[:, V_HALF:V_HALF + 1], inc="ac")
            S.wait("dve", "ac", S.ctr["ac"])
            S.wait("dve", "pe", pe_kvr)
            dve("scalar_tensor_tensor", c0, PSV[tt][:, 0:512], 0.5, c1, ALU.mult, ALU.mult)
            dve("tensor_mul", c0, c0, c2)
            dve("scalar_tensor_tensor", uo, c3, 1.0, c0, ALU.add, ALU.mult, inc="dv")
            chain_dv_set[tt] = S.ctr["dv"]
            if "d_k" in dbg_out:
                DBG1, DBG2, DBG3 = SC_H[:, 0:512], SC_H[:, 513:1025], SC_D[:, 514:1026]
                S.wait("act", "dv", chain_dv_set[tt])
                act(DBG1, PSK[tt][:, 0:512], AF.Identity)
                act(DBG2, PSV[tt][:, 0:512], AF.Identity)
                act(DBG3, PSR[tt][:, 0:512], AF.Identity, inc="ac")
                S.wait("sp", "ac", S.ctr["ac"])
                dma("sp", dbg_out["d_k"][oc][:, tt * 512:tt * 512 + 512], DBG1, "dbg")
                dma("sp", dbg_out["d_v"][oc][:, tt * 512:tt * 512 + 512], DBG2, "dbg")
                dv_dump = dma("sp", dbg_out["d_r"][oc][:, tt * 512:tt * 512 + 512], DBG3, "dbg")
                S.waitt("pe", dv_dump)
                S.waitt("dve", dv_dump)
        S.wait("pe", "ac", halo_stage_ac)
        S.wait("pe", "dv", chain_dv_set[0])
        for mi, dsth, mixsrc in ((0, HK, XK), (1, HV, XV), (2, HR, XR)):
            for kc in range(C):
                mm(dsth, wb[:, (mi * 16 + kc) * 128:(mi * 16 + kc + 1) * 128],
                   mixsrc[:, kc * 1025: kc * 1025 + 1],
                   start=(kc == 0), stop=(kc == C - 1),
                   inc=("pe" if (mi == 2 and kc == C - 1) else None))
        pe_halo = S.ctr["pe"]
        wkvr_pe_free[oc % 2] = pe_halo
        if oc + 2 < C:
            S.wait("pool", "pe", pe_halo)
            wkvr_wd[oc + 2] = dma("pool", WKVR[oc % 2], wkvr_h[oc + 2], f"wk{oc % 2}")
        S.wait("act", "pe", pe_halo)
        act(HST[:, oc:oc + 1], HK, AF.Identity)
        act(HST[:, 16 + oc:16 + oc + 1], HV, AF.Identity)
        act(HST[:, 32 + oc:32 + oc + 1], HR, AF.Identity, inc="ac")
        halo_stage_ac = S.ctr["ac"]
    W16 = HSTX[:, 48:64]
    P16 = HSTX[:, 64:80]
    E16 = HSTX[:, 80:96]
    R16 = HSTX[:, 96:112]
    T16 = HSTX[:, 112:128]
    N16 = HSTX[:, 128:144]
    UH16 = HSTX[:, 144:160]
    act(SPC, SPC, AF.Identity, inc="ac")
    S.wait("dve", "ac", S.ctr["ac"])
    dve("tensor_add", W16, HST[:, 0:16], VEC[:, V_TMF:V_TMF + 16])
    dve("tensor_copy", SPC, SPC)
    dve("tensor_scalar", P16, W16, VEC[:, V_NEG:V_NEG + 1], None, ALU.max)
    dve("tensor_copy", SPC, SPC)
    dve("tensor_sub", W16, W16, P16)
    dve("tensor_copy", SPC, SPC, inc="dv")
    S.wait("act", "dv", S.ctr["dv"])
    act(E16, W16, AF.Exp)
    act(R16, W16, AF.Exp, scale=VEC[:, V_NONE:V_NONE + 1])
    act(T16, HST[:, 32:48], AF.Tanh, scale=VEC[:, V_HALF:V_HALF + 1])
    act(SPC, SPC, AF.Identity, inc="ac")
    S.wait("dve", "ac", S.ctr["ac"])
    dve("scalar_tensor_tensor", N16, HST[:, 16:32], 0.5, E16, ALU.mult, ALU.mult)
    dve("tensor_copy", SPC, SPC)
    dve("tensor_mul", N16, N16, R16)
    dve("tensor_copy", SPC, SPC)
    dve("scalar_tensor_tensor", UH16, T16, 1.0, N16, ALU.add, ALU.mult)
    dve("tensor_copy", SPC, SPC)
    for oc in range(C):
        dve("tensor_copy", U[:, oc * 1025: oc * 1025 + 1], UH16[:, oc:oc + 1])
    dve("tensor_copy", SPC, SPC, inc="dv")
    if "d_hkvr" in dbg_out:
        S.wait("sp", "dv", S.ctr["dv"])
        dma("sp", dbg_out["d_hkvr"], HSTX, "dbg")
    u_done_dv = S.ctr["dv"]
    if "d_u" in dbg_out:
        S.wait("sp", "dv", u_done_dv)
        for fc in range(C):
            dma("sp", dbg_out["d_u"][fc], U[:, fc * 1025:(fc + 1) * 1025], "dbg")

    S.barrier()
    BC_A = sba("BC_A2", [128, 1026], F32, D0 + 165376)
    BC_B = sba("BC_B2", [128, 1026], F32, D0 + 169600)
    broadcast(stv(SL_A0), BC_A, 1026, a0_dv, PS[7])
    bc_ac = broadcast(stv(SL_B0), BC_B, 1026, a0_dv, PS[7])

    XCH = [sba(f"XCHb{i}", [128, 1026], F32, D0 + 139776 + i * 4224) for i in range(2)]
    WO = [sba(f"WO{i}", [128, 2048], BF16, D0 + 131584 + i * 4096) for i in range(2)]
    PSW = [[PS[0], PS[1]], [PS[2], PS[3]]]
    XMCH = [sba(f"XMCH{i}", [128, 1025], F32, D0 + 148224 + i * 4224) for i in range(2)]
    XMBF = [sba(f"XMBF{i}", [128, 1025], BF16, D0 + 156672 + i * 2176) for i in range(2)]
    SQ5 = [sba(f"SQ5{i}", [128, 1025], BF16, D0 + 161024 + i * 2176) for i in range(2)]
    pa, pb = PS[4], PS[5]
    tl5 = _tiles(1025)

    wo_wd = {}
    wo_wd[0] = dma("pool", WO[0], wo_h[0], "wwo0")
    wo_wd[1] = dma("pool", WO[1], wo_h[1], "wwo1")
    wo_pe_free = [0, 0]
    xm_halo_dv_last = 0
    xm_dv = [0, 0]
    xm_spilled = [None, None]
    xm_stats_pe = [0, 0]
    xch_free = [0, 0]
    sq5_ac = [0, 0]
    for oc in range(C):
        b = oc % 2
        S.waitt("pe", wo_wd[oc])
        S.wait("pe", "dv", u_done_dv)
        S.wait("pe", "dv", xm_dv[b])
        for tt in range(2):
            tok0 = 1 + tt * 512
            for kc in range(C):
                mm(PSW[b][tt][:, 0:512], WO[b][:, kc * 128:(kc + 1) * 128],
                   U[:, kc * 1025 + tok0: kc * 1025 + tok0 + 512],
                   start=(kc == 0), stop=(kc == C - 1),
                   inc=("pe" if kc == C - 1 else None))
        S.wait("pe", "dv", xm_halo_dv_last)
        for kc in range(C):
            mm(HW, WO[b][:, kc * 128:(kc + 1) * 128],
               U[:, kc * 1025: kc * 1025 + 1],
               start=(kc == 0), stop=(kc == C - 1),
               inc=("pe" if kc == C - 1 else None))
        pe_wo = S.ctr["pe"]
        wo_pe_free[b] = pe_wo
        if oc + 2 < C:
            S.wait("pool", "pe", pe_wo)
            wo_wd[oc + 2] = dma("pool", WO[b], wo_h[oc + 2], f"wwo{b}")
        S.wait("sp", "dv", xch_free[b])
        xd_tok = dma("sp", XCH[b], x_img[:, oc * 1026:(oc + 1) * 1026], f"xch{b}")
        S.waitt("dve", xd_tok)
        S.wait("dve", "ac", bc_ac)
        S.waitt("dve", xm_spilled[b])
        S.wait("dve", "ac", sq5_ac[b])
        dve("tensor_mul", XMCH[b], XCH[b][:, 1:1026], BC_A[:, 1:1026])
        dve("tensor_add", XMCH[b], XMCH[b], BC_B[:, 1:1026], inc="dv")
        xch_free[b] = S.ctr["dv"]
        S.wait("dve", "pe", pe_wo)
        dve("tensor_add", XMCH[b][:, 0:1], XMCH[b][:, 0:1], HW, inc="dv")
        xm_halo_dv_last = S.ctr["dv"]
        dve("tensor_add", XMCH[b][:, 1:513], XMCH[b][:, 1:513], PSW[b][0][:, 0:512])
        dve("tensor_add", XMCH[b][:, 513:1025], XMCH[b][:, 513:1025], PSW[b][1][:, 0:512])
        S.wait("dve", "pe", xm_stats_pe[b])
        dve("tensor_copy", XMBF[b], XMCH[b], inc="dv")
        xm_dv[b] = S.ctr["dv"]
        S.wait("act", "dv", xm_dv[b])
        S.wait("act", "pe", xm_stats_pe[b])
        act(SQ5[b], XMCH[b], AF.Square, inc="ac")
        ac_sq = S.ctr["ac"]
        sq5_ac[b] = ac_sq
        S.wait("act", "dv", xm_dv[b])
        xm_spilled[b] = dma("act", xm_spill[oc], XMCH[b], f"sxm{b}")
        S.wait("pe", "dv", xm_dv[b])
        S.wait("pe", "ac", ac_sq)
        for ti, (s, l) in enumerate(tl5):
            mm(pa[ROWS3[ti]:ROWS3[ti] + 1, 0:l], ONESB, XMBF[b][:, s:s + l],
               start=(oc == 0), stop=(oc == C - 1))
        for ti, (s, l) in enumerate(tl5):
            mm(pb[ROWS3[ti]:ROWS3[ti] + 1, 0:l], ONESB, SQ5[b][:, s:s + l],
               start=(oc == 0), stop=(oc == C - 1),
               inc=("pe" if ti == len(tl5) - 1 else None))
        xm_stats_pe[b] = S.ctr["pe"]
    a2_dv = stats_finish(pa, pb, 1025, SL_A, SL_B, S.ctr["pe"])
    broadcast(stv(SL_A, 1025), BC_A[:, 0:1025], 1025, a2_dv, PS[7])
    bc_ac = broadcast(stv(SL_B, 1025), BC_B[:, 0:1025], 1025, a2_dv, PS[7])

    S.barrier()
    CMK = sba("CMK", [128, C * T], BF16, D0)
    CMR = [sba(f"CMR{i}", [128, T], BF16, D0 + 32896 + i * 2048) for i in range(2)]
    SC_H2b = sba("SC_H2b", [128, 1026], F32, D0 + 73856)
    SC_H2c = sba("SC_H2c", [128, 1026], F32, D0 + 78080)
    XCHb3 = sba("XCHb3", [128, 1026], F32, D0 + 82304)
    CMR.append(sba("CMR2", [128, T], BF16, D0 + 36992))
    SC_HX = [SC_H, SC_H2b, SC_H2c]
    XCH6 = [XCH[0], XCH[1], XCHb3]
    hx_free = [0, 0, 0]
    S.wait("sp", "dv", max(xch_free))
    xch_free = [0, 0, 0]
    cmr_spilled = [None, None, None]
    spill_q = []
    for fc in range(C):
        b = fc % 3
        S.wait("sp", "pl", xch_free[b])
        xd_tok = dma("sp", XCH6[b][:, 0:1025], xm_spill[fc], f"xch{b}")
        hx = SC_HX[b]
        S.waitt("pool", xd_tok)
        S.wait("pool", "ac", bc_ac)
        S.wait("pool", "dv", hx_free[b])
        pool("tensor_mul", hx[:, 0:1025], XCH6[b][:, 0:1025], BC_A[:, 0:1025])
        pool("tensor_add", hx[:, 0:1025], hx[:, 0:1025], BC_B[:, 0:1025], inc="pl")
        xch_free[b] = S.ctr["pl"]
        S.wait("act", "pl", S.ctr["pl"])
        act(hx[:, 0:1025], hx[:, 0:1025], AF.Identity,
            scale=vcol(V_G2, fc), bias=vcol(V_B2, fc), inc="ac")
        S.wait("dve", "ac", S.ctr["ac"])
        dve("tensor_mul", hx[:, 0:511], hx[:, 0:511], M2[:, 1:512])
        dve("tensor_sub", SC_D[:, 0:T], hx[:, 1:1025], hx[:, 0:T])
        dve("scalar_tensor_tensor", CMK[:, fc * T:(fc + 1) * T], SC_D[:, 0:T],
            vcol(V_CMK, fc), hx[:, 0:T], ALU.mult, ALU.add)
        S.waitt("dve", cmr_spilled[b])
        dve("scalar_tensor_tensor", CMR[b], SC_D[:, 0:T],
            vcol(V_CMR, fc), hx[:, 0:T], ALU.mult, ALU.add, inc="dv")
        hx_free[b] = S.ctr["dv"]
        spill_q.append((fc, b, S.ctr["dv"]))
        if len(spill_q) >= 2:
            jfc, jb, jdv = spill_q.pop(0)
            S.wait("sp", "dv", jdv)
            cmr_spilled[jb] = dma("sp", cmr_spill[jfc], CMR[jb], f"scm{jb}")
    while spill_q:
        jfc, jb, jdv = spill_q.pop(0)
        S.wait("sp", "dv", jdv)
        cmr_spilled[jb] = dma("sp", cmr_spill[jfc], CMR[jb], f"scm{jb}")
    cmk_done_dv = S.ctr["dv"]
    if "d_cmk" in dbg_out:
        S.wait("sp", "dv", cmk_done_dv)
        for fc in range(C):
            dma("sp", dbg_out["d_cmk"][fc], CMK[:, fc * T:(fc + 1) * T], "dbg")

    S.barrier()
    SILU = sba("SILU", [128, IC * T], BF16, D0 + 32896)
    WK3 = [sba(f"WK3{i}", [128, 2048], BF16, D0 + 163968 + i * 4096) for i in range(2)]
    PSA = [PS[0], PS[1]]
    PSB = [PS[2], PS[3]]
    wkey_wd = {}
    wk_pe_free = [0, 0]
    silu_ac = [0, 0]
    for i in range(2):
        wkey_wd[i] = dma("pool", WK3[i], wkey_h[i], f"wy{i}")
    for oi in range(IC):
        S.waitt("pe", wkey_wd[oi])
        S.wait("pe", "dv", cmk_done_dv)
        ps = (PSA, PSB)[oi % 2]
        S.wait("pe", "ac", silu_ac[oi % 2])
        for tt in range(2):
            for kc in range(C):
                mm(ps[tt][:, 0:512], WK3[oi % 2][:, kc * 128:(kc + 1) * 128],
                   CMK[:, kc * T + tt * 512: kc * T + tt * 512 + 512],
                   start=(kc == 0), stop=(kc == C - 1),
                   inc=("pe" if kc == C - 1 else None))
        pe_kk = S.ctr["pe"]
        wk_pe_free[oi % 2] = pe_kk
        if oi + 2 < IC:
            S.wait("pool", "pe", pe_kk)
            wkey_wd[oi + 2] = dma("pool", WK3[oi % 2], wkey_h[oi + 2], f"wy{oi % 2}")
        S.wait("act", "pe", pe_kk)
        act(SILU[:, oi * T: oi * T + 512], ps[0][:, 0:512], AF.Silu)
        silu_ac[oi % 2] = act(SILU[:, oi * T + 512: oi * T + 1024], ps[1][:, 0:512],
                              AF.Silu, inc="ac")
    silu_done_ac = S.ctr["ac"]
    if "d_silu" in dbg_out:
        S.wait("sp", "ac", silu_done_ac)
        for oi in range(IC):
            dma("sp", dbg_out["d_silu"][oi], SILU[:, oi * T:(oi + 1) * T], "dbg")

    S.barrier()
    WV2 = [sba(f"WV2{i}", [128, 2048], BF16, D0 + 163968 + i * 4096) for i in range(2)]
    PKV = [[PS[0], PS[1]], [PS[2], PS[3]]]
    KVCH = [SC_H[:, 0:T], SC_D[:, 0:T]]
    CMRR = sba("CMRR", [128, C * T], BF16, D0)
    cmr_re_xd = 0
    for fc in range(C):
        cmr_re_xd = dma("sp", CMRR[:, fc * T:(fc + 1) * T], cmr_spill[fc], "cmrr")
    wval_wd = {}
    wv_pe_free = [0, 0]
    kv_evac_ac = [0, 0]
    kv_spilled = [None, None]
    for i in range(2):
        wval_wd[i] = dma("pool", WV2[i], wval_h[i], f"wv{i}")
    for oc2 in range(C):
        b = oc2 % 2
        S.wait("pe", "ac", kv_evac_ac[b])
        for kcg in range(4):
            idx = oc2 * 4 + kcg
            S.waitt("pe", wval_wd[idx])
            S.wait("pe", "ac", silu_done_ac)
            for kc in range(C):
                ic = kcg * 16 + kc
                for tt in range(2):
                    mm(PKV[b][tt][:, 0:512], WV2[idx % 2][:, kc * 128:(kc + 1) * 128],
                       SILU[:, ic * T + tt * 512: ic * T + tt * 512 + 512],
                       start=(kcg == 0 and kc == 0), stop=(kcg == 3 and kc == C - 1),
                       inc=("pe" if (kc == C - 1 and tt == 1) else None))
            wv_pe_free[idx % 2] = S.ctr["pe"]
            if idx + 2 < IC:
                S.wait("pool", "pe", wv_pe_free[idx % 2])
                wval_wd[idx + 2] = dma("pool", WV2[idx % 2], wval_h[idx + 2], f"wv{idx % 2}")
        pe_kv = S.ctr["pe"]
        S.wait("act", "pe", pe_kv)
        S.waitt("act", kv_spilled[b])
        act(KVCH[b][:, 0:512], PKV[b][0][:, 0:512], AF.Identity)
        kv_evac_ac[b] = act(KVCH[b][:, 512:1024], PKV[b][1][:, 0:512], AF.Identity,
                            inc="ac")
        S.wait("sp", "ac", kv_evac_ac[b])
        kv_spilled[b] = dma("sp", kv_spill[oc2], KVCH[b], f"skv{b}")

    S.barrier()
    WC2 = [sba(f"WC2{i}", [128, 2048], BF16, D0 + 32896 + i * 4096) for i in range(2)]
    PCR = [[PS[0], PS[1]], [PS[2], PS[3]]]
    pa, pb = PS[4], PS[5]
    XCH = [sba(f"XCHc{i}", [128, T], F32, D0 + 41088 + i * 4096) for i in range(2)]
    XM2 = [sba(f"XM2{i}", [128, T], F32, D0 + 49280 + i * 4096) for i in range(2)]
    KV2 = [sba(f"KV2{i}", [128, T], F32, D0 + 57472 + i * 4096) for i in range(2)]
    X1CH = [sba(f"X1CH{i}", [128, T], F32, D0 + 65664 + i * 4096) for i in range(2)]
    X1BF = [sba(f"X1BF{i}", [128, T], BF16, D0 + 73856 + i * 2048) for i in range(2)]
    SQ8 = [sba(f"SQ8{i}", [128, T], BF16, D0 + 77952 + i * 2048) for i in range(2)]
    CH0 = SC_H[:, 0:512]
    CH1 = SC_H[:, 513:1025]
    wcr_wd = {}
    wcr_wd[0] = dma("pool", WC2[0], wcr_h[0], "wc0")
    wcr_wd[1] = dma("pool", WC2[1], wcr_h[1], "wc1")
    wc_pe_free = [0, 0]
    sq8_ac = [0, 0]
    last_asm_dv = 0
    x1_dv = [0, 0]
    x1_spilled = [None, None]
    x1_stats_pe = [0, 0]
    tl4 = _tiles(T)
    stats_ready = {}

    def emit_x1_stats(j):
        bj = j % 2
        dvv, acv = stats_ready[j]
        S.wait("pe", "dv", dvv)
        S.wait("pe", "ac", acv)
        for ti, (s, l) in enumerate(tl4):
            mm(pa[ROWS3[ti]:ROWS3[ti] + 1, 0:l], ONESB, X1BF[bj][:, s:s + l],
               start=(j == 0), stop=(j == C - 1),
               inc=("pe" if ti == len(tl4) - 1 else None))
        for ti, (s, l) in enumerate(tl4):
            mm(pb[ROWS3[ti]:ROWS3[ti] + 1, 0:l], ONESB, SQ8[bj][:, s:s + l],
               start=(j == 0), stop=(j == C - 1),
               inc=("pe" if ti == len(tl4) - 1 else None))
        x1_stats_pe[bj] = S.ctr["pe"]
    def p8b_loads(j):
        bj = j % 2
        S.wait("sp", "dv", x1_dv[bj])
        t1 = dma("sp", XCH[bj], x_img[:, j * 1026 + 2:(j + 1) * 1026], f"xch{bj}")
        t2 = dma("sp", XM2[bj], xm_spill[j][:, 1:1025], f"xm2{bj}")
        t3 = dma("sp", KV2[bj], kv_spill[j], f"kv2{bj}")
        return (t1, t2, t3)

    xd_loads_of = {0: p8b_loads(0), 1: p8b_loads(1)}
    for oc2 in range(C):
        b = oc2 % 2
        xd_loads = xd_loads_of[oc2]
        S.waitt("pe", wcr_wd[oc2])
        S.waitt("pe", cmr_re_xd)
        S.wait("pe", "dv", x1_dv[b])
        for tt in range(2):
            for kc in range(C):
                mm(PCR[b][tt][:, 0:512], WC2[b][:, kc * 128:(kc + 1) * 128],
                   CMRR[:, kc * T + tt * 512: kc * T + tt * 512 + 512],
                   start=(kc == 0), stop=(kc == C - 1),
                   inc=("pe" if (kc == C - 1 and tt == 1) else None))
        pe_cr = S.ctr["pe"]
        wc_pe_free[b] = pe_cr
        if oc2 + 2 < C:
            S.wait("pool", "pe", pe_cr)
            wcr_wd[oc2 + 2] = dma("pool", WC2[b], wcr_h[oc2 + 2], f"wc{b}")
        for tok_ in xd_loads:
            S.waitt("dve", tok_)
        S.waitt("dve", x1_spilled[b])
        S.wait("dve", "ac", sq8_ac[b])
        S.wait("dve", "pe", pe_cr)
        S.wait("act", "pe", pe_cr)
        for tt in range(2):
            sl = slice(tt * 512, tt * 512 + 512)
            S.wait("act", "dv", last_asm_dv)
            act(CH0, PCR[b][tt][:, 0:512], AF.Sigmoid, inc="ac")
            S.wait("dve", "ac", S.ctr["ac"])
            dve("tensor_add", X1CH[b][:, sl], XCH[b][:, sl], XM2[b][:, sl])
            dve("tensor_mul", CH1, CH0, KV2[b][:, sl])
            dve("tensor_add", X1CH[b][:, sl], X1CH[b][:, sl], CH1, inc="dv")
            last_asm_dv = S.ctr["dv"]
        S.wait("dve", "pe", x1_stats_pe[b])
        dve("tensor_copy", X1BF[b], X1CH[b], inc="dv")
        x1_dv[b] = S.ctr["dv"]
        S.wait("act", "dv", x1_dv[b])
        S.wait("act", "pe", x1_stats_pe[b])
        act(SQ8[b], X1CH[b], AF.Square, inc="ac")
        ac_sq = S.ctr["ac"]
        sq8_ac[b] = ac_sq
        S.wait("sp", "dv", x1_dv[b])
        x1_spilled[b] = dma("sp", x1_spill[oc2], X1CH[b], f"sx1{b}")
        if oc2 + 2 < C:
            xd_loads_of[oc2 + 2] = p8b_loads(oc2 + 2)
        stats_ready[oc2] = (x1_dv[b], ac_sq)
        if oc2 > 0:
            emit_x1_stats(oc2 - 1)
    emit_x1_stats(C - 1)
    a3_dv = stats_finish(pa, pb, T, SL_A, SL_B, S.ctr["pe"])

    S.barrier()
    H3 = sba("H3", [128, C * T], BF16, D0)
    BC_A = sba("BC_A3", [128, T], F32, D0 + 32896)
    BC_B = sba("BC_B3", [128, T], F32, D0 + 36992)
    broadcast(stv(SL_A, T), BC_A, T, a3_dv, PS[0])
    bc_ac = broadcast(stv(SL_B, T), BC_B, T, a3_dv, PS[0])
    XCH = [sba(f"XCHd{i}", [128, T], F32, D0 + 41088 + i * 4224) for i in range(4)]
    HXD = SC_H
    HXP = SC_D
    xch_free = [0, 0, 0, 0]
    for fc in range(C):
        b = fc % 4
        eng, hx = (dve, HXD) if fc % 2 == 0 else (pool, HXP)
        sem = "dv" if fc % 2 == 0 else "pl"
        ename = "dve" if fc % 2 == 0 else "pool"
        S.wait("sp", sem, xch_free[b])
        xd_tok = dma("sp", XCH[b], x1_spill[fc], f"xch{b}")
        S.waitt(ename, xd_tok)
        S.wait(ename, "ac", bc_ac)
        eng("tensor_mul", hx[:, 0:T], XCH[b], BC_A)
        eng("tensor_add", H3[:, fc * T:(fc + 1) * T], hx[:, 0:T], BC_B, inc=sem)
        xch_free[b] = S.ctr[sem]
    h3_dv = S.ctr["dv"]
    h3_pl = S.ctr["pl"]
    if "d_h3" in dbg_out:
        S.wait("sp", "dv", h3_dv)
        for fc in range(C):
            dma("sp", dbg_out["d_h3"][fc], H3[:, fc * T:(fc + 1) * T], "dbg")

    S.barrier()
    MID = sba("MID", [128, IC * T], BF16, D0 + 32896)
    WG = [sba(f"WG{i}", [128, 2048], BF16, D0 + 163968 + i * 4096) for i in range(2)]
    PG = [[PS[0], PS[1]], [PS[2], PS[3]]]
    PU = [[PS[4], PS[5]], [PS[6], PS[7]]]
    GS = [SC_H[:, 0:512], SC_H[:, 513:1025]]
    wgu_wd = {}
    piece_pe_free = [0, 0]
    mid_dv = [0, 0]
    gs_free = [0, 0]
    wgu_wd[0] = dma("pool", WG[0], wgu_h[0][:, 0:2048], "wg0")
    wgu_wd[1] = dma("pool", WG[1], wgu_h[0][:, 2048:4096], "wg1")
    for oi in range(IC):
        b = oi % 2
        S.wait("pe", "pl", h3_pl)
        S.wait("pe", "dv", h3_dv)
        S.wait("pe", "dv", mid_dv[b])
        grp_pe = []
        for part in range(2):
            ip = oi * 2 + part
            S.waitt("pe", wgu_wd[ip])
            ps_ = PG[b] if part == 0 else PU[b]
            for tt in range(2):
                for kc in range(C):
                    mm(ps_[tt][:, 0:512], WG[ip % 2][:, kc * 128:(kc + 1) * 128],
                       H3[:, kc * T + tt * 512: kc * T + tt * 512 + 512],
                       start=(kc == 0), stop=(kc == C - 1),
                       inc=("pe" if kc == C - 1 else None))
                grp_pe.append(S.ctr["pe"])
            piece_pe_free[ip % 2] = S.ctr["pe"]
            if ip + 2 < IC * 2:
                nip = ip + 2
                S.wait("pool", "pe", piece_pe_free[ip % 2])
                srcp = wgu_h[nip // 2][:, (nip % 2) * 2048:(nip % 2) * 2048 + 2048]
                wgu_wd[nip] = dma("pool", WG[nip % 2], srcp, f"wg{nip % 2}")
        for tt in range(2):
            S.wait("act", "pe", grp_pe[tt])
            S.wait("act", "dv", gs_free[tt])
            act(GS[tt], PG[b][tt][:, 0:512], AF.Silu, inc="ac")
            S.wait("dve", "ac", S.ctr["ac"])
            S.wait("dve", "pe", grp_pe[2 + tt])
            dve("tensor_mul", MID[:, oi * T + tt * 512: oi * T + tt * 512 + 512],
                GS[tt], PU[b][tt][:, 0:512], inc="dv")
            gs_free[tt] = S.ctr["dv"]
        mid_dv[b] = S.ctr["dv"]
    mid_done_dv = S.ctr["dv"]
    if "d_mid" in dbg_out:
        S.wait("sp", "dv", mid_done_dv)
        for oi in range(IC):
            dma("sp", dbg_out["d_mid"][oi], MID[:, oi * T:(oi + 1) * T], "dbg")

    S.barrier()
    WD2 = [sba("WD20", [128, 2048], BF16, D0 + 163968),
           sba("WD21", [128, 2048], BF16, D0 + 168064),
           sba("WD22", [128, 2048], BF16, D0)]
    PD = [[PS[0], PS[1]], [PS[2], PS[3]]]
    XCH = [sba(f"XCHe{i}", [128, T], F32, D0 + 4224 + i * 4096) for i in range(2)]
    OCH = [sba(f"OCH{i}", [128, T], F32, D0 + 12544 + i * 4096) for i in range(2)]
    wdn_wd = {}
    wd_pe_free = [0, 0, 0]
    out_dv = [0, 0]
    out_od = [None, None]
    for i in range(3):
        wdn_wd[i] = dma("pool", WD2[i], wdn_h[i], f"wn{i}")
    for oc2 in range(C):
        b = oc2 % 2
        S.wait("sp", "dv", out_dv[b])
        xd_tok = dma("sp", XCH[b], x1_spill[oc2], f"xch{b}")
        S.wait("pe", "dv", out_dv[b])
        for kcg in range(4):
            idx = oc2 * 4 + kcg
            S.waitt("pe", wdn_wd[idx])
            S.wait("pe", "dv", mid_done_dv)
            for kc in range(C):
                ic = kcg * 16 + kc
                for tt in range(2):
                    mm(PD[b][tt][:, 0:512], WD2[idx % 3][:, kc * 128:(kc + 1) * 128],
                       MID[:, ic * T + tt * 512: ic * T + tt * 512 + 512],
                       start=(kcg == 0 and kc == 0), stop=(kcg == 3 and kc == C - 1),
                       inc=("pe" if (kc == C - 1 and tt == 1) else None))
            wd_pe_free[idx % 3] = S.ctr["pe"]
            if idx + 3 < IC:
                S.wait("pool", "pe", wd_pe_free[idx % 3])
                wdn_wd[idx + 3] = dma("pool", WD2[idx % 3], wdn_h[idx + 3], f"wn{idx % 3}")
        pe_dn = S.ctr["pe"]
        S.waitt("dve", xd_tok)
        S.wait("dve", "pe", pe_dn)
        S.waitt("dve", out_od[b])
        dve("tensor_add", OCH[b][:, 0:512], XCH[b][:, 0:512], PD[b][0][:, 0:512])
        dve("tensor_add", OCH[b][:, 512:1024], XCH[b][:, 512:1024], PD[b][1][:, 0:512],
            inc="dv")
        out_dv[b] = S.ctr["dv"]
        S.wait("sp", "dv", out_dv[b])
        out_od[b] = dma("sp", out_h[oc2], OCH[b], f"od{b}")
    S.wait("sp", "od0", S.ctr.get("od0", 0))
    S.wait("sp", "od1", S.ctr.get("od1", 0))

    with contextlib.ExitStack() as st:
        for name in list(S.sems):
            S.sems[name] = st.enter_context(nc.semaphore(name))
        with nc.Block() as block:
            S.replay(block)
    return nc


def _prep(inputs):
    f32 = np.float32
    _ = inputs.get("L0_attn_tm_decay")
    x = np.asarray(inputs["L0_attn_x"], f32)[0]
    Wk = np.asarray(inputs["L0_attn_Wk"], f32)
    Wv = np.asarray(inputs["L0_attn_Wv"], f32)
    Wr = np.asarray(inputs["L0_attn_Wr"], f32)
    Wo = np.asarray(inputs["L0_attn_Wo"], f32)
    Wkey = np.asarray(inputs["L0_attn_Wkey"], f32)
    Wval = np.asarray(inputs["L0_attn_Wval"], f32)
    Wcr = np.asarray(inputs["L0_attn_Wcr"], f32)
    Wgu = np.asarray(inputs["L0_ffn_w_gate_up"], f32)
    Wdn = np.asarray(inputs["L0_ffn_w_down"], f32)

    A = np.stack([Wk, Wv, Wr])
    wkvr = np.ascontiguousarray(
        A.reshape(3, 16, 128, 16, 128).transpose(1, 4, 0, 3, 2).reshape(16, 128, 3 * 2048))
    wo = np.ascontiguousarray(
        Wo.reshape(16, 128, 16, 128).transpose(0, 3, 2, 1).reshape(16, 128, 2048))
    wkey = np.ascontiguousarray(
        Wkey.reshape(64, 128, 16, 128).transpose(0, 3, 2, 1).reshape(64, 128, 2048))
    wval = np.ascontiguousarray(
        Wval.reshape(16, 128, 4, 16, 128).transpose(0, 2, 4, 3, 1).reshape(64, 128, 2048))
    G = Wgu[:8192].reshape(64, 128, 16, 128)
    Uu = Wgu[8192:].reshape(64, 128, 16, 128)
    wgu = np.ascontiguousarray(
        np.stack([G, Uu]).transpose(1, 4, 0, 3, 2).reshape(64, 128, 4096))
    wdn = np.ascontiguousarray(
        Wdn.reshape(16, 128, 4, 16, 128).transpose(0, 2, 4, 3, 1).reshape(64, 128, 2048))
    wcr = np.ascontiguousarray(
        Wcr.reshape(16, 128, 16, 128).transpose(0, 3, 2, 1).reshape(16, 128, 2048))

    vec_list = [
        inputs["L0_attn_ln1_g"], inputs["L0_attn_ln1_b"],
        inputs["L0_attn_ln2_g"], inputs["L0_attn_ln2_b"],
        inputs["L0_attn_tm_first"],
        np.asarray(inputs["L0_attn_tm_mk"]).reshape(-1),
        np.asarray(inputs["L0_attn_tm_mv"]).reshape(-1),
        np.asarray(inputs["L0_attn_tm_mr"]).reshape(-1),
        np.asarray(inputs["L0_attn_cm_mk"]).reshape(-1),
        np.asarray(inputs["L0_attn_cm_mr"]).reshape(-1),
    ]
    V = np.stack([np.asarray(v, f32).reshape(2048) for v in vec_list])
    vecs = np.zeros((128, NVEC), f32)
    vecs[:, :160] = V.reshape(10, 16, 128).transpose(2, 0, 1).reshape(128, 160)
    vecs[:, V_EPS] = EPS
    vecs[:, V_NHALF] = -0.5
    vecs[:, V_NONE] = -1.0
    vecs[:, V_HALF] = 0.5
    vecs[:, V_NEG] = NEG
    vecs = np.ascontiguousarray(vecs)

    x_fm = np.ascontiguousarray(x.T)
    in_maps = []
    for ci in range(NCORES):
        s = ci * T
        sl = np.zeros((2048, 1026), f32)
        if s - 2 < 0:
            sl[:, 2:] = x_fm[:, s:s + T]
        else:
            sl[:, :] = x_fm[:, s - 2:s + T]
        ximg = np.ascontiguousarray(
            sl.reshape(16, 128, 1026).transpose(1, 0, 2).reshape(128, 16 * 1026))
        hm = np.full((128, 1), 0.0 if ci == 0 else 1.0, f32)
        in_maps.append(dict(
            x_img=ximg, wkvr=wkvr, wo=wo, wkey=wkey, wval=wval,
            wgu=wgu, wdn=wdn, wcr=wcr, vecs=vecs, hmask=hm))
    return in_maps


def kernel(**inputs):
    in_maps = _prep(inputs)
    nc = _build()
    res = run_bass_kernel_spmd(nc, in_maps, core_ids=list(range(NCORES)))
    parts = []
    for ci in range(NCORES):
        o = np.asarray(res.results[ci]["out"], np.float32)
        parts.append(o.reshape(2048, T))
    full = np.concatenate(parts, axis=1)
    return np.ascontiguousarray(full.T).reshape(1, 8192, 2048).astype(np.float32)


# version 14
# speedup vs baseline: 1.1373x; 1.0058x over previous
import contextlib
import numpy as np
import concourse.bass as bass
import concourse.mybir as mybir
from concourse.bass_utils import run_bass_kernel_spmd

F32 = mybir.dt.float32
BF16 = mybir.dt.bfloat16
AF = mybir.ActivationFunctionType
ALU = mybir.AluOpType

NCORES = 8
T = 1024
C = 16
IC = 64
H = 2048
EPS = 1e-5
NEG = -1e38

V_G1, V_B1, V_G2, V_B2, V_TMF, V_MK, V_MV, V_MR, V_CMK, V_CMR = (
    0, 16, 32, 48, 64, 80, 96, 112, 128, 144)
V_EPS, V_NHALF, V_NONE, V_HALF, V_NEG = 160, 161, 162, 163, 164
NVEC = 168

ROWS3 = [0, 32, 64]


def _tiles(n, cap=512):
    out, s = [], 0
    while s < n:
        l = min(cap, n - s)
        out.append((s, l))
        s += l
    return out


class Sched:
    ENGS = ("pe", "dve", "act", "sp", "pool")

    def __init__(self):
        self.items = {e: [] for e in self.ENGS}
        self.sems = {}
        self.ctr = {}
        self.hiw = {}

    def sem(self, name):
        if name not in self.sems:
            self.sems[name] = None
            self.ctr[name] = 0

    def emit(self, eng, fn, inc=None, incn=1):
        if inc is not None:
            self.ctr[inc] += incn
        self.items[eng].append((fn, inc, incn))
        return self.ctr[inc] if inc is not None else None

    def wait(self, eng, sem, val):
        if val is None or val <= 0:
            return
        key = (eng, sem)
        if self.hiw.get(key, 0) >= val:
            return
        self.hiw[key] = val

        def f(e, sem=sem, val=val):
            e.wait_ge(self.sems[sem], val)
            return None
        self.items[eng].append((f, None, 1))

    def barrier(self, engs=None):
        vals = [(s, self.ctr[s]) for s in self.ctr]
        for eng in (engs or self.ENGS):
            for s, v in vals:
                self.wait(eng, s, v)

    def waitt(self, eng, tok):
        if tok:
            self.wait(eng, tok[0], tok[1])

    def replay(self, block):
        def mk(name):
            def run(e):
                for fn, inc, incn in self.items[name]:
                    inst = fn(e)
                    if inc is not None:
                        inst.then_inc(self.sems[inc], incn)
            return run
        block.gpsimd(mk("pool"))
        block.sync(mk("sp"))
        block.tensor(mk("pe"))
        block.vector(mk("dve"))
        block.scalar(mk("act"))


def _build(dbg=()):
    nc = bass.Bass("TRN2", target_bir_lowering=False)
    dbg = set(dbg)

    x_img = nc.dram_tensor("x_img", [128, C * 1026], F32, kind="ExternalInput").ap()
    wkvr_h = nc.dram_tensor("wkvr", [C, 128, 3 * 2048], F32, kind="ExternalInput").ap()
    wo_h = nc.dram_tensor("wo", [C, 128, 2048], F32, kind="ExternalInput").ap()
    wkey_h = nc.dram_tensor("wkey", [IC, 128, 2048], F32, kind="ExternalInput").ap()
    wval_h = nc.dram_tensor("wval", [IC, 128, 2048], F32, kind="ExternalInput").ap()
    wgu_h = nc.dram_tensor("wgu", [IC, 128, 4096], F32, kind="ExternalInput").ap()
    wdn_h = nc.dram_tensor("wdn", [IC, 128, 2048], F32, kind="ExternalInput").ap()
    wcr_h = nc.dram_tensor("wcr", [C, 128, 2048], F32, kind="ExternalInput").ap()
    vecs_h = nc.dram_tensor("vecs", [128, NVEC], F32, kind="ExternalInput").ap()
    hmask_h = nc.dram_tensor("hmask", [128, 1], F32, kind="ExternalInput").ap()
    out_h = nc.dram_tensor("out", [C, 128, T], F32, kind="ExternalOutput").ap()

    def scratch(name, shape, dt):
        kind = "ExternalOutput" if name in dbg else "Internal"
        return nc.dram_tensor(name, shape, dt, kind=kind).ap()

    xm_spill = scratch("xm_sp", [C, 128, 1025], F32)
    cmr_spill = scratch("cmr_sp", [C, 128, T], BF16)
    x1_spill = scratch("x1_sp", [C, 128, T], F32)
    kv_spill = scratch("kv_sp", [C, 128, T], F32)

    dbg_out = {}
    for nm, shape, dt in [
        ("d_bca", [128, 1026], F32), ("d_bcb", [128, 1026], F32),
        ("d_xk", [C, 128, 1025], BF16), ("d_xv", [C, 128, 1025], BF16),
        ("d_xr", [C, 128, 1025], BF16), ("d_u", [C, 128, 1025], BF16),
        ("d_cmk", [C, 128, T], BF16), ("d_h3", [C, 128, T], BF16),
        ("d_silu", [IC, 128, T], BF16), ("d_mid", [IC, 128, T], BF16),
        ("d_k", [C, 128, T], F32), ("d_v", [C, 128, T], F32),
        ("d_r", [C, 128, T], F32), ("d_stv", [6, 1, 1026], F32),
        ("d_hkvr", [128, 192], F32),
    ]:
        if nm in dbg:
            dbg_out[nm] = nc.dram_tensor(nm, shape, dt, kind="ExternalOutput").ap()

    S = Sched()
    for s in ("pe", "dv", "ac", "pl"):
        S.sem(s)

    BASE = (nc.sbuf_base + 127) // 128 * 128
    TOP = nc.sbuf_base + nc.sbuf_bytes_remaining

    def sba(name, shape, dtype, off):
        a = BASE + off
        nb = int(np.prod(shape[1:])) * (2 if dtype == BF16 else 4)
        assert a + nb <= TOP, (name, a + nb, TOP)
        return nc.alloc_sbuf_tensor_at(name, shape, dtype, offset=a).ap()

    VEC = sba("VEC", [128, NVEC], F32, 0)
    HM = sba("HM", [128, 1], F32, 768)
    ONESB = sba("ONESB", [128, 1], BF16, 896)
    ONESC = sba("ONESC", [1, 128], F32, 1024)
    STV = [sba(f"STV{i}", [1, 1026], F32, 1536 + i * 4224) for i in range(6)]
    SL_A0, SL_B0, SL_A, SL_B, SL_T0, SL_T1 = range(6)
    SC_H = sba("SC_H", [128, 1026], F32, 26880)
    SC_D = sba("SC_D", [128, 1026], F32, 31104)
    D0 = 35328

    def stv(i, n=1026):
        return STV[i][0:1, 0:n]

    def vcol(base, fc):
        return VEC[:, base + fc: base + fc + 1]

    PS = [nc.alloc_psum_tensor(f"PSB{i}", [128, 512], F32).ap() for i in range(8)]

    def dma(eng, dst, src, sem):
        S.sem(sem)

        def f(e, dst=dst, src=src):
            return e.dma_start(dst, src)
        return (sem, S.emit(eng, f, inc=sem, incn=16))

    def mm(psum, lhsT, rhs, start, stop, inc=None):
        def f(e, psum=psum, lhsT=lhsT, rhs=rhs, start=start, stop=stop):
            return e.matmul(psum, lhsT, rhs, start=start, stop=stop)
        return S.emit("pe", f, inc=inc)

    def act(out, in_, func, bias=0.0, scale=1.0, inc=None):
        def f(e, out=out, in_=in_, func=func, bias=bias, scale=scale):
            return e.activation(out, in_, func, bias=bias, scale=scale)
        return S.emit("act", f, inc=inc)

    def dve(op, *args, inc=None):
        def f(e, op=op, args=args):
            return getattr(e, op)(*args)
        return S.emit("dve", f, inc=inc)

    def pool(op, *args, inc=None):
        def f(e, op=op, args=args):
            return getattr(e, op)(*args)
        return S.emit("pool", f, inc=inc)

    def stats_finish(pa, pb, ncols, slot_a, slot_b, pe_done):
        tl = _tiles(ncols)
        S.wait("act", "pe", pe_done)
        for ti, (s, l) in enumerate(tl):
            act(stv(SL_T0)[0:1, s:s + l], pa[ROWS3[ti]:ROWS3[ti] + 1, 0:l], AF.Identity)
        for ti, (s, l) in enumerate(tl):
            act(stv(SL_T1)[0:1, s:s + l], pb[ROWS3[ti]:ROWS3[ti] + 1, 0:l], AF.Identity,
                inc=("ac" if ti == len(tl) - 1 else None))
        S.wait("dve", "ac", S.ctr["ac"])
        S.wait("dve", "pe", pe_done)
        m = stv(SL_T0, ncols)
        q = stv(SL_T1, ncols)
        a = stv(slot_a, ncols)
        b = stv(slot_b, ncols)
        dve("tensor_scalar", m, m, 1.0 / H, None, ALU.mult)
        dve("tensor_scalar", q, q, 1.0 / H, None, ALU.mult)
        dve("tensor_mul", b, m, m)
        dve("tensor_sub", q, q, b, inc="dv")
        S.wait("act", "dv", S.ctr["dv"])
        act(a, q, AF.Ln, bias=VEC[0:1, V_EPS:V_EPS + 1])
        act(a, a, AF.Exp, scale=VEC[0:1, V_NHALF:V_NHALF + 1], inc="ac")
        S.wait("dve", "ac", S.ctr["ac"])
        dve("scalar_tensor_tensor", b, m, -1.0, a, ALU.mult, ALU.mult, inc="dv")
        return S.ctr["dv"]

    def broadcast(vec_ap, dst, ncols, dv_ready, psb):
        S.wait("pe", "dv", dv_ready)
        last = None
        for (s, l) in _tiles(ncols):
            S.wait("pe", "ac", S.ctr["ac"])
            mm(psb[:, 0:l], ONESC, vec_ap[0:1, s:s + l], start=True, stop=True, inc="pe")
            S.wait("act", "pe", S.ctr["pe"])
            last = act(dst[:, s:s + l], psb[:, 0:l], AF.Identity, inc="ac")
        return last

    M2 = sba("M2", [128, 512], F32, 209920)
    dma("sp", VEC, vecs_h, "ld")
    hm_tok = dma("sp", HM, hmask_h, "ld")
    for e_ in ("act", "dve", "pool"):
        S.wait(e_, "ld", 32)
    dve("memset", ONESB, 1.0)
    dve("memset", ONESC, 1.0, inc="dv")
    S.wait("pe", "dv", S.ctr["dv"])
    dve("memset", M2, 1.0, inc="dv")
    S.wait("dve", "dv", S.ctr["dv"])
    dve("tensor_copy", M2[:, 0:1], HM, inc="dv")
    S.wait("dve", "dv", S.ctr["dv"])
    dve("tensor_copy", M2[:, 1:2], HM, inc="dv")

    XBF = sba("XBF", [128, C * 1026], BF16, D0)
    xbf_tok = dma("pool", XBF, x_img, "xbf")

    SQ = [sba(f"SQ{i}", [128, 1026], BF16, D0 + 33024 + i * 2176) for i in range(2)]
    pa, pb = PS[0], PS[1]
    tl6 = _tiles(1026)
    sq_pe_free = [0, 0]
    for fc in range(C):
        b = fc % 2
        xv = XBF[:, fc * 1026:(fc + 1) * 1026]
        S.waitt("act", xbf_tok)
        S.wait("act", "pe", sq_pe_free[b])
        act(SQ[b], xv, AF.Square, inc="ac")
        sq_ac = S.ctr["ac"]
        S.waitt("pe", xbf_tok)
        S.wait("pe", "ac", sq_ac)
        for ti, (s, l) in enumerate(tl6):
            mm(pa[ROWS3[ti]:ROWS3[ti] + 1, 0:l], ONESB, xv[:, s:s + l],
               start=(fc == 0), stop=(fc == C - 1))
        for ti, (s, l) in enumerate(tl6):
            mm(pb[ROWS3[ti]:ROWS3[ti] + 1, 0:l], ONESB, SQ[b][:, s:s + l],
               start=(fc == 0), stop=(fc == C - 1),
               inc=("pe" if ti == len(tl6) - 1 else None))
        sq_pe_free[b] = S.ctr["pe"]
    a0_dv = stats_finish(pa, pb, 1026, SL_A0, SL_B0, S.ctr["pe"])

    BC_A = sba("BC_A1", [128, 1026], F32, D0 + 140032)
    BC_B = sba("BC_B1", [128, 1026], F32, D0 + 144256)
    XCH = [sba(f"XCHa{i}", [128, 1026], F32, D0 + 131584 + i * 4224) for i in range(2)]
    xch_free = [0, 0]
    S.wait("dve", "dv", a0_dv)
    dve("tensor_mul", stv(SL_A), stv(SL_T1), stv(SL_A0))
    dve("tensor_mul", stv(SL_A), stv(SL_A), stv(SL_A0), inc="dv")
    S.wait("act", "dv", S.ctr["dv"])
    act(stv(SL_A), stv(SL_A), AF.Ln, bias=VEC[0:1, V_EPS:V_EPS + 1])
    act(stv(SL_A), stv(SL_A), AF.Exp, scale=VEC[0:1, V_NHALF:V_NHALF + 1], inc="ac")
    S.wait("dve", "ac", S.ctr["ac"])
    dve("tensor_mul", stv(SL_T1), stv(SL_A0), stv(SL_A))
    dve("tensor_mul", stv(SL_T0), stv(SL_B0), stv(SL_A), inc="dv")
    a01_dv = S.ctr["dv"]
    broadcast(stv(SL_T1), BC_A, 1026, a01_dv, PS[2])
    bc_ac = broadcast(stv(SL_T0), BC_B, 1026, a01_dv, PS[2])

    XK = sba("XK", [128, C * 1025], BF16, D0)
    XV = sba("XV", [128, C * 1025], BF16, D0 + 32896)
    XR = sba("XR", [128, C * 1025], BF16, D0 + 65792)
    WKVR = [sba(f"WKVR{i}", [128, 3 * 2048], BF16, D0 + 148480 + i * 12288)
            for i in range(2)]
    wkvr_wd = {}
    wkvr_wd[0] = dma("pool", WKVR[0], wkvr_h[0], "wk0")
    wkvr_wd[1] = dma("pool", WKVR[1], wkvr_h[1], "wk1")

    SC_H2 = sba("SC_H2", [128, 1026], F32, D0 + 98688)
    SC_HX = [SC_H, SC_H2]
    hx_free = [0, 0]
    S.wait("sp", "dv", max(xch_free))
    xch_free = [0, 0]
    for fc in range(C):
        b = fc % 2
        S.wait("sp", "pl", xch_free[b])
        xd_tok = dma("sp", XCH[b], x_img[:, fc * 1026:(fc + 1) * 1026], f"xch{b}")
        hx = SC_HX[b]
        S.waitt("pool", xd_tok)
        S.wait("pool", "ac", bc_ac)
        S.wait("pool", "dv", hx_free[b])
        pool("tensor_mul", hx, XCH[b], BC_A)
        pool("tensor_add", hx, hx, BC_B, inc="pl")
        xch_free[b] = S.ctr["pl"]
        S.wait("act", "pl", S.ctr["pl"])
        act(hx, hx, AF.Identity, scale=vcol(V_G1, fc), bias=vcol(V_B1, fc), inc="ac")
        S.wait("dve", "ac", S.ctr["ac"])
        dve("tensor_mul", hx[:, 0:512], hx[:, 0:512], M2)
        dve("tensor_sub", SC_D[:, 0:1025], hx[:, 1:1026], hx[:, 0:1025])
        o = fc * 1025
        dve("scalar_tensor_tensor", XK[:, o:o + 1025], SC_D[:, 0:1025], vcol(V_MK, fc),
            hx[:, 0:1025], ALU.mult, ALU.add)
        dve("scalar_tensor_tensor", XV[:, o:o + 1025], SC_D[:, 0:1025], vcol(V_MV, fc),
            hx[:, 0:1025], ALU.mult, ALU.add)
        dve("scalar_tensor_tensor", XR[:, o:o + 1025], SC_D[:, 0:1025], vcol(V_MR, fc),
            hx[:, 0:1025], ALU.mult, ALU.add, inc="dv")
        hx_free[b] = S.ctr["dv"]
    mixes_dv = S.ctr["dv"]
    mixes_pl = S.ctr["pl"]
    if "d_xk" in dbg_out:
        S.wait("sp", "dv", mixes_dv)
        for fc in range(C):
            dma("sp", dbg_out["d_xk"][fc], XK[:, fc * 1025:(fc + 1) * 1025], "dbg")
            dma("sp", dbg_out["d_xv"][fc], XV[:, fc * 1025:(fc + 1) * 1025], "dbg")
            dma("sp", dbg_out["d_xr"][fc], XR[:, fc * 1025:(fc + 1) * 1025], "dbg")
    if "d_bca" in dbg_out:
        S.wait("sp", "ac", bc_ac)
        dma("sp", dbg_out["d_bca"], BC_A, "dbg")
        dma("sp", dbg_out["d_bcb"], BC_B, "dbg")

    S.barrier()
    U = sba("U", [128, C * 1025], BF16, D0 + 98688)
    CH = [sba(f"CH{i}", [128, 512], F32, D0 + 131584 + i * 2048) for i in range(8)]
    PSK = [PS[0], PS[1]]
    PSV = [PS[2], PS[3]]
    PSR = [PS[4], PS[5]]
    HK, HV, HR = PS[0][:, 0:1], PS[2][:, 0:1], PS[4][:, 0:1]
    HW = PS[6][:, 0:1]
    HSTX = sba("HSTX", [128, 192], F32, 209152)
    HST = HSTX[:, 0:48]
    SPC = SC_H[:, 0:512]

    chain_dv_set = [0, 0]
    wkvr_pe_free = [0, 0]
    halo_stage_ac = 0
    for oc in range(C):
        wb = WKVR[oc % 2]
        S.waitt("pe", wkvr_wd[oc])
        S.wait("pe", "dv", mixes_dv)
        for tt in range(2):
            S.wait("pe", "dv", chain_dv_set[tt])
            if tt == 0:
                S.wait("pe", "ac", halo_stage_ac)
            tok0 = 1 + tt * 512
            for mi, dst, mixsrc in ((0, PSK, XK), (1, PSV, XV), (2, PSR, XR)):
                for kc in range(C):
                    mm(dst[tt][:, 0:512],
                       wb[:, (mi * 16 + kc) * 128:(mi * 16 + kc + 1) * 128],
                       mixsrc[:, kc * 1025 + tok0: kc * 1025 + tok0 + 512],
                       start=(kc == 0), stop=(kc == C - 1),
                       inc=("pe" if kc == C - 1 else None))
            pe_kvr = S.ctr["pe"]
            c0, c1, c2, c3 = CH[tt * 4: tt * 4 + 4]
            uo = U[:, oc * 1025 + tok0: oc * 1025 + tok0 + 512]
            S.wait("act", "pe", pe_kvr - 2)
            act(c0, PSK[tt][:, 0:512], AF.Identity, bias=vcol(V_TMF, oc), inc="ac")
            S.wait("dve", "ac", S.ctr["ac"])
            dve("tensor_scalar", c1, c0, VEC[:, V_NEG:V_NEG + 1], None, ALU.max)
            dve("tensor_sub", c0, c0, c1, inc="dv")
            S.wait("act", "dv", S.ctr["dv"])
            act(c1, c0, AF.Exp)
            act(c2, c0, AF.Exp, scale=VEC[:, V_NONE:V_NONE + 1])
            S.wait("act", "pe", pe_kvr)
            act(c3, PSR[tt][:, 0:512], AF.Tanh, scale=VEC[:, V_HALF:V_HALF + 1], inc="ac")
            S.wait("dve", "ac", S.ctr["ac"])
            S.wait("dve", "pe", pe_kvr)
            dve("scalar_tensor_tensor", c0, PSV[tt][:, 0:512], 0.5, c1, ALU.mult, ALU.mult)
            dve("tensor_mul", c0, c0, c2)
            dve("scalar_tensor_tensor", uo, c3, 1.0, c0, ALU.add, ALU.mult, inc="dv")
            chain_dv_set[tt] = S.ctr["dv"]
            if "d_k" in dbg_out:
                DBG1, DBG2, DBG3 = SC_H[:, 0:512], SC_H[:, 513:1025], SC_D[:, 514:1026]
                S.wait("act", "dv", chain_dv_set[tt])
                act(DBG1, PSK[tt][:, 0:512], AF.Identity)
                act(DBG2, PSV[tt][:, 0:512], AF.Identity)
                act(DBG3, PSR[tt][:, 0:512], AF.Identity, inc="ac")
                S.wait("sp", "ac", S.ctr["ac"])
                dma("sp", dbg_out["d_k"][oc][:, tt * 512:tt * 512 + 512], DBG1, "dbg")
                dma("sp", dbg_out["d_v"][oc][:, tt * 512:tt * 512 + 512], DBG2, "dbg")
                dv_dump = dma("sp", dbg_out["d_r"][oc][:, tt * 512:tt * 512 + 512], DBG3, "dbg")
                S.waitt("pe", dv_dump)
                S.waitt("dve", dv_dump)
        S.wait("pe", "ac", halo_stage_ac)
        S.wait("pe", "dv", chain_dv_set[0])
        for mi, dsth, mixsrc in ((0, HK, XK), (1, HV, XV), (2, HR, XR)):
            for kc in range(C):
                mm(dsth, wb[:, (mi * 16 + kc) * 128:(mi * 16 + kc + 1) * 128],
                   mixsrc[:, kc * 1025: kc * 1025 + 1],
                   start=(kc == 0), stop=(kc == C - 1),
                   inc=("pe" if (mi == 2 and kc == C - 1) else None))
        pe_halo = S.ctr["pe"]
        wkvr_pe_free[oc % 2] = pe_halo
        if oc + 2 < C:
            S.wait("pool", "pe", pe_halo)
            wkvr_wd[oc + 2] = dma("pool", WKVR[oc % 2], wkvr_h[oc + 2], f"wk{oc % 2}")
        S.wait("act", "pe", pe_halo)
        act(HST[:, oc:oc + 1], HK, AF.Identity)
        act(HST[:, 16 + oc:16 + oc + 1], HV, AF.Identity)
        act(HST[:, 32 + oc:32 + oc + 1], HR, AF.Identity, inc="ac")
        halo_stage_ac = S.ctr["ac"]
    W16 = HSTX[:, 48:64]
    P16 = HSTX[:, 64:80]
    E16 = HSTX[:, 80:96]
    R16 = HSTX[:, 96:112]
    T16 = HSTX[:, 112:128]
    N16 = HSTX[:, 128:144]
    UH16 = HSTX[:, 144:160]
    act(SPC, SPC, AF.Identity, inc="ac")
    S.wait("dve", "ac", S.ctr["ac"])
    dve("tensor_add", W16, HST[:, 0:16], VEC[:, V_TMF:V_TMF + 16])
    dve("tensor_copy", SPC, SPC)
    dve("tensor_scalar", P16, W16, VEC[:, V_NEG:V_NEG + 1], None, ALU.max)
    dve("tensor_copy", SPC, SPC)
    dve("tensor_sub", W16, W16, P16)
    dve("tensor_copy", SPC, SPC, inc="dv")
    S.wait("act", "dv", S.ctr["dv"])
    act(E16, W16, AF.Exp)
    act(R16, W16, AF.Exp, scale=VEC[:, V_NONE:V_NONE + 1])
    act(T16, HST[:, 32:48], AF.Tanh, scale=VEC[:, V_HALF:V_HALF + 1])
    act(SPC, SPC, AF.Identity, inc="ac")
    S.wait("dve", "ac", S.ctr["ac"])
    dve("scalar_tensor_tensor", N16, HST[:, 16:32], 0.5, E16, ALU.mult, ALU.mult)
    dve("tensor_copy", SPC, SPC)
    dve("tensor_mul", N16, N16, R16)
    dve("tensor_copy", SPC, SPC)
    dve("scalar_tensor_tensor", UH16, T16, 1.0, N16, ALU.add, ALU.mult)
    dve("tensor_copy", SPC, SPC)
    for oc in range(C):
        dve("tensor_copy", U[:, oc * 1025: oc * 1025 + 1], UH16[:, oc:oc + 1])
    dve("tensor_copy", SPC, SPC, inc="dv")
    if "d_hkvr" in dbg_out:
        S.wait("sp", "dv", S.ctr["dv"])
        dma("sp", dbg_out["d_hkvr"], HSTX, "dbg")
    u_done_dv = S.ctr["dv"]
    if "d_u" in dbg_out:
        S.wait("sp", "dv", u_done_dv)
        for fc in range(C):
            dma("sp", dbg_out["d_u"][fc], U[:, fc * 1025:(fc + 1) * 1025], "dbg")

    S.barrier()
    BC_A = sba("BC_A2", [128, 1026], F32, D0 + 165376)
    BC_B = sba("BC_B2", [128, 1026], F32, D0 + 169600)
    broadcast(stv(SL_A0), BC_A, 1026, a0_dv, PS[7])
    bc_ac = broadcast(stv(SL_B0), BC_B, 1026, a0_dv, PS[7])

    XCH = [sba(f"XCHb{i}", [128, 1026], F32, D0 + 139776 + i * 4224) for i in range(2)]
    WO = [sba(f"WO{i}", [128, 2048], BF16, D0 + 131584 + i * 4096) for i in range(2)]
    PSW = [[PS[0], PS[1]], [PS[2], PS[3]]]
    XMCH = [sba(f"XMCH{i}", [128, 1025], F32, D0 + 148224 + i * 4224) for i in range(2)]
    XMBF = [sba(f"XMBF{i}", [128, 1025], BF16, D0 + 156672 + i * 2176) for i in range(2)]
    SQ5 = [sba(f"SQ5{i}", [128, 1025], BF16, D0 + 161024 + i * 2176) for i in range(2)]
    pa, pb = PS[4], PS[5]
    tl5 = _tiles(1025)

    wo_wd = {}
    wo_wd[0] = dma("pool", WO[0], wo_h[0], "wwo0")
    wo_wd[1] = dma("pool", WO[1], wo_h[1], "wwo1")
    wo_pe_free = [0, 0]
    xm_halo_dv_last = 0
    xm_dv = [0, 0]
    xm_spilled = [None, None]
    xm_stats_pe = [0, 0]
    xch_free = [0, 0]
    sq5_ac = [0, 0]
    for oc in range(C):
        b = oc % 2
        S.waitt("pe", wo_wd[oc])
        S.wait("pe", "dv", u_done_dv)
        S.wait("pe", "dv", xm_dv[b])
        for tt in range(2):
            tok0 = 1 + tt * 512
            for kc in range(C):
                mm(PSW[b][tt][:, 0:512], WO[b][:, kc * 128:(kc + 1) * 128],
                   U[:, kc * 1025 + tok0: kc * 1025 + tok0 + 512],
                   start=(kc == 0), stop=(kc == C - 1),
                   inc=("pe" if kc == C - 1 else None))
        S.wait("pe", "dv", xm_halo_dv_last)
        for kc in range(C):
            mm(HW, WO[b][:, kc * 128:(kc + 1) * 128],
               U[:, kc * 1025: kc * 1025 + 1],
               start=(kc == 0), stop=(kc == C - 1),
               inc=("pe" if kc == C - 1 else None))
        pe_wo = S.ctr["pe"]
        wo_pe_free[b] = pe_wo
        if oc + 2 < C:
            S.wait("pool", "pe", pe_wo)
            wo_wd[oc + 2] = dma("pool", WO[b], wo_h[oc + 2], f"wwo{b}")
        S.wait("sp", "dv", xch_free[b])
        xd_tok = dma("sp", XCH[b], x_img[:, oc * 1026:(oc + 1) * 1026], f"xch{b}")
        S.waitt("dve", xd_tok)
        S.wait("dve", "ac", bc_ac)
        S.waitt("dve", xm_spilled[b])
        S.wait("dve", "ac", sq5_ac[b])
        dve("tensor_mul", XMCH[b], XCH[b][:, 1:1026], BC_A[:, 1:1026])
        dve("tensor_add", XMCH[b], XMCH[b], BC_B[:, 1:1026], inc="dv")
        xch_free[b] = S.ctr["dv"]
        S.wait("dve", "pe", pe_wo)
        dve("tensor_add", XMCH[b][:, 0:1], XMCH[b][:, 0:1], HW, inc="dv")
        xm_halo_dv_last = S.ctr["dv"]
        dve("tensor_add", XMCH[b][:, 1:513], XMCH[b][:, 1:513], PSW[b][0][:, 0:512])
        dve("tensor_add", XMCH[b][:, 513:1025], XMCH[b][:, 513:1025], PSW[b][1][:, 0:512])
        S.wait("dve", "pe", xm_stats_pe[b])
        dve("tensor_copy", XMBF[b], XMCH[b], inc="dv")
        xm_dv[b] = S.ctr["dv"]
        S.wait("act", "dv", xm_dv[b])
        S.wait("act", "pe", xm_stats_pe[b])
        act(SQ5[b], XMCH[b], AF.Square, inc="ac")
        ac_sq = S.ctr["ac"]
        sq5_ac[b] = ac_sq
        S.wait("act", "dv", xm_dv[b])
        xm_spilled[b] = dma("act", xm_spill[oc], XMCH[b], f"sxm{b}")
        S.wait("pe", "dv", xm_dv[b])
        S.wait("pe", "ac", ac_sq)
        for ti, (s, l) in enumerate(tl5):
            mm(pa[ROWS3[ti]:ROWS3[ti] + 1, 0:l], ONESB, XMBF[b][:, s:s + l],
               start=(oc == 0), stop=(oc == C - 1))
        for ti, (s, l) in enumerate(tl5):
            mm(pb[ROWS3[ti]:ROWS3[ti] + 1, 0:l], ONESB, SQ5[b][:, s:s + l],
               start=(oc == 0), stop=(oc == C - 1),
               inc=("pe" if ti == len(tl5) - 1 else None))
        xm_stats_pe[b] = S.ctr["pe"]
    a2_dv = stats_finish(pa, pb, 1025, SL_A, SL_B, S.ctr["pe"])
    broadcast(stv(SL_A, 1025), BC_A[:, 0:1025], 1025, a2_dv, PS[7])
    bc_ac = broadcast(stv(SL_B, 1025), BC_B[:, 0:1025], 1025, a2_dv, PS[7])

    S.barrier()
    CMK = sba("CMK", [128, C * T], BF16, D0)
    CMR = [sba(f"CMR{i}", [128, T], BF16, D0 + 32896 + i * 2048) for i in range(2)]
    SC_H2b = sba("SC_H2b", [128, 1026], F32, D0 + 73856)
    SC_H2c = sba("SC_H2c", [128, 1026], F32, D0 + 78080)
    XCHb3 = sba("XCHb3", [128, 1026], F32, D0 + 82304)
    CMR.append(sba("CMR2", [128, T], BF16, D0 + 36992))
    SC_HX = [SC_H, SC_H2b, SC_H2c]
    XCH6 = [XCH[0], XCH[1], XCHb3]
    hx_free = [0, 0, 0]
    S.wait("sp", "dv", max(xch_free))
    xch_free = [0, 0, 0]
    cmr_spilled = [None, None, None]
    spill_q = []
    for fc in range(C):
        b = fc % 3
        S.wait("sp", "pl", xch_free[b])
        xd_tok = dma("sp", XCH6[b][:, 0:1025], xm_spill[fc], f"xch{b}")
        hx = SC_HX[b]
        S.waitt("pool", xd_tok)
        S.wait("pool", "ac", bc_ac)
        S.wait("pool", "dv", hx_free[b])
        pool("tensor_mul", hx[:, 0:1025], XCH6[b][:, 0:1025], BC_A[:, 0:1025])
        pool("tensor_add", hx[:, 0:1025], hx[:, 0:1025], BC_B[:, 0:1025], inc="pl")
        xch_free[b] = S.ctr["pl"]
        S.wait("act", "pl", S.ctr["pl"])
        act(hx[:, 0:1025], hx[:, 0:1025], AF.Identity,
            scale=vcol(V_G2, fc), bias=vcol(V_B2, fc), inc="ac")
        S.wait("dve", "ac", S.ctr["ac"])
        dve("tensor_mul", hx[:, 0:511], hx[:, 0:511], M2[:, 1:512])
        dve("tensor_sub", SC_D[:, 0:T], hx[:, 1:1025], hx[:, 0:T])
        dve("scalar_tensor_tensor", CMK[:, fc * T:(fc + 1) * T], SC_D[:, 0:T],
            vcol(V_CMK, fc), hx[:, 0:T], ALU.mult, ALU.add)
        S.waitt("dve", cmr_spilled[b])
        dve("scalar_tensor_tensor", CMR[b], SC_D[:, 0:T],
            vcol(V_CMR, fc), hx[:, 0:T], ALU.mult, ALU.add, inc="dv")
        hx_free[b] = S.ctr["dv"]
        spill_q.append((fc, b, S.ctr["dv"]))
        if len(spill_q) >= 2:
            jfc, jb, jdv = spill_q.pop(0)
            S.wait("sp", "dv", jdv)
            cmr_spilled[jb] = dma("sp", cmr_spill[jfc], CMR[jb], f"scm{jb}")
    while spill_q:
        jfc, jb, jdv = spill_q.pop(0)
        S.wait("sp", "dv", jdv)
        cmr_spilled[jb] = dma("sp", cmr_spill[jfc], CMR[jb], f"scm{jb}")
    cmk_done_dv = S.ctr["dv"]
    if "d_cmk" in dbg_out:
        S.wait("sp", "dv", cmk_done_dv)
        for fc in range(C):
            dma("sp", dbg_out["d_cmk"][fc], CMK[:, fc * T:(fc + 1) * T], "dbg")

    S.barrier()
    SILU = sba("SILU", [128, IC * T], BF16, D0 + 32896)
    WK3 = [sba(f"WK3{i}", [128, 2048], BF16, D0 + 163968 + i * 4096) for i in range(2)]
    PSA = [PS[0], PS[1]]
    PSB = [PS[2], PS[3]]
    wkey_wd = {}
    wk_pe_free = [0, 0]
    silu_ac = [0, 0]
    for i in range(2):
        wkey_wd[i] = dma("pool", WK3[i], wkey_h[i], f"wy{i}")
    for oi in range(IC):
        S.waitt("pe", wkey_wd[oi])
        S.wait("pe", "dv", cmk_done_dv)
        ps = (PSA, PSB)[oi % 2]
        S.wait("pe", "ac", silu_ac[oi % 2])
        for tt in range(2):
            for kc in range(C):
                mm(ps[tt][:, 0:512], WK3[oi % 2][:, kc * 128:(kc + 1) * 128],
                   CMK[:, kc * T + tt * 512: kc * T + tt * 512 + 512],
                   start=(kc == 0), stop=(kc == C - 1),
                   inc=("pe" if kc == C - 1 else None))
        pe_kk = S.ctr["pe"]
        wk_pe_free[oi % 2] = pe_kk
        if oi + 2 < IC:
            S.wait("pool", "pe", pe_kk)
            wkey_wd[oi + 2] = dma("pool", WK3[oi % 2], wkey_h[oi + 2], f"wy{oi % 2}")
        S.wait("act", "pe", pe_kk)
        act(SILU[:, oi * T: oi * T + 512], ps[0][:, 0:512], AF.Silu)
        silu_ac[oi % 2] = act(SILU[:, oi * T + 512: oi * T + 1024], ps[1][:, 0:512],
                              AF.Silu, inc="ac")
    silu_done_ac = S.ctr["ac"]
    if "d_silu" in dbg_out:
        S.wait("sp", "ac", silu_done_ac)
        for oi in range(IC):
            dma("sp", dbg_out["d_silu"][oi], SILU[:, oi * T:(oi + 1) * T], "dbg")

    S.barrier()
    WV2 = [sba(f"WV2{i}", [128, 2048], BF16, D0 + 163968 + i * 4096) for i in range(2)]
    PKV = [[PS[0], PS[1]], [PS[2], PS[3]]]
    KVCH = [SC_H[:, 0:T], SC_D[:, 0:T]]
    CMRR = sba("CMRR", [128, C * T], BF16, D0)
    cmr_re_xd = 0
    for fc in range(C):
        cmr_re_xd = dma("sp", CMRR[:, fc * T:(fc + 1) * T], cmr_spill[fc], "cmrr")
    wval_wd = {}
    wv_pe_free = [0, 0]
    kv_evac_ac = [0, 0]
    kv_spilled = [None, None]
    for i in range(2):
        wval_wd[i] = dma("pool", WV2[i], wval_h[i], f"wv{i}")
    for oc2 in range(C):
        b = oc2 % 2
        S.wait("pe", "ac", kv_evac_ac[b])
        for kcg in range(4):
            idx = oc2 * 4 + kcg
            S.waitt("pe", wval_wd[idx])
            S.wait("pe", "ac", silu_done_ac)
            for kc in range(C):
                ic = kcg * 16 + kc
                for tt in range(2):
                    mm(PKV[b][tt][:, 0:512], WV2[idx % 2][:, kc * 128:(kc + 1) * 128],
                       SILU[:, ic * T + tt * 512: ic * T + tt * 512 + 512],
                       start=(kcg == 0 and kc == 0), stop=(kcg == 3 and kc == C - 1),
                       inc=("pe" if (kc == C - 1 and tt == 1) else None))
            wv_pe_free[idx % 2] = S.ctr["pe"]
            if idx + 2 < IC:
                S.wait("pool", "pe", wv_pe_free[idx % 2])
                wval_wd[idx + 2] = dma("pool", WV2[idx % 2], wval_h[idx + 2], f"wv{idx % 2}")
        pe_kv = S.ctr["pe"]
        S.wait("act", "pe", pe_kv)
        S.waitt("act", kv_spilled[b])
        act(KVCH[b][:, 0:512], PKV[b][0][:, 0:512], AF.Identity)
        kv_evac_ac[b] = act(KVCH[b][:, 512:1024], PKV[b][1][:, 0:512], AF.Identity,
                            inc="ac")
        S.wait("sp", "ac", kv_evac_ac[b])
        kv_spilled[b] = dma("sp", kv_spill[oc2], KVCH[b], f"skv{b}")

    S.barrier()
    WC2 = [sba(f"WC2{i}", [128, 2048], BF16, D0 + 32896 + i * 4096) for i in range(2)]
    PCR = [[PS[0], PS[1]], [PS[2], PS[3]]]
    pa, pb = PS[4], PS[5]
    XCH = [sba(f"XCHc{i}", [128, T], F32, D0 + 41088 + i * 4096) for i in range(2)]
    XM2 = [sba(f"XM2{i}", [128, T], F32, D0 + 49280 + i * 4096) for i in range(2)]
    KV2 = [sba(f"KV2{i}", [128, T], F32, D0 + 57472 + i * 4096) for i in range(2)]
    X1CH = [sba(f"X1CH{i}", [128, T], F32, D0 + 65664 + i * 4096) for i in range(2)]
    X1BF = [sba(f"X1BF{i}", [128, T], BF16, D0 + 73856 + i * 2048) for i in range(2)]
    SQ8 = [sba(f"SQ8{i}", [128, T], BF16, D0 + 77952 + i * 2048) for i in range(2)]
    CH0 = SC_H[:, 0:512]
    CH1 = SC_H[:, 513:1025]
    wcr_wd = {}
    wcr_wd[0] = dma("pool", WC2[0], wcr_h[0], "wc0")
    wcr_wd[1] = dma("pool", WC2[1], wcr_h[1], "wc1")
    wc_pe_free = [0, 0]
    sq8_ac = [0, 0]
    last_asm_dv = 0
    x1_dv = [0, 0]
    x1_spilled = [None, None]
    x1_stats_pe = [0, 0]
    tl4 = _tiles(T)
    stats_ready = {}

    def emit_x1_stats(j):
        bj = j % 2
        dvv, acv = stats_ready[j]
        S.wait("pe", "dv", dvv)
        S.wait("pe", "ac", acv)
        for ti, (s, l) in enumerate(tl4):
            mm(pa[ROWS3[ti]:ROWS3[ti] + 1, 0:l], ONESB, X1BF[bj][:, s:s + l],
               start=(j == 0), stop=(j == C - 1),
               inc=("pe" if ti == len(tl4) - 1 else None))
        for ti, (s, l) in enumerate(tl4):
            mm(pb[ROWS3[ti]:ROWS3[ti] + 1, 0:l], ONESB, SQ8[bj][:, s:s + l],
               start=(j == 0), stop=(j == C - 1),
               inc=("pe" if ti == len(tl4) - 1 else None))
        x1_stats_pe[bj] = S.ctr["pe"]
    def p8b_loads(j):
        bj = j % 2
        S.wait("sp", "dv", x1_dv[bj])
        t1 = dma("sp", XCH[bj], x_img[:, j * 1026 + 2:(j + 1) * 1026], f"xch{bj}")
        t2 = dma("sp", XM2[bj], xm_spill[j][:, 1:1025], f"xm2{bj}")
        t3 = dma("sp", KV2[bj], kv_spill[j], f"kv2{bj}")
        return (t1, t2, t3)

    xd_loads_of = {0: p8b_loads(0), 1: p8b_loads(1)}
    for oc2 in range(C):
        b = oc2 % 2
        xd_loads = xd_loads_of[oc2]
        S.waitt("pe", wcr_wd[oc2])
        S.waitt("pe", cmr_re_xd)
        S.wait("pe", "dv", x1_dv[b])
        for tt in range(2):
            for kc in range(C):
                mm(PCR[b][tt][:, 0:512], WC2[b][:, kc * 128:(kc + 1) * 128],
                   CMRR[:, kc * T + tt * 512: kc * T + tt * 512 + 512],
                   start=(kc == 0), stop=(kc == C - 1),
                   inc=("pe" if (kc == C - 1 and tt == 1) else None))
        pe_cr = S.ctr["pe"]
        wc_pe_free[b] = pe_cr
        if oc2 + 2 < C:
            S.wait("pool", "pe", pe_cr)
            wcr_wd[oc2 + 2] = dma("pool", WC2[b], wcr_h[oc2 + 2], f"wc{b}")
        for tok_ in xd_loads:
            S.waitt("dve", tok_)
        S.waitt("dve", x1_spilled[b])
        S.wait("dve", "ac", sq8_ac[b])
        S.wait("dve", "pe", pe_cr)
        S.wait("act", "pe", pe_cr)
        for tt in range(2):
            sl = slice(tt * 512, tt * 512 + 512)
            S.wait("act", "dv", last_asm_dv)
            act(CH0, PCR[b][tt][:, 0:512], AF.Sigmoid, inc="ac")
            S.wait("dve", "ac", S.ctr["ac"])
            dve("tensor_add", X1CH[b][:, sl], XCH[b][:, sl], XM2[b][:, sl])
            dve("tensor_mul", CH1, CH0, KV2[b][:, sl])
            dve("tensor_add", X1CH[b][:, sl], X1CH[b][:, sl], CH1, inc="dv")
            last_asm_dv = S.ctr["dv"]
        S.wait("dve", "pe", x1_stats_pe[b])
        dve("tensor_copy", X1BF[b], X1CH[b], inc="dv")
        x1_dv[b] = S.ctr["dv"]
        S.wait("act", "dv", x1_dv[b])
        S.wait("act", "pe", x1_stats_pe[b])
        act(SQ8[b], X1CH[b], AF.Square, inc="ac")
        ac_sq = S.ctr["ac"]
        sq8_ac[b] = ac_sq
        S.wait("sp", "dv", x1_dv[b])
        x1_spilled[b] = dma("sp", x1_spill[oc2], X1CH[b], f"sx1{b}")
        if oc2 + 2 < C:
            xd_loads_of[oc2 + 2] = p8b_loads(oc2 + 2)
        stats_ready[oc2] = (x1_dv[b], ac_sq)
        if oc2 > 0:
            emit_x1_stats(oc2 - 1)
    emit_x1_stats(C - 1)
    a3_dv = stats_finish(pa, pb, T, SL_A, SL_B, S.ctr["pe"])

    S.barrier()
    H3 = sba("H3", [128, C * T], BF16, D0)
    BC_A = sba("BC_A3", [128, T], F32, D0 + 32896)
    BC_B = sba("BC_B3", [128, T], F32, D0 + 36992)
    broadcast(stv(SL_A, T), BC_A, T, a3_dv, PS[0])
    bc_ac = broadcast(stv(SL_B, T), BC_B, T, a3_dv, PS[0])
    XCH = [sba(f"XCHd{i}", [128, T], F32, D0 + 41088 + i * 4224) for i in range(4)]
    HXD = SC_H
    HXP = SC_D
    xch_free = [0, 0, 0, 0]
    for fc in range(C):
        b = fc % 4
        eng, hx = (dve, HXD) if fc % 2 == 0 else (pool, HXP)
        sem = "dv" if fc % 2 == 0 else "pl"
        ename = "dve" if fc % 2 == 0 else "pool"
        S.wait("sp", sem, xch_free[b])
        xd_tok = dma("sp", XCH[b], x1_spill[fc], f"xch{b}")
        S.waitt(ename, xd_tok)
        S.wait(ename, "ac", bc_ac)
        eng("tensor_mul", hx[:, 0:T], XCH[b], BC_A)
        eng("tensor_add", H3[:, fc * T:(fc + 1) * T], hx[:, 0:T], BC_B, inc=sem)
        xch_free[b] = S.ctr[sem]
    h3_dv = S.ctr["dv"]
    h3_pl = S.ctr["pl"]
    if "d_h3" in dbg_out:
        S.wait("sp", "dv", h3_dv)
        for fc in range(C):
            dma("sp", dbg_out["d_h3"][fc], H3[:, fc * T:(fc + 1) * T], "dbg")

    S.barrier()
    MID = sba("MID", [128, IC * T], BF16, D0 + 32896)
    WG = [sba(f"WG{i}", [128, 2048], BF16, D0 + 163968 + i * 4096) for i in range(2)]
    PG = [[PS[0], PS[1]], [PS[2], PS[3]]]
    PU = [[PS[4], PS[5]], [PS[6], PS[7]]]
    GS = [SC_H[:, 0:512], SC_H[:, 513:1025]]
    wgu_wd = {}
    piece_pe_free = [0, 0]
    mid_dv = [0, 0]
    gs_free = [0, 0]
    wgu_wd[0] = dma("pool", WG[0], wgu_h[0][:, 0:2048], "wg0")
    wgu_wd[1] = dma("pool", WG[1], wgu_h[0][:, 2048:4096], "wg1")
    for oi in range(IC):
        b = oi % 2
        S.wait("pe", "pl", h3_pl)
        S.wait("pe", "dv", h3_dv)
        S.wait("pe", "dv", mid_dv[b])
        grp_pe = []
        for part in range(2):
            ip = oi * 2 + part
            S.waitt("pe", wgu_wd[ip])
            ps_ = PG[b] if part == 0 else PU[b]
            for tt in range(2):
                for kc in range(C):
                    mm(ps_[tt][:, 0:512], WG[ip % 2][:, kc * 128:(kc + 1) * 128],
                       H3[:, kc * T + tt * 512: kc * T + tt * 512 + 512],
                       start=(kc == 0), stop=(kc == C - 1),
                       inc=("pe" if kc == C - 1 else None))
                grp_pe.append(S.ctr["pe"])
            piece_pe_free[ip % 2] = S.ctr["pe"]
            if ip + 2 < IC * 2:
                nip = ip + 2
                S.wait("pool", "pe", piece_pe_free[ip % 2])
                srcp = wgu_h[nip // 2][:, (nip % 2) * 2048:(nip % 2) * 2048 + 2048]
                wgu_wd[nip] = dma("pool", WG[nip % 2], srcp, f"wg{nip % 2}")
        for tt in range(2):
            S.wait("act", "pe", grp_pe[tt])
            S.wait("act", "dv", gs_free[tt])
            act(GS[tt], PG[b][tt][:, 0:512], AF.Silu, inc="ac")
            S.wait("dve", "ac", S.ctr["ac"])
            S.wait("dve", "pe", grp_pe[2 + tt])
            dve("tensor_mul", MID[:, oi * T + tt * 512: oi * T + tt * 512 + 512],
                GS[tt], PU[b][tt][:, 0:512], inc="dv")
            gs_free[tt] = S.ctr["dv"]
        mid_dv[b] = S.ctr["dv"]
    mid_done_dv = S.ctr["dv"]
    if "d_mid" in dbg_out:
        S.wait("sp", "dv", mid_done_dv)
        for oi in range(IC):
            dma("sp", dbg_out["d_mid"][oi], MID[:, oi * T:(oi + 1) * T], "dbg")

    S.barrier()
    WD2 = [sba("WD20", [128, 2048], BF16, D0 + 163968),
           sba("WD21", [128, 2048], BF16, D0 + 168064),
           sba("WD22", [128, 2048], BF16, D0)]
    PD = [[PS[0], PS[1]], [PS[2], PS[3]]]
    XCH = [sba(f"XCHe{i}", [128, T], F32, D0 + 4224 + i * 4096) for i in range(2)]
    OCH = [sba(f"OCH{i}", [128, T], F32, D0 + 12544 + i * 4096) for i in range(2)]
    wdn_wd = {}
    wd_pe_free = [0, 0, 0]
    out_dv = [0, 0]
    out_od = [None, None]
    for i in range(3):
        wdn_wd[i] = dma("pool", WD2[i], wdn_h[i], f"wn{i}")
    for oc2 in range(C):
        b = oc2 % 2
        S.wait("sp", "dv", out_dv[b])
        xd_tok = dma("sp", XCH[b], x1_spill[oc2], f"xch{b}")
        S.wait("pe", "dv", out_dv[b])
        for kcg in range(4):
            idx = oc2 * 4 + kcg
            S.waitt("pe", wdn_wd[idx])
            S.wait("pe", "dv", mid_done_dv)
            for kc in range(C):
                ic = kcg * 16 + kc
                for tt in range(2):
                    mm(PD[b][tt][:, 0:512], WD2[idx % 3][:, kc * 128:(kc + 1) * 128],
                       MID[:, ic * T + tt * 512: ic * T + tt * 512 + 512],
                       start=(kcg == 0 and kc == 0), stop=(kcg == 3 and kc == C - 1),
                       inc=("pe" if (kc == C - 1 and tt == 1) else None))
            wd_pe_free[idx % 3] = S.ctr["pe"]
            if idx + 3 < IC:
                S.wait("pool", "pe", wd_pe_free[idx % 3])
                wdn_wd[idx + 3] = dma("pool", WD2[idx % 3], wdn_h[idx + 3], f"wn{idx % 3}")
        pe_dn = S.ctr["pe"]
        S.waitt("dve", xd_tok)
        S.wait("dve", "pe", pe_dn)
        S.waitt("dve", out_od[b])
        dve("tensor_add", OCH[b][:, 0:512], XCH[b][:, 0:512], PD[b][0][:, 0:512])
        dve("tensor_add", OCH[b][:, 512:1024], XCH[b][:, 512:1024], PD[b][1][:, 0:512],
            inc="dv")
        out_dv[b] = S.ctr["dv"]
        S.wait("sp", "dv", out_dv[b])
        out_od[b] = dma("sp", out_h[oc2], OCH[b], f"od{b}")
    S.wait("sp", "od0", S.ctr.get("od0", 0))
    S.wait("sp", "od1", S.ctr.get("od1", 0))

    with contextlib.ExitStack() as st:
        for name in list(S.sems):
            S.sems[name] = st.enter_context(nc.semaphore(name))
        with nc.Block() as block:
            S.replay(block)
    return nc


def _prep(inputs):
    f32 = np.float32
    _ = inputs.get("L0_attn_tm_decay")
    x = np.asarray(inputs["L0_attn_x"], f32)[0]
    Wk = np.asarray(inputs["L0_attn_Wk"], f32)
    Wv = np.asarray(inputs["L0_attn_Wv"], f32)
    Wr = np.asarray(inputs["L0_attn_Wr"], f32)
    Wo = np.asarray(inputs["L0_attn_Wo"], f32)
    Wkey = np.asarray(inputs["L0_attn_Wkey"], f32)
    Wval = np.asarray(inputs["L0_attn_Wval"], f32)
    Wcr = np.asarray(inputs["L0_attn_Wcr"], f32)
    Wgu = np.asarray(inputs["L0_ffn_w_gate_up"], f32)
    Wdn = np.asarray(inputs["L0_ffn_w_down"], f32)

    A = np.stack([Wk, Wv, Wr])
    wkvr = np.ascontiguousarray(
        A.reshape(3, 16, 128, 16, 128).transpose(1, 4, 0, 3, 2).reshape(16, 128, 3 * 2048))
    wo = np.ascontiguousarray(
        Wo.reshape(16, 128, 16, 128).transpose(0, 3, 2, 1).reshape(16, 128, 2048))
    wkey = np.ascontiguousarray(
        Wkey.reshape(64, 128, 16, 128).transpose(0, 3, 2, 1).reshape(64, 128, 2048))
    wval = np.ascontiguousarray(
        Wval.reshape(16, 128, 4, 16, 128).transpose(0, 2, 4, 3, 1).reshape(64, 128, 2048))
    G = Wgu[:8192].reshape(64, 128, 16, 128)
    Uu = Wgu[8192:].reshape(64, 128, 16, 128)
    wgu = np.ascontiguousarray(
        np.stack([G, Uu]).transpose(1, 4, 0, 3, 2).reshape(64, 128, 4096))
    wdn = np.ascontiguousarray(
        Wdn.reshape(16, 128, 4, 16, 128).transpose(0, 2, 4, 3, 1).reshape(64, 128, 2048))
    wcr = np.ascontiguousarray(
        Wcr.reshape(16, 128, 16, 128).transpose(0, 3, 2, 1).reshape(16, 128, 2048))

    vec_list = [
        inputs["L0_attn_ln1_g"], inputs["L0_attn_ln1_b"],
        inputs["L0_attn_ln2_g"], inputs["L0_attn_ln2_b"],
        inputs["L0_attn_tm_first"],
        np.asarray(inputs["L0_attn_tm_mk"]).reshape(-1),
        np.asarray(inputs["L0_attn_tm_mv"]).reshape(-1),
        np.asarray(inputs["L0_attn_tm_mr"]).reshape(-1),
        np.asarray(inputs["L0_attn_cm_mk"]).reshape(-1),
        np.asarray(inputs["L0_attn_cm_mr"]).reshape(-1),
    ]
    V = np.stack([np.asarray(v, f32).reshape(2048) for v in vec_list])
    vecs = np.zeros((128, NVEC), f32)
    vecs[:, :160] = V.reshape(10, 16, 128).transpose(2, 0, 1).reshape(128, 160)
    vecs[:, V_EPS] = EPS
    vecs[:, V_NHALF] = -0.5
    vecs[:, V_NONE] = -1.0
    vecs[:, V_HALF] = 0.5
    vecs[:, V_NEG] = NEG
    vecs = np.ascontiguousarray(vecs)

    x_fm = np.ascontiguousarray(x.T)
    in_maps = []
    for ci in range(NCORES):
        s = ci * T
        sl = np.zeros((2048, 1026), f32)
        if s - 2 < 0:
            sl[:, 2:] = x_fm[:, s:s + T]
        else:
            sl[:, :] = x_fm[:, s - 2:s + T]
        ximg = np.ascontiguousarray(
            sl.reshape(16, 128, 1026).transpose(1, 0, 2).reshape(128, 16 * 1026))
        hm = np.full((128, 1), 0.0 if ci == 0 else 1.0, f32)
        in_maps.append(dict(
            x_img=ximg, wkvr=wkvr, wo=wo, wkey=wkey, wval=wval,
            wgu=wgu, wdn=wdn, wcr=wcr, vecs=vecs, hmask=hm))
    return in_maps


def kernel(**inputs):
    in_maps = _prep(inputs)
    nc = _build()
    res = run_bass_kernel_spmd(nc, in_maps, core_ids=list(range(NCORES)))
    parts = []
    for ci in range(NCORES):
        o = np.asarray(res.results[ci]["out"], np.float32)
        parts.append(o.reshape(2048, T))
    full = np.concatenate(parts, axis=1)
    return np.ascontiguousarray(full.T).reshape(1, 8192, 2048).astype(np.float32)
